# version 29
# baseline (speedup 1.0000x reference)
import numpy as np
import ml_dtypes
import contextlib
import concourse.bass as bass
import concourse.mybir as mybir
from concourse.bass_utils import run_bass_kernel_spmd

F32 = mybir.dt.float32
BF16 = mybir.dt.bfloat16
AF = mybir.ActivationFunctionType
ALU = mybir.AluOpType
AX = mybir.AxisListType

D = 1024
NCH = 8
NH = 16
HD = 64
NE = 16
DFF = 512
FDFF = 1024
EPS = 1e-5
STRICT_WAITS = False
DEFER_OFF = False
DEFER_AT = 3
kernel_NRING = 3


class Sem:
    __slots__ = ("h", "n", "name")

    def __init__(self, h, name):
        self.h, self.n, self.name = h, 0, name


class Sched:
    ENGS = ("T", "V", "A", "S", "P")

    def __init__(self):
        self.ops = {k: [] for k in self.ENGS}
        self.hw = {k: {} for k in self.ENGS}
        self.last = {k: None for k in self.ENGS}
        self.lastinc = {k: None for k in self.ENGS}
        self.dirty = {k: False for k in self.ENGS}
        self.pending = {k: None for k in self.ENGS}

    def op(self, eng, build, waits=(), incs=()):
        wl = []
        if self.pending[eng]:
            wl += self.pending[eng]
            self.pending[eng] = None
        wl += list(waits)
        best = {}
        for s, v in wl:
            if v > 0 and v > best.get(id(s), (None, 0))[1]:
                best[id(s)] = (s, v)
        if globals().get('STRICT_WAITS', False):
            wfin = [(s.h, v) for s, v in best.values()]
        else:
            wfin = []
            for s, v in best.values():
                if v > self.hw[eng].get(id(s), 0):
                    self.hw[eng][id(s)] = v
                    wfin.append((s.h, v))
        il = []
        for s, k in incs:
            s.n += k
            il.append((s.h, k))
        ent = {"w": wfin, "i": il, "b": build}
        self.ops[eng].append(ent)
        self.last[eng] = ent
        if incs:
            s0, _ = incs[0]
            self.lastinc[eng] = (s0, s0.n)
        self.dirty[eng] = True
        return ent

    def barrier(self, sem=None, extra=()):
        pend = list(extra)
        for eng in ("T", "V", "A"):
            if self.dirty[eng] and self.lastinc[eng] is not None:
                pend.append(self.lastinc[eng])
        for eng in self.ENGS:
            self.dirty[eng] = False
            self.pending[eng] = list(pend)

    def replay(self, eng, e):
        for ent in self.ops[eng]:
            for h, v in ent["w"]:
                e.wait_ge(h, v)
            ins = ent["b"](e)
            for h, k in ent["i"]:
                ins.then_inc(h, k)


class Arena:
    def __init__(self, h):
        self.h = h
        self.h32 = h.bitcast(F32)
        self.off = 0
        self.marks = []
        self.cap = h.shape[1] * 2

    def alloc(self, nbytes):
        nbytes = (nbytes + 63) // 64 * 64
        o = self.off
        self.off += nbytes
        assert self.off <= self.cap, f"arena overflow {self.off} > {self.cap}"
        return o

    def bf(self, off, p0, p1, cols):
        return self.h[p0:p1, off // 2: off // 2 + cols]

    def f32(self, off, p0, p1, cols):
        return self.h32[p0:p1, off // 4: off // 4 + cols]

    def push(self):
        self.marks.append(self.off)

    def pop(self):
        self.off = self.marks.pop()


def _build(S):
    NT = S // 128
    NB = S // 512
    QT = min(S, 1024)
    NQ = S // QT
    NBQ = QT // 512
    NTQ = QT // 128
    NPAIR = NH // 2

    nc = bass.Bass("TRN2", target_bir_lowering=False)
    dt = nc.dram_tensor
    X = dt("x", [S, D], F32, kind="ExternalInput").ap()
    OUT = dt("out", [S, D], F32, kind="ExternalOutput").ap()
    WQd = dt("wq", [D, D], F32, kind="ExternalInput").ap()
    WKd = dt("wk", [D, D], F32, kind="ExternalInput").ap()
    WVd = dt("wv", [D, D], F32, kind="ExternalInput").ap()
    WOd = dt("wo", [D, D], F32, kind="ExternalInput").ap()
    LNW = dt("lnw", [1, D], F32, kind="ExternalInput").ap()
    LNB = dt("lnb", [1, D], F32, kind="ExternalInput").ap()
    BQd = dt("bq", [1, D], F32, kind="ExternalInput").ap()
    BKd = dt("bk", [1, D], F32, kind="ExternalInput").ap()
    BVd = dt("bv", [1, D], F32, kind="ExternalInput").ap()
    BOd = dt("bo", [1, D], F32, kind="ExternalInput").ap()
    GWT = dt("gwt", [D, NE], F32, kind="ExternalInput").ap()
    EBd = dt("eb", [1, NE], F32, kind="ExternalInput").ap()
    W1T = dt("w1t", [NE, D, DFF], F32, kind="ExternalInput").ap()
    W3T = dt("w3t", [NE, D, DFF], F32, kind="ExternalInput").ap()
    W2T = dt("w2t", [NE, DFF, D], F32, kind="ExternalInput").ap()
    B1d = dt("b1", [NE, DFF], F32, kind="ExternalInput").ap()
    B3d = dt("b3", [NE, DFF], F32, kind="ExternalInput").ap()
    B2d = dt("b2", [NE, D], F32, kind="ExternalInput").ap()
    SW1T = dt("sw1t", [D, FDFF], F32, kind="ExternalInput").ap()
    SW3T = dt("sw3t", [D, FDFF], F32, kind="ExternalInput").ap()
    SW2T = dt("sw2t", [FDFF, D], F32, kind="ExternalInput").ap()
    SB1d = dt("sb1", [1, FDFF], F32, kind="ExternalInput").ap()
    SB3d = dt("sb3", [1, FDFF], F32, kind="ExternalInput").ap()
    SB2d = dt("sb2", [1, D], F32, kind="ExternalInput").ap()

    XN_SP = dt("xn_sp", [S, D], BF16, kind="Internal").ap()
    QH_SP = dt("qh_sp", [S, D], BF16, kind="Internal").ap()
    KH_SP = dt("kh_sp", [S, D], BF16, kind="Internal").ap()
    va_cols = (S // 128) * D
    VA_SP = dt("va_sp", [128, va_cols], BF16, kind="Internal").ap()
    G_SP = dt("g_sp", [S, D], F32, kind="Internal").ap()
    IDEN = nc.inline_tensor(np.eye(128, dtype=ml_dtypes.bfloat16), "iden").ap()

    stack = contextlib.ExitStack()

    def sem(name):
        return Sem(stack.enter_context(nc.semaphore(name)), name)

    s_set, s_setp, s_setc = sem("set"), sem("setp"), sem("setc")
    s_bar = [sem(f"bar{i}") for i in range(5)]
    s_x = [sem("sx0"), sem("sx1")]
    s_v0, s_a0 = sem("sv0"), sem("sa0")
    s_xnsp = [sem("sxnsp0"), sem("sxnsp1")]
    s_xnT = sem("sxnT")
    s_xnd = [sem(f"sxnd{c}") for c in range(8)]
    s_t1, s_v1, s_a1 = sem("st1"), sem("sv1"), sem("sa1")
    s_wld = [sem("swld0"), sem("swld1")]
    s_th, s_ah, s_vh = sem("sth"), sem("sah"), sem("svh")
    s_wsh = sem("swsh")
    s_ty, s_vy, s_ag = sem("sty"), sem("svy"), sem("sag")
    s_gsp = [sem("sgsp0"), sem("sgsp1")]
    s_ws = [sem("sws0"), sem("sws1")]
    s_v3, s_a3, s_t3 = sem("sv3"), sem("sa3"), sem("st3")
    s_qsp = [sem("sqsp0"), sem("sqsp1")]
    s_ksp = [sem("sksp0"), sem("sksp1")]
    s_vsp = sem("svsp")
    s_p4l, s_wo4 = sem("sp4l"), sem("swo4")
    s_p4a = sem("sp4a")
    s_ktd = [sem(f"sktd{c}") for c in range(8)]
    s_qT = [sem("sqT0"), sem("sqT1")]
    s_tsc, s_aex, s_tav = sem("stsc"), sem("saex"), sem("stav")
    s_an, s_tn, s_vn = sem("san"), sem("stn"), sem("svn")
    s_nbc = sem("snbc")
    s_tt = sem("stt")
    s_xt = [sem("sxt0"), sem("sxt1")]
    s_gt = [sem("sgt0"), sem("sgt1")]
    s_vt, s_at = sem("svt"), sem("sat")
    s_out = [sem("sout0"), sem("sout1")]

    def all_dma_sems():
        out = []
        for s in (s_set, s_setp, s_xnT, s_vsp, s_p4l, s_wo4, s_p4a):
            out.append((s, s.n))
        for s in s_ktd:
            out.append((s, s.n))
        for pairl in (s_x, s_xnsp, s_wld, s_gsp, s_ws, s_qsp, s_ksp, s_qT,
                      s_xt, s_gt, s_out):
            for s in pairl:
                out.append((s, s.n))
        return out

    rem = nc.sbuf_bytes_remaining
    arena_cols = (rem // 2) // 64 * 64
    AR = Arena(nc.alloc_sbuf_tensor("arena", [128, arena_cols], BF16))

    ones_row = AR.bf(AR.alloc(256), 0, 1, 128)
    ones32 = AR.f32(AR.alloc(256), 0, 1, 64)
    ones_col = AR.bf(AR.alloc(64), 0, 128, 1)
    bo_row = AR.bf(AR.alloc(2048), 0, 1, D)
    R2 = AR.f32(AR.alloc(NT * 4), 0, 128, NT)

    AR.push()
    o_xnT = AR.alloc(NCH * S * 2)
    gatew = AR.f32(AR.alloc(NT * NE * 4), 0, 128, NT * NE)
    gatewT = AR.bf(AR.alloc(S * 2), 0, 16, S)
    iden_sb = AR.bf(AR.alloc(256), 0, 128, 128)
    gwt_sb = AR.bf(AR.alloc(NCH * NE * 2), 0, 128, NCH * NE)
    eb_sb = AR.f32(AR.alloc(NE * 4), 0, 128, NE)
    b1T = AR.bf(AR.alloc(4 * NE * 2), 0, 128, 4 * NE)
    b3T = AR.bf(AR.alloc(4 * NE * 2), 0, 128, 4 * NE)
    sb13T = AR.bf(AR.alloc(16 * 2), 0, 128, 16)
    b2_sb = AR.bf(AR.alloc(D * 2), 0, 16, D)
    sb2_row = AR.bf(AR.alloc(D * 2), 0, 1, D)
    b1s = AR.bf(AR.alloc(DFF * 2), 0, 16, DFF)
    b3s = AR.bf(AR.alloc(DFF * 2), 0, 16, DFF)
    sb13s = AR.bf(AR.alloc(128 * 2), 0, 16, 128)
    o_brow = AR.alloc(3 * D * 2)
    brow = [AR.bf(o_brow + i * D * 2, 0, 1, D) for i in range(3)]
    lnw_sb = AR.f32(AR.alloc(NCH * 4), 0, 128, NCH)
    lnb_sb = AR.bf(AR.alloc(NCH * 2), 0, 128, NCH)
    gsc = AR.f32(AR.alloc(NE * 4), 0, 128, NE)
    gbi = AR.f32(AR.alloc(NE * 4), 0, 128, NE)
    gb2 = AR.f32(AR.alloc(NE * 4), 0, 128, NE)
    gwu = AR.f32(AR.alloc(NE * 4), 0, 128, NE)
    gm = AR.f32(AR.alloc(32), 0, 128, 8)
    gbf = AR.bf(AR.alloc(NT * NE * 2), 0, 128, NT * NE)

    def xnT(c, t0, t1):
        return AR.h[:, o_xnT // 2 + c * S + t0: o_xnT // 2 + c * S + t1]

    PS = nc.alloc_psum_tensor("ps", [128, 4096], F32)
    PSb = PS.bitcast(BF16)

    sch = Sched()

    def mm(out, lhsT, rhs, start, stop, waits=(), incs=(), tp=None):
        def b(e, out=out, lhsT=lhsT, rhs=rhs, start=start, stop=stop, tp=tp):
            return e.matmul(out, lhsT, rhs, start=start, stop=stop,
                            tile_position=tp)
        return sch.op("T", b, waits, incs)

    def dmaS(out, in_, waits=(), incs=(), transpose=False, slow=False):
        def b(e, out=out, in_=in_, t=transpose, slow=slow):
            return e.dma_start(out, in_, transpose=t,
                               allow_slow_non_contiguous=slow)
        return sch.op("S", b, waits, incs)

    def dmaV(out, in_, waits=(), incs=(), transpose=False, slow=False):
        def b(e, out=out, in_=in_, t=transpose, slow=slow):
            return e.dma_start(out, in_, transpose=t,
                               allow_slow_non_contiguous=slow)
        return sch.op("V", b, waits, incs)

    def dmaA(out, in_, waits=(), incs=(), transpose=False, slow=False):
        def b(e, out=out, in_=in_, t=transpose, slow=slow):
            return e.dma_start(out, in_, transpose=t,
                               allow_slow_non_contiguous=slow)
        return sch.op("A", b, waits, incs)

    def dmaP(out, in_, waits=(), incs=(), slow=False):
        def b(e, out=out, in_=in_, slow=slow):
            return e.dma_start(out, in_, allow_slow_non_contiguous=slow)
        return sch.op("P", b, waits, incs)

    def vop(name, *args, waits=(), incs=()):
        def b(e, name=name, args=args):
            return getattr(e, name)(*args)
        return sch.op("V", b, waits, incs)

    def aop(out, in_, func, bias=0.0, scale=1.0, accum=None, waits=(), incs=()):
        def b(e, out=out, in_=in_, func=func, bias=bias, scale=scale, ac=accum):
            return e.activation(out, in_, func, bias=bias, scale=scale,
                                accum_out=ac)
        return sch.op("A", b, waits, incs)

    dmaP(gwt_sb.rearrange("p (c s) -> p c s", s=NE),
         GWT.rearrange("(c p) s -> p c s", p=128), incs=[(s_setp, 16)])
    dmaP(b2_sb, B2d[:, :], incs=[(s_setp, 16)])
    dmaP(sb2_row, SB2d[:, :], incs=[(s_setp, 16)])
    dmaP(bo_row, BOd[:, :], incs=[(s_setp, 16)])
    for i, B_ in enumerate((BQd, BKd, BVd)):
        dmaP(brow[i], B_[:, :], incs=[(s_setp, 16)])
    dmaP(b1s, B1d[:, :], incs=[(s_setp, 16)])
    dmaP(b3s, B3d[:, :], incs=[(s_setp, 16)])
    dmaP(sb13s[0:8, :], SB1d.rearrange("o (f p) -> (o f) p", p=128),
         incs=[(s_setp, 16)])
    dmaP(sb13s[8:16, :], SB3d.rearrange("o (f p) -> (o f) p", p=128),
         incs=[(s_setp, 16)])
    dmaP(lnb_sb.rearrange("p (c u) -> p c u", u=1),
         LNB.rearrange("o (c p u) -> (o p) c u", p=128, u=1),
         incs=[(s_setp, 16)], slow=True)
    dmaS(iden_sb, IDEN[:, :], incs=[(s_set, 16)])
    dmaS(lnw_sb.rearrange("p (c u) -> p c u", u=1),
         LNW.rearrange("o (c p u) -> (o p) c u", p=128, u=1),
         incs=[(s_set, 16)], slow=True)
    eb_bc = bass.AP(tensor=EBd.tensor, offset=0, ap=[[0, 128], [1, NE]])
    dmaS(eb_sb, eb_bc, incs=[(s_set, 16)])
    SETP_N = s_setp.n
    nset0 = s_set.n
    for i in range(4):
        dmaS(b1T[:, i * 16:(i + 1) * 16], b1s[:, i * 128:(i + 1) * 128],
             waits=[(s_set, s_set.n), (s_setp, SETP_N)],
             incs=[(s_set, 16)], transpose=True)
    for i in range(4):
        dmaS(b3T[:, i * 16:(i + 1) * 16], b3s[:, i * 128:(i + 1) * 128],
             waits=[(s_set, s_set.n)], incs=[(s_set, 16)], transpose=True)
    dmaS(sb13T, sb13s, waits=[(s_set, s_set.n)], incs=[(s_set, 16)],
         transpose=True)
    SETUP_N = s_set.n
    vop("memset", ones_row, 1.0, incs=[(s_setc, 1)])
    vop("memset", ones_col, 1.0, incs=[(s_setc, 1)])
    vop("memset", ones32, 1.0, incs=[(s_setc, 1)])

    AR.push()
    x_t = [AR.f32(AR.alloc(D * 4), 0, 128, D) for _ in range(2)]
    xn_bf = [AR.bf(AR.alloc(D * 2), 0, 128, D) for _ in range(2)]
    sq_t = AR.f32(AR.alloc(D * 4), 0, 128, D)
    STT = [AR.f32(AR.alloc(64), 0, 128, 16) for _ in range(2)]
    v_done, a_done = {}, {}
    for j in range(NT):
        p = j % 2
        st = STT[p]

        def c(i, st=st):
            return st[:, i:i + 1]
        dmaS(x_t[p], X[j * 128:(j + 1) * 128, :],
             waits=[(s_v0, v_done.get(j - 2, 0)), (s_a0, a_done.get(j - 2, 0))],
             incs=[(s_x[p], 16)])
        wx = [(s_x[p], 16 * (j // 2 + 1))]
        wl = [(s_v0, v_done.get(j - 1, 0)), (s_a0, a_done.get(j - 1, 0))]
        vop("reduce_sum", c(0), x_t[p], AX.X, waits=wx + wl, incs=[(s_v0, 1)])
        aop(sq_t, x_t[p], AF.Square, accum=c(9),
            waits=wx + wl + [(s_a0, s_a0.n)], incs=[(s_a0, 1)])
        asq = s_a0.n
        vop("tensor_scalar", c(1), c(0), 1.0 / D, None, ALU.mult,
            waits=[(s_v0, s_v0.n)], incs=[(s_v0, 1)])
        vop("tensor_tensor", c(2), c(0), c(0), ALU.mult,
            waits=[(s_v0, s_v0.n)], incs=[(s_v0, 1)])
        vop("tensor_scalar", c(3), c(2), 1.0 / (D * D), None, ALU.mult,
            waits=[(s_v0, s_v0.n)], incs=[(s_v0, 1)])
        vop("scalar_tensor_tensor", c(4), c(9), 1.0 / D, c(3), ALU.mult,
            ALU.subtract, waits=[(s_a0, asq), (s_v0, s_v0.n)],
            incs=[(s_v0, 1)])
        vop("tensor_scalar", c(5), c(4), EPS, None, ALU.add,
            waits=[(s_v0, s_v0.n)], incs=[(s_v0, 1)])
        vvar = s_v0.n
        aop(c(10), c(5), AF.Ln, waits=[(s_v0, vvar), (s_a0, s_a0.n)],
            incs=[(s_a0, 1)])
        aop(c(10), c(10), AF.Exp, scale=-0.5, waits=[(s_a0, s_a0.n)],
            incs=[(s_a0, 1)])
        ar1 = s_a0.n
        vop("reciprocal", c(6), c(5), waits=[(s_v0, s_v0.n)], incs=[(s_v0, 1)])
        vop("tensor_tensor", c(7), c(4), c(6), ALU.mult,
            waits=[(s_v0, s_v0.n)], incs=[(s_v0, 1)])
        vop("tensor_scalar", c(8), c(7), EPS, None, ALU.add,
            waits=[(s_v0, s_v0.n)], incs=[(s_v0, 1)])
        vt2 = s_v0.n
        aop(R2[:, j:j + 1], c(8), AF.Ln, waits=[(s_v0, vt2), (s_a0, s_a0.n)],
            incs=[(s_a0, 1)])
        aop(R2[:, j:j + 1], R2[:, j:j + 1], AF.Exp, scale=-0.5,
            waits=[(s_a0, s_a0.n)], incs=[(s_a0, 1)])
        vop("tensor_scalar", xn_bf[p], x_t[p], c(1), c(10), ALU.subtract,
            ALU.mult, waits=[(s_a0, ar1), (s_v0, s_v0.n),
                             (s_xnsp[p], 16 * (j // 2))],
            incs=[(s_v0, 1)])
        v_done[j], a_done[j] = s_v0.n, s_a0.n
        dmaS(XN_SP[j * 128:(j + 1) * 128, :], xn_bf[p],
             waits=[(s_v0, v_done[j])], incs=[(s_xnsp[p], 16)])
    xnd_done = {}
    for cc in range(NCH):
        dmaF = dmaS
        dmaF(xnT(cc, 0, S), XN_SP[:, cc * 128:(cc + 1) * 128],
             waits=[(s_xnsp[0], 16 * ((NT + 1) // 2)),
                    (s_xnsp[1], 16 * (NT // 2))],
             incs=[(s_xnd[cc], 16)], transpose=True)
        xnd_done[cc] = (s_xnd[cc], s_xnd[cc].n)

    sch.barrier(s_bar[0], extra=all_dma_sems())
    AR.pop()
    lg_ps = [PS[:, 2048:2048 + NE], PS[:, 2560:2560 + NE]]
    tr_ps = PSb[0:16, 6144:6272]
    v1_done, t1tr = {}, {}
    for j in range(NT):
        p = j % 2
        for cc in range(NCH):
            mm(lg_ps[p], xnT(cc, j * 128, (j + 1) * 128),
               gwt_sb[:, cc * NE:(cc + 1) * NE], start=(cc == 0),
               stop=(cc == NCH - 1),
               waits=[xnd_done[cc], (s_set, SETUP_N), (s_setp, SETP_N),
                      (s_v1, v1_done.get(j - 2, 0))],
               incs=[(s_t1, 1)] if cc == NCH - 1 else ())
        tmm = s_t1.n
        wl = [(s_v1, v1_done.get(j - 1, 0))]
        aop(gsc, lg_ps[p], AF.Sigmoid,
            waits=[(s_t1, tmm), (s_a1, s_a1.n)] + wl, incs=[(s_a1, 1)])
        asig = s_a1.n
        vop("tensor_tensor", gbi, lg_ps[p], eb_sb, ALU.add,
            waits=[(s_t1, tmm), (s_set, SETUP_N), (s_setp, SETP_N)] + wl,
            incs=[(s_v1, 1)])
        vop("reduce_max", gm[:, 0:1], gbi, AX.X, waits=[(s_v1, s_v1.n)],
            incs=[(s_v1, 1)])
        vop("tensor_scalar", gwu, gbi, gm[:, 0:1], None, ALU.is_equal,
            waits=[(s_v1, s_v1.n)], incs=[(s_v1, 1)])
        vop("scalar_tensor_tensor", gb2, gwu, -1e30, gbi, ALU.mult, ALU.add,
            waits=[(s_v1, s_v1.n)], incs=[(s_v1, 1)])
        vop("reduce_max", gm[:, 1:2], gb2, AX.X, waits=[(s_v1, s_v1.n)],
            incs=[(s_v1, 1)])
        vop("tensor_scalar", gb2, gb2, gm[:, 1:2], None, ALU.is_equal,
            waits=[(s_v1, s_v1.n)], incs=[(s_v1, 1)])
        vop("tensor_tensor", gwu, gwu, gb2, ALU.add, waits=[(s_v1, s_v1.n)],
            incs=[(s_v1, 1)])
        vop("tensor_tensor", gwu, gwu, gsc, ALU.mult,
            waits=[(s_v1, s_v1.n), (s_a1, asig)], incs=[(s_v1, 1)])
        vop("reduce_sum", gm[:, 2:3], gwu, AX.X, waits=[(s_v1, s_v1.n)],
            incs=[(s_v1, 1)])
        vop("tensor_scalar", gm[:, 2:3], gm[:, 2:3], 1e-10, None, ALU.add,
            waits=[(s_v1, s_v1.n)], incs=[(s_v1, 1)])
        vop("reciprocal", gm[:, 3:4], gm[:, 2:3], waits=[(s_v1, s_v1.n)],
            incs=[(s_v1, 1)])
        vop("tensor_scalar", gatew[:, j * NE:(j + 1) * NE], gwu, gm[:, 3:4],
            None, ALU.mult, waits=[(s_v1, s_v1.n)], incs=[(s_v1, 1)])
        vop("tensor_copy", gbf[:, j * NE:(j + 1) * NE],
            gatew[:, j * NE:(j + 1) * NE],
            waits=[(s_v1, s_v1.n)], incs=[(s_v1, 1)])
        v1_done[j] = s_v1.n
    for j in range(NT):
        sch.op("T", lambda e, j=j: e.transpose(
            tr_ps, gbf[:, j * NE:(j + 1) * NE], iden_sb),
            [(s_v1, v1_done[j]), (s_set, SETUP_N),
             (s_v1, t1tr.get(j - 1, 0))],
            incs=[(s_t1, 1)])
        tn_ = s_t1.n
        vop("tensor_copy", gatewT[:, j * 128:(j + 1) * 128], tr_ps,
            waits=[(s_t1, tn_), (s_v1, s_v1.n)], incs=[(s_v1, 1)])
        t1tr[j] = s_v1.n
        V1N = s_v1.n

    P1T = s_t1.n
    P1V = s_v1.n
    AR.push()
    o_y = AR.alloc(NTQ * D * 4)
    y_acc = AR.f32(o_y, 0, 128, NTQ * D)
    o_wb = AR.alloc(2 * 24 * 1024)
    o_gh = AR.alloc(8 * 512 * 2)
    ghT = AR.bf(o_gh, 0, 128, 8 * 512)
    s1_t = [AR.f32(AR.alloc(512 * 4), 0, 128, 512) for _ in range(2)]
    g_t = [AR.f32(AR.alloc(D * 4), 0, 128, D) for _ in range(2)]

    def wsl(base_bytes, c, span, f0, f1):
        o = (o_wb + base_bytes) // 2
        return AR.h[:, o + c * span + f0: o + c * span + f1]

    h_ps = [(PS[:, 0:512], PS[:, 512:1024]),
            (PS[:, 1024:1536], PS[:, 1536:2048])]
    y_ps = [PS[:, 2048:3072], PS[:, 3072:4096]]
    wfree = {0: (s_set, 0), 1: (s_set, 0)}
    lded = {}

    def load_expert(eidx, buf):
        base = buf * 24 * 1024
        dmaP(AR.h[:, (o_wb + base) // 2:(o_wb + base) // 2 + 4096]
             .rearrange("p (c f) -> p c f", f=DFF),
             W1T[eidx].rearrange("(c p) f -> p c f", p=128),
             waits=[wfree[buf]], incs=[(s_wld[buf], 16)])
        dmaP(AR.h[:, (o_wb + base + 8192) // 2:
                  (o_wb + base + 8192) // 2 + 4096]
             .rearrange("p (c f) -> p c f", f=DFF),
             W3T[eidx].rearrange("(c p) f -> p c f", p=128),
             incs=[(s_wld[buf], 16)])
        dmaP(AR.h[:, (o_wb + base + 16384) // 2:
                  (o_wb + base + 16384) // 2 + 4096]
             .rearrange("p (c d) -> p c d", d=D),
             W2T[eidx].rearrange("(c p) d -> p c d", p=128),
             incs=[(s_wld[buf], 16)])
        lded[eidx] = (s_wld[buf], s_wld[buf].n)

    def load_shared(part):
        if part == 0:
            items = ((0, SW1T, 0, 8, FDFF, 0),
                     (16384, SW3T, 0, 4, FDFF, 0))
        elif part == 1:
            items = ((24576, SW3T, 4, 4, FDFF, 1),
                     (32768, SW2T, 0, 4, D, 1))
        else:
            items = ((40960, SW2T, 4, 4, D, 2),)
        sems = {0: s_wld[0], 1: s_wld[1], 2: s_wsh}
        for off, W_, c0, ncnk, span, sb in items:
            dmaP(AR.h[:, (o_wb + off) // 2:
                      (o_wb + off) // 2 + ncnk * span]
                 .rearrange("p (c f) -> p c f", f=span),
                 W_[c0 * 128:(c0 + ncnk) * 128, :]
                 .rearrange("(c p) f -> p c f", p=128),
                 waits=[(s_th, s_th.n), (s_ty, s_ty.n)],
                 incs=[(sems[sb], 16)])
        sh_ld[part] = ([(s_wsh, s_wsh.n)] if part == 2 else
                       [(s_wld[0], s_wld[0].n), (s_wld[1], s_wld[1].n)])

    sh_ld = {}
    hfreeA = {0: (s_ah, 0), 1: (s_ah, 0)}
    hfreeV = {0: (s_vh, 0), 1: (s_vh, 0)}
    yfree = {0: (s_v1, P1V), 1: (s_v1, P1V)}
    ghfree = {i: (s_ty, 0) for i in range(8)}
    vy_last = {}
    load_expert(0, 0)
    for q in range(NQ):
        tok0 = q * QT
        for ei in range(17):
            is_sh = ei == 16
            buf = ei % 2
            nfc = 8 if is_sh else 4
            if ei + 1 < 16:
                load_expert(ei + 1, (ei + 1) % 2)
            elif ei + 1 == 16:
                load_shared(0)
            if is_sh:
                load_shared(2)
            gh_ready = {}
            for blk in range(NBQ):
                bt0 = tok0 + blk * 512
                for fc in range(nfc):
                    sp = fc % 2
                    slot = fc if is_sh else (blk % 2) * 4 + fc
                    pa, pb = h_ps[sp]
                    for cc in range(NCH):
                        if is_sh:
                            l1 = wsl(0, cc, FDFF, fc * 128, (fc + 1) * 128)
                            l3 = wsl(16384, cc, FDFF, fc * 128, (fc + 1) * 128)
                        else:
                            l1 = wsl(buf * 24576, cc, DFF,
                                     fc * 128, (fc + 1) * 128)
                            l3 = wsl(buf * 24576 + 8192, cc, DFF,
                                     fc * 128, (fc + 1) * 128)
                        if is_sh:
                            w = sh_ld[1] + [
                                 xnd_done[cc], hfreeA[sp], hfreeV[sp]]
                        else:
                            w = [lded[ei],
                                 xnd_done[cc], hfreeA[sp], hfreeV[sp]]
                        mm(pa, l1, xnT(cc, bt0, bt0 + 512), start=(cc == 0),
                           stop=(cc == NCH - 1), waits=w,
                           incs=[(s_th, 1)] if cc == NCH - 1 else ())
                        mm(pb, l3, xnT(cc, bt0, bt0 + 512), start=(cc == 0),
                           stop=(cc == NCH - 1),
                           incs=[(s_th, 1)] if cc == NCH - 1 else ())
                    thn = s_th.n
                    if is_sh:
                        bc1 = sb13T[:, fc:fc + 1]
                        bc3 = sb13T[:, 8 + fc:9 + fc]
                    else:
                        bc1 = b1T[:, fc * NE + ei:fc * NE + ei + 1]
                        bc3 = b3T[:, fc * NE + ei:fc * NE + ei + 1]
                    aop(s1_t[sp], pa, AF.Silu, bias=bc1,
                        waits=[(s_th, thn), (s_set, SETUP_N), (s_setp, SETP_N),
                               (s_vh, hfreeV[sp][1]), (s_ah, s_ah.n)],
                        incs=[(s_ah, 1)])
                    hfreeA[sp] = (s_ah, s_ah.n)
                    vop("scalar_tensor_tensor",
                        ghT[:, slot * 512:(slot + 1) * 512], pb, bc3,
                        s1_t[sp], ALU.add, ALU.mult,
                        waits=[(s_ah, s_ah.n), (s_th, thn), ghfree[slot],
                               (s_vy, s_vy.n)], incs=[(s_vh, 1)])
                    hfreeV[sp] = (s_vh, s_vh.n)
                    gh_ready[(blk, fc)] = s_vh.n
                if is_sh and NBQ > 1:
                    break
            nblk_h = 1 if is_sh else NBQ
            for rep in range(NBQ if is_sh else 1):
                pass
            if ei == 15:
                load_shared(1)
            w2blocks = range(NBQ)
            for blk in w2blocks:
                bt0 = tok0 + blk * 512
                if is_sh and blk > 0:
                    for fc in range(nfc):
                        sp = fc % 2
                        slot = fc
                        pa, pb = h_ps[sp]
                        for cc in range(NCH):
                            l1 = wsl(0, cc, FDFF, fc * 128, (fc + 1) * 128)
                            l3 = wsl(16384, cc, FDFF,
                                     fc * 128, (fc + 1) * 128)
                            w = sh_ld[1] + [
                                 xnd_done[cc], hfreeA[sp], hfreeV[sp]]
                            mm(pa, l1, xnT(cc, bt0, bt0 + 512),
                               start=(cc == 0), stop=(cc == NCH - 1), waits=w,
                               incs=[(s_th, 1)] if cc == NCH - 1 else ())
                            mm(pb, l3, xnT(cc, bt0, bt0 + 512),
                               start=(cc == 0), stop=(cc == NCH - 1),
                               incs=[(s_th, 1)] if cc == NCH - 1 else ())
                        thn = s_th.n
                        bc1 = sb13T[:, fc:fc + 1]
                        bc3 = sb13T[:, 8 + fc:9 + fc]
                        aop(s1_t[sp], pa, AF.Silu, bias=bc1,
                            waits=[(s_th, thn), (s_vh, hfreeV[sp][1]),
                                   (s_ah, s_ah.n)], incs=[(s_ah, 1)])
                        hfreeA[sp] = (s_ah, s_ah.n)
                        vop("scalar_tensor_tensor",
                            ghT[:, slot * 512:(slot + 1) * 512], pb, bc3,
                            s1_t[sp], ALU.add, ALU.mult,
                            waits=[(s_ah, s_ah.n), (s_th, thn), ghfree[slot],
                                   (s_vy, s_vy.n)], incs=[(s_vh, 1)])
                        hfreeV[sp] = (s_vh, s_vh.n)
                        gh_ready[(blk, fc)] = s_vh.n
                    if blk == NBQ - 1 and q + 1 < NQ:
                        wfree[0] = (s_th, s_th.n)
                        load_expert(0, 0)
                for tw in range(4):
                    tg = bt0 // 128 + tw
                    tq = tg - tok0 // 128
                    yp = y_ps[tw % 2]
                    for fc in range(nfc):
                        slot = fc if is_sh else (blk % 2) * 4 + fc
                        for dh in range(2):
                            r = wsl(32768, fc, D, dh * 512, (dh + 1) * 512) \
                                if is_sh else \
                                wsl(buf * 24576 + 16384, fc, D,
                                    dh * 512, (dh + 1) * 512)
                            stop = (not is_sh) and fc == nfc - 1
                            last = stop and dh == 1
                            w2w = [(s_vh, gh_ready[(blk, fc)]),
                                   yfree[tw % 2]]
                            if is_sh:
                                w2w += sh_ld[2]
                            mm(yp[:, dh * 512:(dh + 1) * 512],
                               ghT[:, slot * 512 + tw * 128:
                                   slot * 512 + tw * 128 + 128], r,
                               start=(fc == 0), stop=stop,
                               waits=w2w,
                               incs=[(s_ty, 1)] if last else ())
                    if is_sh:
                        for dh in range(2):
                            mm(yp[:, dh * 512:(dh + 1) * 512],
                               gatewT[:, tg * 128:(tg + 1) * 128],
                               b2_sb[:, dh * 512:(dh + 1) * 512],
                               start=False, stop=False,
                               waits=[(s_v1, t1tr.get(tg, 0)),
                                      (s_v1, V1N)])
                            mm(yp[:, dh * 512:(dh + 1) * 512],
                               ones_row[:, 0:128],
                               sb2_row[:, dh * 512:(dh + 1) * 512],
                               start=False, stop=True,
                               waits=[(s_setc, 3)],
                               incs=[(s_ty, 1)] if dh == 1 else ())
                    tyn = s_ty.n
                    ycol = y_acc[:, tq * D:(tq + 1) * D]
                    gcol = None if is_sh else \
                        gatew[:, tg * NE + ei:tg * NE + ei + 1]
                    if ei == 0:
                        vop("tensor_scalar", ycol, yp, gcol, None, ALU.mult,
                            waits=[(s_ty, tyn), (s_vy, vy_last.get(tq, 0)),
                                   (s_v1, V1N)], incs=[(s_vy, 1)])
                    elif not is_sh:
                        vop("scalar_tensor_tensor", ycol, yp, gcol, ycol,
                            ALU.mult, ALU.add,
                            waits=[(s_ty, tyn), (s_vy, vy_last.get(tq, 0))],
                            incs=[(s_vy, 1)])
                    else:
                        vop("tensor_tensor", ycol, yp, ycol, ALU.add,
                            waits=[(s_ty, tyn), (s_vy, vy_last.get(tq, 0))],
                            incs=[(s_vy, 1)])
                    vy_last[tq] = s_vy.n
                    yfree[tw % 2] = (s_vy, s_vy.n)
                    if is_sh:
                        aop(g_t[tw % 2], ycol, AF.Silu,
                            waits=[(s_vy, vy_last[tq]), (s_a1, s_a1.n),
                                   (s_ag, s_ag.n),
                                   (s_gsp[tw % 2], 16 * (tg // 2))],
                            incs=[(s_ag, 1)])
                        dmaS(G_SP[tg * 128:(tg + 1) * 128, :], g_t[tw % 2],
                             waits=[(s_ag, s_ag.n)],
                             incs=[(s_gsp[tw % 2], 16)])
                for fc in range(nfc):
                    slot = fc if is_sh else (blk % 2) * 4 + fc
                    ghfree[slot] = (s_ty, s_ty.n)
            wfree[buf] = (s_ty, s_ty.n)
            if is_sh:
                wfree[0] = wfree[1] = (s_ty, s_ty.n)

    sch.barrier(s_bar[2], extra=all_dma_sems())
    AR.pop()
    AR.push()
    o_va = AR.alloc(va_cols * 2)
    V_aug = AR.bf(o_va, 0, 128, va_cols)
    o_wp = AR.alloc(NCH * D * 2)
    Wp = AR.bf(o_wp, 0, 128, NCH * D)
    stg = [AR.f32(AR.alloc(D * 4), 0, 128, D) for _ in range(2)]
    raw = [AR.bf(AR.alloc(D * 2), 0, 128, D) for _ in range(2)]
    Bp = AR.f32(AR.alloc(D * 4), 0, 128, D)
    qf = [AR.f32(AR.alloc(D * 4), 0, 128, D) for _ in range(2)]
    qh_bf = [AR.bf(AR.alloc(D * 2), 0, 128, D) for _ in range(2)]
    sq3 = AR.f32(AR.alloc(D * 4), 0, 128, D)
    ssq = AR.f32(AR.alloc(NH * 4), 0, 128, NH)
    rn3 = AR.f32(AR.alloc(NH * 4), 0, 128, NH)
    bp_row = AR.bf(AR.alloc(D * 2), 0, 1, D)

    WMAP = [(WQd, 0, QH_SP, s_qsp), (WKd, 1, KH_SP, s_ksp),
            (WVd, 2, None, None)]
    v3l, a3l = {}, {}
    for pi, (Wd, bi, SPt, ssp) in enumerate(WMAP):
        for kk in (-2, -1):
            v3l[("w", pi, kk)] = s_v3.n
            v3l[("t", pi, kk)] = s_v3.n
            a3l[("t", pi, kk)] = s_a3.n
        prevsp = []
        if pi == 1:
            prevsp = [(s_qsp[0], s_qsp[0].n), (s_qsp[1], s_qsp[1].n)]
        elif pi == 2:
            prevsp = [(s_ksp[0], s_ksp[0].n), (s_ksp[1], s_ksp[1].n)]
        for cc in range(NCH):
            dmaS(stg[cc % 2], Wd[cc * 128:(cc + 1) * 128, :],
                 waits=[(s_v3, v3l.get(("w", pi, cc - 2), 0))],
                 incs=[(s_ws[cc % 2], 16)])
            wsn = (s_ws[cc % 2], s_ws[cc % 2].n)
            vop("tensor_scalar", Wp[:, cc * D:(cc + 1) * D], stg[cc % 2],
                lnw_sb[:, cc:cc + 1], None, ALU.mult,
                waits=[wsn, (s_set, SETUP_N), (s_setp, SETP_N),
                       (s_t3, s_t3.n), (s_v3, s_v3.n)], incs=[(s_v3, 1)])
            vop("tensor_copy", raw[cc % 2], stg[cc % 2],
                waits=[wsn, (s_t3, s_t3.n), (s_v3, s_v3.n)],
                incs=[(s_v3, 1)])
            v3l[("w", pi, cc)] = s_v3.n
            vraw = s_v3.n
            for dh in range(2):
                mm(PS[0:1, 2048 + dh * 512:2560 + dh * 512],
                   lnb_sb[:, cc:cc + 1],
                   raw[cc % 2][:, dh * 512:(dh + 1) * 512],
                   start=(cc == 0), stop=False,
                   waits=[(s_v3, vraw), (s_set, SETUP_N), (s_setp, SETP_N)], incs=[(s_t3, 1)])
        for dh in range(2):
            mm(PS[0:1, 2048 + dh * 512:2560 + dh * 512], ones_row[:, 0:1],
               brow[bi][:, dh * 512:(dh + 1) * 512], start=False, stop=True,
               waits=[(s_setc, 3)], incs=[(s_t3, 1)])
        t3b = s_t3.n
        vop("tensor_copy", bp_row, PS[0:1, 2048:3072],
            waits=[(s_t3, t3b), (s_v3, s_v3.n)], incs=[(s_v3, 1)])
        vbp = s_v3.n
        for dh in range(2):
            mm(PS[:, 3072 + dh * 512:3584 + dh * 512], ones_row[:, 0:128],
               bp_row[:, dh * 512:(dh + 1) * 512], start=True, stop=True,
               waits=[(s_v3, vbp)], incs=[(s_t3, 1)])
        t3c = s_t3.n
        vop("tensor_copy", Bp, PS[:, 3072:4096],
            waits=[(s_t3, t3c), (s_v3, s_v3.n)], incs=[(s_v3, 1)])
        bp_done = s_v3.n
        for j in range(NT):
            p = j % 2
            pq = PS[:, 2048:3072] if p == 0 else PS[:, 3072:4096]
            wl = [(s_v3, v3l.get(("t", pi, j - 1), 0)),
                  (s_a3, a3l.get(("t", pi, j - 1), 0))]
            for cc in range(NCH):
                for dh in range(2):
                    mm(pq[:, dh * 512:(dh + 1) * 512],
                       xnT(cc, j * 128, (j + 1) * 128),
                       Wp[:, cc * D + dh * 512:cc * D + (dh + 1) * 512],
                       start=(cc == 0), stop=(cc == NCH - 1),
                       waits=[(s_v3, bp_done), xnd_done[cc],
                              (s_v3, v3l.get(("t", pi, j - 2), 0))],
                       incs=[(s_t3, 1)] if (cc == NCH - 1 and dh == 1) else ())
            tq3 = s_t3.n
            vop("scalar_tensor_tensor", qf[p], pq, R2[:, j:j + 1], Bp,
                ALU.mult, ALU.add,
                waits=[(s_t3, tq3), (s_v3, s_v3.n)] + wl, incs=[(s_v3, 1)])
            vqf = s_v3.n
            if SPt is not None:
                aop(sq3, qf[p], AF.Square,
                    waits=[(s_v3, vqf), (s_a3, s_a3.n)] + wl,
                    incs=[(s_a3, 1)])
                vop("tensor_reduce", ssq.rearrange("p (h u) -> p h u", u=1),
                    sq3.rearrange("p (h d) -> p h d", d=HD), AX.X, ALU.add,
                    waits=[(s_a3, s_a3.n), (s_v3, s_v3.n)], incs=[(s_v3, 1)])
                vop("tensor_scalar", ssq, ssq, 1e-24, None, ALU.max,
                    waits=[(s_v3, s_v3.n)], incs=[(s_v3, 1)])
                vsq = s_v3.n
                aop(rn3, ssq, AF.Ln, waits=[(s_v3, vsq), (s_a3, s_a3.n)],
                    incs=[(s_a3, 1)])
                aop(rn3, rn3, AF.Exp, scale=-0.5, waits=[(s_a3, s_a3.n)],
                    incs=[(s_a3, 1)])
                arn = s_a3.n
                rnb = bass.AP(tensor=rn3.tensor, offset=rn3.offset,
                              ap=[[rn3.ap[0][0], 128], [1, NH], [0, HD]])
                vop("tensor_tensor",
                    qh_bf[p].rearrange("p (h d) -> p h d", d=HD),
                    qf[p].rearrange("p (h d) -> p h d", d=HD), rnb, ALU.mult,
                    waits=[(s_a3, arn), (s_v3, s_v3.n),
                           (ssp[p], 16 * (j // 2))] + prevsp,
                    incs=[(s_v3, 1)])
                v3l[("t", pi, j)] = s_v3.n
                a3l[("t", pi, j)] = s_a3.n
                dmaS(SPt[j * 128:(j + 1) * 128, :], qh_bf[p],
                     waits=[(s_v3, s_v3.n)], incs=[(ssp[p], 16)])
            else:
                vop("tensor_copy", V_aug[:, j * D:(j + 1) * D], qf[p],
                    waits=[(s_v3, vqf)], incs=[(s_v3, 1)])
                v3l[("t", pi, j)] = s_v3.n
                a3l[("t", pi, j)] = s_a3.n
    V3F = s_v3.n

    sch.barrier(s_bar[3], extra=all_dma_sems())
    AR.pop()
    AR.pop()
    AR.push()

    def allocA(nbytes):
        nb = (nbytes + 63) // 64 * 64
        if AR.off + nb <= o_va:
            return AR.alloc(nbytes)
        return None
    NRING = 4
    o_kt = allocA(NCH * S * 2)
    o_wo = allocA(NCH * D * 2)
    o_ring = allocA(NRING * 1024 * 2)
    o_st4 = [allocA(64) for _ in range(2)]
    assert AR.off <= o_va, (AR.off, o_va)
    AR.off = o_va + va_cols * 2
    V2 = AR.bf(o_va, 0, 128, va_cols)
    if o_kt is None:
        o_kt = AR.alloc(NCH * S * 2)
    if o_wo is None:
        o_wo = AR.alloc(NCH * D * 2)
    if o_ring is None:
        o_ring = AR.alloc(NRING * 1024 * 2)
    o_st4 = [o if o is not None else AR.alloc(64) for o in o_st4]
    Wo_sb = AR.bf(o_wo, 0, 128, NCH * D)
    ring = AR.bf(o_ring, 0, 128, NRING * 1024)
    ST4 = [AR.f32(o, 0, 128, 16) for o in o_st4]
    qTb = AR.bf(AR.alloc(NCH * 512 * 2), 0, 128, NCH * 512)
    O_T = AR.bf(AR.alloc(NPAIR * 512 * 2), 0, 128, NPAIR * 512)
    stT = [AR.bf(AR.alloc(512 * 2), 0, 128, 512) for _ in range(2)]
    su_s = AR.f32(AR.alloc(512 * 4), 0, 128, 512)
    su = [su_s, su_s]
    lr = AR.f32(AR.alloc(512 * 4), 0, 1, 512)
    rinv_p = [AR.f32(AR.alloc(512 * 4), 0, 1, 512) for _ in range(2)]
    rinv = [rinv_p, rinv_p]
    x4 = [AR.f32(AR.alloc(D * 4), 0, 128, D) for _ in range(2)]
    g4_s = AR.f32(AR.alloc(D * 4), 0, 128, D)
    g4 = [g4_s, g4_s]
    o4_s = AR.f32(AR.alloc(D * 4), 0, 128, D)
    o4 = [o4_s, o4_s]
    xn4_s = AR.f32(AR.alloc(D * 4), 0, 128, D)
    xn4 = [xn4_s, xn4_s]
    o_rbc = AR.alloc(512 * 4)
    rbcA_sb = AR.f32(o_rbc, 0, 64, 512)
    rbcB_sb = AR.f32(o_rbc, 64, 128, 512)

    def kTs(pair, p0, p1, c0, c1):
        return AR.h[p0:p1, o_kt // 2 + pair * S + c0: o_kt // 2 + pair * S + c1]

    kt_done = {}

    def load_kT(cc, dmaF, sm=None):
        sm = s_ktd[cc]
        dmaF(kTs(cc, 0, 128, 0, S), KH_SP[:, cc * 128:(cc + 1) * 128],
             waits=[(s_ksp[0], 16 * ((NT + 1) // 2)),
                    (s_ksp[1], 16 * (NT // 2)), (s_v3, V3F)],
             incs=[(sm, 16)], transpose=True)
        kt_done[cc] = (sm, sm.n)
    load_kT(0, dmaS)
    for cc in range(4, NCH):
        load_kT(cc, dmaA)
    dmaP(Wo_sb.rearrange("p (c d) -> p c d", d=D),
         WOd.rearrange("(c p) d -> p c d", p=128), waits=[(s_v3, V3F)],
         incs=[(s_wo4, 16)])
    WO4 = s_wo4.n

    SC = [PS[:, 0:1024], PS[:, 1024:2048]]
    AVb = [PS[:, 2048:3072], PS[:, 3072:4096]]
    avA = [AVb[s][0:64, 0:512] for s in range(2)]
    rsA = [AVb[s][64:65, 0:512] for s in range(2)]
    avB = [AVb[s][64:128, 512:1024] for s in range(2)]
    rsB = [AVb[s][0:1, 512:1024] for s in range(2)]
    rbcA = [AVb[s][0:64, 0:512] for s in range(2)]
    rbcB = [AVb[s][64:128, 512:1024] for s in range(2)]

    SCF = PS[:, 0:2048]
    sc_free = {0: (s_aex, 0), 1: (s_aex, 0)}
    ring_free = {i: (s_tav, 0) for i in range(2)}
    av_free = {0: [], 1: []}
    stT_rd = {0: (s_tsc, 0), 1: (s_tsc, 0)}
    state_ok = {0: 0, 1: 0}
    oT_free = (s_tt, 0)
    qT_rd = (s_tsc, 0)
    gc = [0]
    rbc_rd = [(s_vn, 0)]
    pending = [None]
    ti_v, ti_a = {}, {}
    ti = 0

    def emit_norm(unit):
        pair, st2, slot, tavn = unit
        aop(lr, rsA[slot], AF.Ln,
            waits=[(s_tav, tavn), (s_an, s_an.n), (s_nbc, s_nbc.n)],
            incs=[(s_an, 1)])
        aop(rinv[slot][0], lr, AF.Exp, scale=-1.0,
            waits=[(s_an, s_an.n), (s_nbc, s_nbc.n)], incs=[(s_an, 1)])
        anA = s_an.n
        aop(lr, rsB[slot], AF.Ln, waits=[(s_an, s_an.n)], incs=[(s_an, 1)])
        aop(rinv[slot][1], lr, AF.Exp, scale=-1.0,
            waits=[(s_an, s_an.n), (s_nbc, s_nbc.n)], incs=[(s_an, 1)])
        anB = s_an.n
        dstA = stT[slot][0:64, :] if st2 == 0 else \
            O_T[0:64, pair * 512:(pair + 1) * 512]
        dstB = stT[slot][64:128, :] if st2 == 0 else \
            O_T[64:128, pair * 512:(pair + 1) * 512]
        wdst = [stT_rd[slot]] if st2 == 0 else [oT_free]
        vop("tensor_copy", su[slot][0:64, :], AVb[slot][0:64, 0:512],
            waits=[(s_tav, tavn), (s_vn, s_vn.n)], incs=[(s_vn, 1)])
        vop("tensor_copy", su[slot][64:128, :], avB[slot],
            waits=[(s_tav, tavn), (s_vn, s_vn.n)], incs=[(s_vn, 1)])
        vsu = s_vn.n
        r0 = rinv[slot][0]
        r1 = rinv[slot][1]
        dmaP(rbcA_sb,
             bass.AP(tensor=r0.tensor, offset=r0.offset,
                     ap=[[r0.ap[0][0], 1], [0, 64], [1, 512]]),
             waits=[(s_an, anA), rbc_rd[0]], incs=[(s_nbc, 16)])
        dmaP(rbcB_sb,
             bass.AP(tensor=r1.tensor, offset=r1.offset,
                     ap=[[r1.ap[0][0], 1], [0, 64], [1, 512]]),
             waits=[(s_an, anB)], incs=[(s_nbc, 16)])
        nbcn = s_nbc.n
        vop("tensor_tensor", dstA, su[slot][0:64, :], rbcA_sb, ALU.mult,
            waits=[(s_nbc, nbcn), (s_vn, s_vn.n)] + wdst, incs=[(s_vn, 1)])
        vop("tensor_tensor", dstB, su[slot][64:128, :], rbcB_sb, ALU.mult,
            waits=[(s_nbc, nbcn), (s_vn, s_vn.n)] + wdst, incs=[(s_vn, 1)])
        rbc_rd[0] = (s_vn, s_vn.n)
        av_free[slot] = [(s_vn, vsu), (s_an, anB)]
        if st2 == 0:
            state_ok[slot] = s_vn.n

    def emit_unit(qb, pair, st2, slot, qtn):
        LAG = 2
        expcnt = {}
        for it in range(NT + LAG):
            c = it
            if c < NT:
                scs = c % 2
                w = [sc_free[scs], qtn, kt_done[pair]]
                if st2 == 1:
                    w.append((s_vn, state_ok[slot]))
                rhs = qTb[:, pair * 512:(pair + 1) * 512] if st2 == 0 \
                    else stT[slot][:, :]
                mm(SC[scs][:, 0:512],
                   kTs(pair, 0, 64, c * 128, (c + 1) * 128), rhs[0:64, :],
                   start=True, stop=True, waits=w, tp=(0, 0))
                mm(SC[scs][:, 512:1024],
                   kTs(pair, 64, 128, c * 128, (c + 1) * 128), rhs[64:128, :],
                   start=True, stop=True, tp=(64, 0), incs=[(s_tsc, 1)])
                if c % 2 == 1:
                    prs = (c // 2) % 2
                    aop(ring[:, prs * 2048:(prs + 1) * 2048], SCF, AF.Exp,
                        waits=[(s_tsc, s_tsc.n), ring_free[prs],
                               (s_aex, s_aex.n)], incs=[(s_aex, 1)])
                    sc_free[0] = sc_free[1] = (s_aex, s_aex.n)
                    expcnt[c - 1] = expcnt[c] = s_aex.n
                gc[0] += 1
            ca = it - LAG
            if ca >= 0:
                aw = [(s_aex, expcnt[ca])]
                if ca == 0:
                    aw += av_free[slot] + [(s_setc, 3), (s_v3, V3F)]
                vaA = V2[:, ca * D + 2 * pair * HD:
                         ca * D + 2 * pair * HD + HD]
                vaB = V2[:, ca * D + (2 * pair + 1) * HD:
                         ca * D + (2 * pair + 1) * HD + HD]
                rbase = ((ca // 2) % 2) * 2048 + (ca % 2) * 1024
                rgA = ring[:, rbase:rbase + 512]
                rgB = ring[:, rbase + 512:rbase + 1024]
                mm(avA[slot], vaA, rgA, start=(ca == 0), stop=(ca == NT - 1),
                   waits=aw)
                mm(avB[slot], vaB, rgB, start=(ca == 0), stop=(ca == NT - 1),
                   tp=(0, 64))
                mm(rsB[slot], ones_col, rgB, start=(ca == 0),
                   stop=(ca == NT - 1), tp=(0, 0))
                mm(rsA[slot], ones_col, rgA, start=(ca == 0),
                   stop=(ca == NT - 1), tp=(0, 64), incs=[(s_tav, 4)])
                if ca % 2 == 1:
                    ring_free[(ca // 2) % 2] = (s_tav, s_tav.n)
            if it == DEFER_AT and pending[0] is not None and not DEFER_OFF:
                emit_norm(pending[0])
                pending[0] = None
        if st2 == 1:
            stT_rd[slot] = (s_tsc, s_tsc.n)
        tavn = s_tav.n
        if DEFER_OFF:
            emit_norm((pair, st2, slot, tavn))
        else:
            pending[0] = (pair, st2, slot, tavn)

    def emit_qT(qb):
        for cc in range(NCH):
            dmaS(qTb[:, cc * 512:(cc + 1) * 512],
                 QH_SP[qb * 512:(qb + 1) * 512, cc * 128:(cc + 1) * 128],
                 waits=[(s_qsp[0], 16 * ((NT + 1) // 2)),
                        (s_qsp[1], 16 * (NT // 2)), qT_rd, (s_v3, V3F)],
                 incs=[(s_qT[0], 16)], transpose=True)
        return (s_qT[0], s_qT[0].n)

    qtn = emit_qT(0)
    for cc in range(1, 4):
        load_kT(cc, dmaS)
    for qb in range(NB):
        qtn_next = qtn
        for duo in range(0, NPAIR, 2):
            emit_unit(qb, duo, 0, 0, qtn)
            emit_unit(qb, duo + 1, 0, 1, qtn)
            if duo == NPAIR - 2:
                qT_rd = (s_tsc, s_tsc.n)
                if qb + 1 < NB:
                    qtn_next = emit_qT(qb + 1)
            emit_unit(qb, duo, 1, 0, qtn)
            emit_unit(qb, duo + 1, 1, 1, qtn)
        if pending[0] is not None:
            emit_norm(pending[0])
            pending[0] = None
        oT_ready = s_vn.n
        for tw in range(4):
            p = ti % 2
            row0 = qb * 512 + tw * 128
            dmaS(x4[p], X[row0:row0 + 128, :],
                 waits=[(s_vt, ti_v.get(ti - 2, 0)),
                        (s_at, ti_a.get(ti - 2, 0))], incs=[(s_xt[p], 16)])
            xt_n = (s_xt[p], s_xt[p].n)
            dmaS(g4[p], G_SP[row0:row0 + 128, :],
                 waits=[(s_gsp[0], 16 * ((NT + 1) // 2)),
                        (s_gsp[1], 16 * (NT // 2)),
                        (s_vt, ti_v.get(ti - 1, 0))],
                 incs=[(s_gt[p], 16)])
            gt_n = (s_gt[p], s_gt[p].n)
            for pr in range(NPAIR):
                for dh in range(2):
                    mm(AVb[1][:, dh * 512:(dh + 1) * 512],
                       O_T[:, pr * 512 + tw * 128:pr * 512 + tw * 128 + 128],
                       Wo_sb[:, pr * D + dh * 512:pr * D + (dh + 1) * 512],
                       start=(pr == 0), stop=False,
                       waits=[(s_vn, oT_ready), (s_v3, V3F), (s_wo4, WO4),
                              (s_an, s_an.n),
                              (s_vt, ti_v.get(ti - 1, 0))] + av_free[1])
            for dh in range(2):
                mm(AVb[1][:, dh * 512:(dh + 1) * 512], ones_row[:, 0:128],
                   bo_row[:, dh * 512:(dh + 1) * 512], start=False, stop=True,
                   waits=[(s_set, SETUP_N), (s_setp, SETP_N), (s_setc, 3)],
                   incs=[(s_tt, 1)] if dh == 1 else ())
            ttn = s_tt.n
            st = ST4[p]

            def c4(i, st=st):
                return st[:, i:i + 1]
            wl = [(s_vt, ti_v.get(ti - 1, 0)), (s_at, ti_a.get(ti - 1, 0))]
            vop("reduce_sum", c4(0), x4[p], AX.X,
                waits=[xt_n, (s_vt, s_vt.n)] + wl, incs=[(s_vt, 1)])
            aop(xn4[p], x4[p], AF.Square, accum=c4(9),
                waits=[xt_n, (s_at, s_at.n)] + wl, incs=[(s_at, 1)])
            asq = s_at.n
            vop("tensor_scalar", c4(1), c4(0), 1.0 / D, None, ALU.mult,
                waits=[(s_vt, s_vt.n)], incs=[(s_vt, 1)])
            vop("tensor_tensor", c4(2), c4(0), c4(0), ALU.mult,
                waits=[(s_vt, s_vt.n)], incs=[(s_vt, 1)])
            vop("tensor_scalar", c4(3), c4(2), 1.0 / (D * D), None, ALU.mult,
                waits=[(s_vt, s_vt.n)], incs=[(s_vt, 1)])
            vop("scalar_tensor_tensor", c4(4), c4(9), 1.0 / D, c4(3),
                ALU.mult, ALU.subtract,
                waits=[(s_at, asq), (s_vt, s_vt.n)], incs=[(s_vt, 1)])
            vop("tensor_scalar", c4(5), c4(4), EPS, None, ALU.add,
                waits=[(s_vt, s_vt.n)], incs=[(s_vt, 1)])
            vv4 = s_vt.n
            aop(c4(10), c4(5), AF.Ln,
                waits=[(s_vt, vv4), (s_at, s_at.n)], incs=[(s_at, 1)])
            aop(c4(10), c4(10), AF.Exp, scale=-0.5, waits=[(s_at, s_at.n)],
                incs=[(s_at, 1)])
            ar14 = s_at.n
            vop("tensor_scalar", xn4[p], x4[p], c4(1), c4(10), ALU.subtract,
                ALU.mult, waits=[(s_at, ar14), (s_vt, s_vt.n)],
                incs=[(s_vt, 1)])
            vop("tensor_tensor", o4[p], AVb[1][:, :], xn4[p], ALU.add,
                waits=[(s_tt, ttn), (s_vt, s_vt.n),
                       (s_out[0], s_out[0].n), (s_out[1], s_out[1].n)],
                incs=[(s_vt, 1)])
            vop("tensor_tensor", xn4[p], o4[p], g4[p], ALU.mult,
                waits=[gt_n, (s_vt, s_vt.n)], incs=[(s_vt, 1)])
            vop("tensor_tensor", o4[p], xn4[p], x4[p], ALU.add,
                waits=[(s_vt, s_vt.n)], incs=[(s_vt, 1)])
            ti_v[ti] = s_vt.n
            ti_a[ti] = s_at.n
            dmaS(OUT[row0:row0 + 128, :], o4[p], waits=[(s_vt, ti_v[ti])],
                 incs=[(s_out[p], 16)])
            ti += 1
        av_free[1] = [(s_vt, ti_v[ti - 1]), (s_an, s_an.n)]
        oT_free = (s_tt, s_tt.n)
        qtn = qtn_next

    sch.op("S", lambda e: e.nop(),
           waits=[(s_out[0], 16 * ((NT + 1) // 2)),
                  (s_out[1], 16 * (NT // 2))])

    with nc.Block() as blk:
        blk.tensor(lambda e: sch.replay("T", e))
        blk.vector(lambda e: sch.replay("V", e))
        blk.scalar(lambda e: sch.replay("A", e))
        blk.sync(lambda e: sch.replay("S", e))
        blk.gpsimd(lambda e: sch.replay("P", e))
    return nc, stack


_CACHE = {}


def _get(S):
    if S not in _CACHE:
        _CACHE[S] = _build(S)
    return _CACHE[S]


def kernel(**inputs):
    ip = {k: np.asarray(v) for k, v in inputs.items()}
    x = ip["L0_attn_hidden_states"]
    B, S, D_ = x.shape
    C = np.ascontiguousarray
    common = {
        "wq": C(ip["L0_attn_Wq"]), "wk": C(ip["L0_attn_Wk"]),
        "wv": C(ip["L0_attn_Wv"]), "wo": C(ip["L0_attn_Wo"]),
        "lnw": C(ip["L0_attn_ln_w"].reshape(1, -1)),
        "lnb": C(ip["L0_attn_ln_b"].reshape(1, -1)),
        "bq": C(ip["L0_attn_bq"].reshape(1, -1)),
        "bk": C(ip["L0_attn_bk"].reshape(1, -1)),
        "bv": C(ip["L0_attn_bv"].reshape(1, -1)),
        "bo": C(ip["L0_attn_bo"].reshape(1, -1)),
        "gwt": C(ip["L0_ffn_gate_w"].T),
        "eb": C(ip["L0_ffn_expert_bias"].reshape(1, -1)),
        "w1t": C(ip["L0_ffn_w1"].transpose(0, 2, 1)),
        "w3t": C(ip["L0_ffn_w3"].transpose(0, 2, 1)),
        "w2t": C(ip["L0_ffn_w2"].transpose(0, 2, 1)),
        "b1": C(ip["L0_ffn_b1"]), "b3": C(ip["L0_ffn_b3"]),
        "b2": C(ip["L0_ffn_b2"]),
        "sw1t": C(ip["L0_ffn_sw1"].T), "sw3t": C(ip["L0_ffn_sw3"].T),
        "sw2t": C(ip["L0_ffn_sw2"].T),
        "sb1": C(ip["L0_ffn_sb1"].reshape(1, -1)),
        "sb3": C(ip["L0_ffn_sb3"].reshape(1, -1)),
        "sb2": C(ip["L0_ffn_sb2"].reshape(1, -1)),
    }
    nc, _ = _get(S)
    in_maps = [dict(common, x=C(x[b])) for b in range(B)]
    res = run_bass_kernel_spmd(nc, in_maps, core_ids=list(range(B)))
    return np.stack([r["out"] for r in res.results], axis=0)


# version 30
# speedup vs baseline: 1.1854x; 1.1854x over previous
import numpy as np
import ml_dtypes
import contextlib
import concourse.bass as bass
import concourse.mybir as mybir
from concourse.bass_utils import run_bass_kernel_spmd

F32 = mybir.dt.float32
BF16 = mybir.dt.bfloat16
AF = mybir.ActivationFunctionType
ALU = mybir.AluOpType
AX = mybir.AxisListType

D = 1024
NCH = 8
NH = 16
HD = 64
NE = 16
DFF = 512
FDFF = 1024
EPS = 1e-5
STRICT_WAITS = False
DEFER_OFF = False
DEFER_AT = 3
kernel_NRING = 3


class Sem:
    __slots__ = ("h", "n", "name")

    def __init__(self, h, name):
        self.h, self.n, self.name = h, 0, name


class Sched:
    ENGS = ("T", "V", "A", "S", "P")

    def __init__(self):
        self.ops = {k: [] for k in self.ENGS}
        self.hw = {k: {} for k in self.ENGS}
        self.last = {k: None for k in self.ENGS}
        self.lastinc = {k: None for k in self.ENGS}
        self.dirty = {k: False for k in self.ENGS}
        self.pending = {k: None for k in self.ENGS}

    def op(self, eng, build, waits=(), incs=()):
        wl = []
        if self.pending[eng]:
            wl += self.pending[eng]
            self.pending[eng] = None
        wl += list(waits)
        best = {}
        for s, v in wl:
            if v > 0 and v > best.get(id(s), (None, 0))[1]:
                best[id(s)] = (s, v)
        if globals().get('STRICT_WAITS', False):
            wfin = [(s.h, v) for s, v in best.values()]
        else:
            wfin = []
            for s, v in best.values():
                if v > self.hw[eng].get(id(s), 0):
                    self.hw[eng][id(s)] = v
                    wfin.append((s.h, v))
        il = []
        for s, k in incs:
            s.n += k
            il.append((s.h, k))
        ent = {"w": wfin, "i": il, "b": build}
        self.ops[eng].append(ent)
        self.last[eng] = ent
        if incs:
            s0, _ = incs[0]
            self.lastinc[eng] = (s0, s0.n)
        self.dirty[eng] = True
        return ent

    def barrier(self, sem=None, extra=()):
        pend = list(extra)
        for eng in ("T", "V", "A"):
            if self.dirty[eng] and self.lastinc[eng] is not None:
                pend.append(self.lastinc[eng])
        for eng in self.ENGS:
            self.dirty[eng] = False
            self.pending[eng] = list(pend)

    def replay(self, eng, e):
        for ent in self.ops[eng]:
            for h, v in ent["w"]:
                e.wait_ge(h, v)
            ins = ent["b"](e)
            for h, k in ent["i"]:
                ins.then_inc(h, k)


class Arena:
    def __init__(self, h):
        self.h = h
        self.h32 = h.bitcast(F32)
        self.off = 0
        self.marks = []
        self.cap = h.shape[1] * 2

    def alloc(self, nbytes):
        nbytes = (nbytes + 63) // 64 * 64
        o = self.off
        self.off += nbytes
        assert self.off <= self.cap, f"arena overflow {self.off} > {self.cap}"
        return o

    def bf(self, off, p0, p1, cols):
        return self.h[p0:p1, off // 2: off // 2 + cols]

    def f32(self, off, p0, p1, cols):
        return self.h32[p0:p1, off // 4: off // 4 + cols]

    def push(self):
        self.marks.append(self.off)

    def pop(self):
        self.off = self.marks.pop()


def _build(S):
    NT = S // 128
    NB = S // 512
    QT = min(S, 1024)
    NQ = S // QT
    NBQ = QT // 512
    NTQ = QT // 128
    NPAIR = NH // 2

    nc = bass.Bass("TRN2", target_bir_lowering=False)
    dt = nc.dram_tensor
    X = dt("x", [S, D], F32, kind="ExternalInput").ap()
    OUT = dt("out", [S, D], F32, kind="ExternalOutput").ap()
    WQd = dt("wq", [D, D], F32, kind="ExternalInput").ap()
    WKd = dt("wk", [D, D], F32, kind="ExternalInput").ap()
    WVd = dt("wv", [D, D], F32, kind="ExternalInput").ap()
    WOd = dt("wo", [D, D], F32, kind="ExternalInput").ap()
    LNW = dt("lnw", [1, D], F32, kind="ExternalInput").ap()
    LNB = dt("lnb", [1, D], F32, kind="ExternalInput").ap()
    BQd = dt("bq", [1, D], F32, kind="ExternalInput").ap()
    BKd = dt("bk", [1, D], F32, kind="ExternalInput").ap()
    BVd = dt("bv", [1, D], F32, kind="ExternalInput").ap()
    BOd = dt("bo", [1, D], F32, kind="ExternalInput").ap()
    GWT = dt("gwt", [D, NE], F32, kind="ExternalInput").ap()
    EBd = dt("eb", [1, NE], F32, kind="ExternalInput").ap()
    W1T = dt("w1t", [NE, D, DFF], F32, kind="ExternalInput").ap()
    W3T = dt("w3t", [NE, D, DFF], F32, kind="ExternalInput").ap()
    W2T = dt("w2t", [NE, DFF, D], F32, kind="ExternalInput").ap()
    B1d = dt("b1", [NE, DFF], F32, kind="ExternalInput").ap()
    B3d = dt("b3", [NE, DFF], F32, kind="ExternalInput").ap()
    B2d = dt("b2", [NE, D], F32, kind="ExternalInput").ap()
    SW1T = dt("sw1t", [D, FDFF], F32, kind="ExternalInput").ap()
    SW3T = dt("sw3t", [D, FDFF], F32, kind="ExternalInput").ap()
    SW2T = dt("sw2t", [FDFF, D], F32, kind="ExternalInput").ap()
    SB1d = dt("sb1", [1, FDFF], F32, kind="ExternalInput").ap()
    SB3d = dt("sb3", [1, FDFF], F32, kind="ExternalInput").ap()
    SB2d = dt("sb2", [1, D], F32, kind="ExternalInput").ap()

    XN_SP = dt("xn_sp", [S, D], BF16, kind="Internal").ap()
    QH_SP = dt("qh_sp", [S, D], BF16, kind="Internal").ap()
    KH_SP = dt("kh_sp", [S, D], BF16, kind="Internal").ap()
    va_cols = (S // 128) * D
    VA_SP = dt("va_sp", [128, va_cols], BF16, kind="Internal").ap()
    G_SP = dt("g_sp", [S, D], F32, kind="Internal").ap()
    IDEN = nc.inline_tensor(np.eye(128, dtype=ml_dtypes.bfloat16), "iden").ap()

    stack = contextlib.ExitStack()

    def sem(name):
        return Sem(stack.enter_context(nc.semaphore(name)), name)

    s_set, s_setp, s_setc = sem("set"), sem("setp"), sem("setc")
    s_bar = [sem(f"bar{i}") for i in range(5)]
    s_x = [sem("sx0"), sem("sx1")]
    s_v0, s_a0 = sem("sv0"), sem("sa0")
    s_xnsp = [sem("sxnsp0"), sem("sxnsp1")]
    s_xnT = sem("sxnT")
    s_xnd = [sem(f"sxnd{c}") for c in range(8)]
    s_t1, s_v1, s_a1 = sem("st1"), sem("sv1"), sem("sa1")
    s_wld = [sem("swld0"), sem("swld1")]
    s_th, s_ah, s_vh = sem("sth"), sem("sah"), sem("svh")
    s_wsh = sem("swsh")
    s_ty, s_vy, s_ag = sem("sty"), sem("svy"), sem("sag")
    s_gsp = [sem("sgsp0"), sem("sgsp1")]
    s_ws = [sem("sws0"), sem("sws1")]
    s_v3, s_a3, s_t3 = sem("sv3"), sem("sa3"), sem("st3")
    s_qsp = [sem("sqsp0"), sem("sqsp1")]
    s_ksp = [sem("sksp0"), sem("sksp1")]
    s_vsp = sem("svsp")
    s_p4l, s_wo4 = sem("sp4l"), sem("swo4")
    s_p4a = sem("sp4a")
    s_ktd = [sem(f"sktd{c}") for c in range(8)]
    s_qT = [sem("sqT0"), sem("sqT1")]
    s_tsc, s_aex, s_tav = sem("stsc"), sem("saex"), sem("stav")
    s_an, s_tn, s_vn = sem("san"), sem("stn"), sem("svn")
    s_nbc = sem("snbc")
    s_tt = sem("stt")
    s_xt = [sem("sxt0"), sem("sxt1")]
    s_gt = [sem("sgt0"), sem("sgt1")]
    s_vt, s_at = sem("svt"), sem("sat")
    s_out = [sem("sout0"), sem("sout1")]

    def all_dma_sems():
        out = []
        for s in (s_set, s_setp, s_xnT, s_vsp, s_p4l, s_wo4, s_p4a):
            out.append((s, s.n))
        for s in s_ktd:
            out.append((s, s.n))
        for pairl in (s_x, s_xnsp, s_wld, s_gsp, s_ws, s_qsp, s_ksp, s_qT,
                      s_xt, s_gt, s_out):
            for s in pairl:
                out.append((s, s.n))
        return out

    rem = nc.sbuf_bytes_remaining
    arena_cols = (rem // 2) // 64 * 64
    AR = Arena(nc.alloc_sbuf_tensor("arena", [128, arena_cols], BF16))

    ones_row = AR.bf(AR.alloc(256), 0, 1, 128)
    ones32 = AR.f32(AR.alloc(256), 0, 1, 64)
    ones_col = AR.bf(AR.alloc(64), 0, 128, 1)
    bo_row = AR.bf(AR.alloc(2048), 0, 1, D)
    R2 = AR.f32(AR.alloc(NT * 4), 0, 128, NT)

    AR.push()
    o_xnT = AR.alloc(NCH * S * 2)
    gatew = AR.f32(AR.alloc(NT * NE * 4), 0, 128, NT * NE)
    gatewT = AR.bf(AR.alloc(S * 2), 0, 16, S)
    iden_sb = AR.bf(AR.alloc(256), 0, 128, 128)
    gwt_sb = AR.bf(AR.alloc(NCH * NE * 2), 0, 128, NCH * NE)
    eb_sb = AR.f32(AR.alloc(NE * 4), 0, 128, NE)
    b1T = AR.bf(AR.alloc(4 * NE * 2), 0, 128, 4 * NE)
    b3T = AR.bf(AR.alloc(4 * NE * 2), 0, 128, 4 * NE)
    sb13T = AR.bf(AR.alloc(16 * 2), 0, 128, 16)
    b2_sb = AR.bf(AR.alloc(D * 2), 0, 16, D)
    sb2_row = AR.bf(AR.alloc(D * 2), 0, 1, D)
    b1s = AR.bf(AR.alloc(DFF * 2), 0, 16, DFF)
    b3s = AR.bf(AR.alloc(DFF * 2), 0, 16, DFF)
    sb13s = AR.bf(AR.alloc(128 * 2), 0, 16, 128)
    o_brow = AR.alloc(3 * D * 2)
    brow = [AR.bf(o_brow + i * D * 2, 0, 1, D) for i in range(3)]
    lnw_sb = AR.f32(AR.alloc(NCH * 4), 0, 128, NCH)
    lnb_sb = AR.bf(AR.alloc(NCH * 2), 0, 128, NCH)
    gsc = AR.f32(AR.alloc(NE * 4), 0, 128, NE)
    gbi = AR.f32(AR.alloc(NE * 4), 0, 128, NE)
    gb2 = AR.f32(AR.alloc(NE * 4), 0, 128, NE)
    gwu = AR.f32(AR.alloc(NE * 4), 0, 128, NE)
    gm = AR.f32(AR.alloc(32), 0, 128, 8)
    gbf = AR.bf(AR.alloc(NT * NE * 2), 0, 128, NT * NE)

    def xnT(c, t0, t1):
        return AR.h[:, o_xnT // 2 + c * S + t0: o_xnT // 2 + c * S + t1]

    PS = nc.alloc_psum_tensor("ps", [128, 4096], F32)
    PSb = PS.bitcast(BF16)

    sch = Sched()

    def mm(out, lhsT, rhs, start, stop, waits=(), incs=(), tp=None):
        def b(e, out=out, lhsT=lhsT, rhs=rhs, start=start, stop=stop, tp=tp):
            return e.matmul(out, lhsT, rhs, start=start, stop=stop,
                            tile_position=tp)
        return sch.op("T", b, waits, incs)

    def dmaS(out, in_, waits=(), incs=(), transpose=False, slow=False):
        def b(e, out=out, in_=in_, t=transpose, slow=slow):
            return e.dma_start(out, in_, transpose=t,
                               allow_slow_non_contiguous=slow)
        return sch.op("S", b, waits, incs)

    def dmaV(out, in_, waits=(), incs=(), transpose=False, slow=False):
        def b(e, out=out, in_=in_, t=transpose, slow=slow):
            return e.dma_start(out, in_, transpose=t,
                               allow_slow_non_contiguous=slow)
        return sch.op("V", b, waits, incs)

    def dmaA(out, in_, waits=(), incs=(), transpose=False, slow=False):
        def b(e, out=out, in_=in_, t=transpose, slow=slow):
            return e.dma_start(out, in_, transpose=t,
                               allow_slow_non_contiguous=slow)
        return sch.op("A", b, waits, incs)

    def dmaP(out, in_, waits=(), incs=(), slow=False):
        def b(e, out=out, in_=in_, slow=slow):
            return e.dma_start(out, in_, allow_slow_non_contiguous=slow)
        return sch.op("P", b, waits, incs)

    def vop(name, *args, waits=(), incs=()):
        def b(e, name=name, args=args):
            return getattr(e, name)(*args)
        return sch.op("V", b, waits, incs)

    def aop(out, in_, func, bias=0.0, scale=1.0, accum=None, waits=(), incs=()):
        def b(e, out=out, in_=in_, func=func, bias=bias, scale=scale, ac=accum):
            return e.activation(out, in_, func, bias=bias, scale=scale,
                                accum_out=ac)
        return sch.op("A", b, waits, incs)

    dmaP(gwt_sb.rearrange("p (c s) -> p c s", s=NE),
         GWT.rearrange("(c p) s -> p c s", p=128), incs=[(s_setp, 16)])
    dmaP(b2_sb, B2d[:, :], incs=[(s_setp, 16)])
    dmaP(sb2_row, SB2d[:, :], incs=[(s_setp, 16)])
    dmaP(bo_row, BOd[:, :], incs=[(s_setp, 16)])
    for i, B_ in enumerate((BQd, BKd, BVd)):
        dmaP(brow[i], B_[:, :], incs=[(s_setp, 16)])
    dmaP(b1s, B1d[:, :], incs=[(s_setp, 16)])
    dmaP(b3s, B3d[:, :], incs=[(s_setp, 16)])
    dmaP(sb13s[0:8, :], SB1d.rearrange("o (f p) -> (o f) p", p=128),
         incs=[(s_setp, 16)])
    dmaP(sb13s[8:16, :], SB3d.rearrange("o (f p) -> (o f) p", p=128),
         incs=[(s_setp, 16)])
    dmaP(lnb_sb.rearrange("p (c u) -> p c u", u=1),
         LNB.rearrange("o (c p u) -> (o p) c u", p=128, u=1),
         incs=[(s_setp, 16)], slow=True)
    dmaS(iden_sb, IDEN[:, :], incs=[(s_set, 16)])
    dmaS(lnw_sb.rearrange("p (c u) -> p c u", u=1),
         LNW.rearrange("o (c p u) -> (o p) c u", p=128, u=1),
         incs=[(s_set, 16)], slow=True)
    eb_bc = bass.AP(tensor=EBd.tensor, offset=0, ap=[[0, 128], [1, NE]])
    dmaS(eb_sb, eb_bc, incs=[(s_set, 16)])
    SETP_N = s_setp.n
    nset0 = s_set.n
    for i in range(4):
        dmaS(b1T[:, i * 16:(i + 1) * 16], b1s[:, i * 128:(i + 1) * 128],
             waits=[(s_set, s_set.n), (s_setp, SETP_N)],
             incs=[(s_set, 16)], transpose=True)
    for i in range(4):
        dmaS(b3T[:, i * 16:(i + 1) * 16], b3s[:, i * 128:(i + 1) * 128],
             waits=[(s_set, s_set.n)], incs=[(s_set, 16)], transpose=True)
    dmaS(sb13T, sb13s, waits=[(s_set, s_set.n)], incs=[(s_set, 16)],
         transpose=True)
    SETUP_N = s_set.n
    vop("memset", ones_row, 1.0, incs=[(s_setc, 1)])
    vop("memset", ones_col, 1.0, incs=[(s_setc, 1)])
    vop("memset", ones32, 1.0, incs=[(s_setc, 1)])

    AR.push()
    x_t = [AR.f32(AR.alloc(D * 4), 0, 128, D) for _ in range(2)]
    xn_bf = [AR.bf(AR.alloc(D * 2), 0, 128, D) for _ in range(2)]
    sq_t = AR.f32(AR.alloc(D * 4), 0, 128, D)
    STT = [AR.f32(AR.alloc(64), 0, 128, 16) for _ in range(2)]
    v_done, a_done = {}, {}
    for j in range(NT):
        p = j % 2
        st = STT[p]

        def c(i, st=st):
            return st[:, i:i + 1]
        dmaS(x_t[p], X[j * 128:(j + 1) * 128, :],
             waits=[(s_v0, v_done.get(j - 2, 0)), (s_a0, a_done.get(j - 2, 0))],
             incs=[(s_x[p], 16)])
        wx = [(s_x[p], 16 * (j // 2 + 1))]
        wl = [(s_v0, v_done.get(j - 1, 0)), (s_a0, a_done.get(j - 1, 0))]
        vop("reduce_sum", c(0), x_t[p], AX.X, waits=wx + wl, incs=[(s_v0, 1)])
        aop(sq_t, x_t[p], AF.Square, accum=c(9),
            waits=wx + wl + [(s_a0, s_a0.n)], incs=[(s_a0, 1)])
        asq = s_a0.n
        vop("tensor_scalar", c(1), c(0), 1.0 / D, None, ALU.mult,
            waits=[(s_v0, s_v0.n)], incs=[(s_v0, 1)])
        vop("tensor_tensor", c(2), c(0), c(0), ALU.mult,
            waits=[(s_v0, s_v0.n)], incs=[(s_v0, 1)])
        vop("tensor_scalar", c(3), c(2), 1.0 / (D * D), None, ALU.mult,
            waits=[(s_v0, s_v0.n)], incs=[(s_v0, 1)])
        vop("scalar_tensor_tensor", c(4), c(9), 1.0 / D, c(3), ALU.mult,
            ALU.subtract, waits=[(s_a0, asq), (s_v0, s_v0.n)],
            incs=[(s_v0, 1)])
        vop("tensor_scalar", c(5), c(4), EPS, None, ALU.add,
            waits=[(s_v0, s_v0.n)], incs=[(s_v0, 1)])
        vvar = s_v0.n
        aop(c(10), c(5), AF.Ln, waits=[(s_v0, vvar), (s_a0, s_a0.n)],
            incs=[(s_a0, 1)])
        aop(c(10), c(10), AF.Exp, scale=-0.5, waits=[(s_a0, s_a0.n)],
            incs=[(s_a0, 1)])
        ar1 = s_a0.n
        vop("reciprocal", c(6), c(5), waits=[(s_v0, s_v0.n)], incs=[(s_v0, 1)])
        vop("tensor_tensor", c(7), c(4), c(6), ALU.mult,
            waits=[(s_v0, s_v0.n)], incs=[(s_v0, 1)])
        vop("tensor_scalar", c(8), c(7), EPS, None, ALU.add,
            waits=[(s_v0, s_v0.n)], incs=[(s_v0, 1)])
        vt2 = s_v0.n
        aop(R2[:, j:j + 1], c(8), AF.Ln, waits=[(s_v0, vt2), (s_a0, s_a0.n)],
            incs=[(s_a0, 1)])
        aop(R2[:, j:j + 1], R2[:, j:j + 1], AF.Exp, scale=-0.5,
            waits=[(s_a0, s_a0.n)], incs=[(s_a0, 1)])
        vop("tensor_scalar", xn_bf[p], x_t[p], c(1), c(10), ALU.subtract,
            ALU.mult, waits=[(s_a0, ar1), (s_v0, s_v0.n),
                             (s_xnsp[p], 16 * (j // 2))],
            incs=[(s_v0, 1)])
        v_done[j], a_done[j] = s_v0.n, s_a0.n
        dmaS(XN_SP[j * 128:(j + 1) * 128, :], xn_bf[p],
             waits=[(s_v0, v_done[j])], incs=[(s_xnsp[p], 16)])
    xnd_done = {}
    for cc in range(NCH):
        dmaF = dmaS
        dmaF(xnT(cc, 0, S), XN_SP[:, cc * 128:(cc + 1) * 128],
             waits=[(s_xnsp[0], 16 * ((NT + 1) // 2)),
                    (s_xnsp[1], 16 * (NT // 2))],
             incs=[(s_xnd[cc], 16)], transpose=True)
        xnd_done[cc] = (s_xnd[cc], s_xnd[cc].n)

    sch.barrier(s_bar[0], extra=all_dma_sems())
    AR.pop()
    lg_ps = [PS[:, 2048:2048 + NE], PS[:, 2560:2560 + NE]]
    tr_ps = PSb[0:16, 6144:6272]
    v1_done, t1tr = {}, {}
    for j in range(NT):
        p = j % 2
        for cc in range(NCH):
            mm(lg_ps[p], xnT(cc, j * 128, (j + 1) * 128),
               gwt_sb[:, cc * NE:(cc + 1) * NE], start=(cc == 0),
               stop=(cc == NCH - 1),
               waits=[xnd_done[cc], (s_set, SETUP_N), (s_setp, SETP_N),
                      (s_v1, v1_done.get(j - 2, 0))],
               incs=[(s_t1, 1)] if cc == NCH - 1 else ())
        tmm = s_t1.n
        wl = [(s_v1, v1_done.get(j - 1, 0))]
        aop(gsc, lg_ps[p], AF.Sigmoid,
            waits=[(s_t1, tmm), (s_a1, s_a1.n)] + wl, incs=[(s_a1, 1)])
        asig = s_a1.n
        vop("tensor_tensor", gbi, lg_ps[p], eb_sb, ALU.add,
            waits=[(s_t1, tmm), (s_set, SETUP_N), (s_setp, SETP_N)] + wl,
            incs=[(s_v1, 1)])
        vop("reduce_max", gm[:, 0:1], gbi, AX.X, waits=[(s_v1, s_v1.n)],
            incs=[(s_v1, 1)])
        vop("tensor_scalar", gwu, gbi, gm[:, 0:1], None, ALU.is_equal,
            waits=[(s_v1, s_v1.n)], incs=[(s_v1, 1)])
        vop("scalar_tensor_tensor", gb2, gwu, -1e30, gbi, ALU.mult, ALU.add,
            waits=[(s_v1, s_v1.n)], incs=[(s_v1, 1)])
        vop("reduce_max", gm[:, 1:2], gb2, AX.X, waits=[(s_v1, s_v1.n)],
            incs=[(s_v1, 1)])
        vop("tensor_scalar", gb2, gb2, gm[:, 1:2], None, ALU.is_equal,
            waits=[(s_v1, s_v1.n)], incs=[(s_v1, 1)])
        vop("tensor_tensor", gwu, gwu, gb2, ALU.add, waits=[(s_v1, s_v1.n)],
            incs=[(s_v1, 1)])
        vop("tensor_tensor", gwu, gwu, gsc, ALU.mult,
            waits=[(s_v1, s_v1.n), (s_a1, asig)], incs=[(s_v1, 1)])
        vop("reduce_sum", gm[:, 2:3], gwu, AX.X, waits=[(s_v1, s_v1.n)],
            incs=[(s_v1, 1)])
        vop("tensor_scalar", gm[:, 2:3], gm[:, 2:3], 1e-10, None, ALU.add,
            waits=[(s_v1, s_v1.n)], incs=[(s_v1, 1)])
        vop("reciprocal", gm[:, 3:4], gm[:, 2:3], waits=[(s_v1, s_v1.n)],
            incs=[(s_v1, 1)])
        vop("tensor_scalar", gatew[:, j * NE:(j + 1) * NE], gwu, gm[:, 3:4],
            None, ALU.mult, waits=[(s_v1, s_v1.n)], incs=[(s_v1, 1)])
        vop("tensor_copy", gbf[:, j * NE:(j + 1) * NE],
            gatew[:, j * NE:(j + 1) * NE],
            waits=[(s_v1, s_v1.n)], incs=[(s_v1, 1)])
        v1_done[j] = s_v1.n
    for j in range(NT):
        sch.op("T", lambda e, j=j: e.transpose(
            tr_ps, gbf[:, j * NE:(j + 1) * NE], iden_sb),
            [(s_v1, v1_done[j]), (s_set, SETUP_N),
             (s_v1, t1tr.get(j - 1, 0))],
            incs=[(s_t1, 1)])
        tn_ = s_t1.n
        vop("tensor_copy", gatewT[:, j * 128:(j + 1) * 128], tr_ps,
            waits=[(s_t1, tn_), (s_v1, s_v1.n)], incs=[(s_v1, 1)])
        t1tr[j] = s_v1.n
        V1N = s_v1.n

    P1T = s_t1.n
    P1V = s_v1.n
    AR.push()
    o_y = AR.alloc(NTQ * D * 4)
    y_acc = AR.f32(o_y, 0, 128, NTQ * D)
    o_wb = AR.alloc(2 * 24 * 1024)
    o_gh = AR.alloc(8 * 512 * 2)
    ghT = AR.bf(o_gh, 0, 128, 8 * 512)
    s1_t = [AR.f32(AR.alloc(512 * 4), 0, 128, 512) for _ in range(2)]
    g_t = [AR.f32(AR.alloc(D * 4), 0, 128, D) for _ in range(2)]

    def wsl(base_bytes, c, span, f0, f1):
        o = (o_wb + base_bytes) // 2
        return AR.h[:, o + c * span + f0: o + c * span + f1]

    h_ps = [(PS[:, 0:512], PS[:, 512:1024]),
            (PS[:, 1024:1536], PS[:, 1536:2048])]
    y_ps = [PS[:, 2048:3072], PS[:, 3072:4096]]
    wfree = {0: (s_set, 0), 1: (s_set, 0)}
    lded = {}

    def load_expert(eidx, buf):
        base = buf * 24 * 1024
        dmaP(AR.h[:, (o_wb + base) // 2:(o_wb + base) // 2 + 4096]
             .rearrange("p (c f) -> p c f", f=DFF),
             W1T[eidx].rearrange("(c p) f -> p c f", p=128),
             waits=[wfree[buf]], incs=[(s_wld[buf], 16)])
        dmaP(AR.h[:, (o_wb + base + 8192) // 2:
                  (o_wb + base + 8192) // 2 + 4096]
             .rearrange("p (c f) -> p c f", f=DFF),
             W3T[eidx].rearrange("(c p) f -> p c f", p=128),
             incs=[(s_wld[buf], 16)])
        dmaP(AR.h[:, (o_wb + base + 16384) // 2:
                  (o_wb + base + 16384) // 2 + 4096]
             .rearrange("p (c d) -> p c d", d=D),
             W2T[eidx].rearrange("(c p) d -> p c d", p=128),
             incs=[(s_wld[buf], 16)])
        lded[eidx] = (s_wld[buf], s_wld[buf].n)

    def load_shared(part):
        if part == 0:
            items = ((0, SW1T, 0, 8, FDFF, 0),
                     (16384, SW3T, 0, 4, FDFF, 0))
        elif part == 1:
            items = ((24576, SW3T, 4, 4, FDFF, 1),
                     (32768, SW2T, 0, 4, D, 1))
        else:
            items = ((40960, SW2T, 4, 4, D, 2),)
        sems = {0: s_wld[0], 1: s_wld[1], 2: s_wsh}
        for off, W_, c0, ncnk, span, sb in items:
            dmaP(AR.h[:, (o_wb + off) // 2:
                      (o_wb + off) // 2 + ncnk * span]
                 .rearrange("p (c f) -> p c f", f=span),
                 W_[c0 * 128:(c0 + ncnk) * 128, :]
                 .rearrange("(c p) f -> p c f", p=128),
                 waits=[(s_th, s_th.n), (s_ty, s_ty.n)],
                 incs=[(sems[sb], 16)])
        sh_ld[part] = ([(s_wsh, s_wsh.n)] if part == 2 else
                       [(s_wld[0], s_wld[0].n), (s_wld[1], s_wld[1].n)])

    sh_ld = {}
    hfreeA = {0: (s_ah, 0), 1: (s_ah, 0)}
    hfreeV = {0: (s_vh, 0), 1: (s_vh, 0)}
    yfree = {0: (s_v1, P1V), 1: (s_v1, P1V)}
    ghfree = {i: (s_ty, 0) for i in range(8)}
    vy_last = {}
    load_expert(0, 0)
    for q in range(NQ):
        tok0 = q * QT
        for ei in range(17):
            is_sh = ei == 16
            buf = ei % 2
            nfc = 8 if is_sh else 4
            if ei + 1 < 16:
                load_expert(ei + 1, (ei + 1) % 2)
            elif ei + 1 == 16:
                load_shared(0)
            if is_sh:
                load_shared(2)
            gh_ready = {}
            for blk in range(NBQ):
                bt0 = tok0 + blk * 512
                for fc in range(nfc):
                    sp = fc % 2
                    slot = fc if is_sh else (blk % 2) * 4 + fc
                    pa, pb = h_ps[sp]
                    for cc in range(NCH):
                        if is_sh:
                            l1 = wsl(0, cc, FDFF, fc * 128, (fc + 1) * 128)
                            l3 = wsl(16384, cc, FDFF, fc * 128, (fc + 1) * 128)
                        else:
                            l1 = wsl(buf * 24576, cc, DFF,
                                     fc * 128, (fc + 1) * 128)
                            l3 = wsl(buf * 24576 + 8192, cc, DFF,
                                     fc * 128, (fc + 1) * 128)
                        if is_sh:
                            w = sh_ld[1] + [
                                 xnd_done[cc], hfreeA[sp], hfreeV[sp]]
                        else:
                            w = [lded[ei],
                                 xnd_done[cc], hfreeA[sp], hfreeV[sp]]
                        mm(pa, l1, xnT(cc, bt0, bt0 + 512), start=(cc == 0),
                           stop=(cc == NCH - 1), waits=w,
                           incs=[(s_th, 1)] if cc == NCH - 1 else ())
                        mm(pb, l3, xnT(cc, bt0, bt0 + 512), start=(cc == 0),
                           stop=(cc == NCH - 1),
                           incs=[(s_th, 1)] if cc == NCH - 1 else ())
                    thn = s_th.n
                    if is_sh:
                        bc1 = sb13T[:, fc:fc + 1]
                        bc3 = sb13T[:, 8 + fc:9 + fc]
                    else:
                        bc1 = b1T[:, fc * NE + ei:fc * NE + ei + 1]
                        bc3 = b3T[:, fc * NE + ei:fc * NE + ei + 1]
                    aop(s1_t[sp], pa, AF.Silu, bias=bc1,
                        waits=[(s_th, thn), (s_set, SETUP_N), (s_setp, SETP_N),
                               (s_vh, hfreeV[sp][1]), (s_ah, s_ah.n)],
                        incs=[(s_ah, 1)])
                    hfreeA[sp] = (s_ah, s_ah.n)
                    vop("scalar_tensor_tensor",
                        ghT[:, slot * 512:(slot + 1) * 512], pb, bc3,
                        s1_t[sp], ALU.add, ALU.mult,
                        waits=[(s_ah, s_ah.n), (s_th, thn), ghfree[slot],
                               (s_vy, s_vy.n)], incs=[(s_vh, 1)])
                    hfreeV[sp] = (s_vh, s_vh.n)
                    gh_ready[(blk, fc)] = s_vh.n
                if is_sh and NBQ > 1:
                    break
            nblk_h = 1 if is_sh else NBQ
            for rep in range(NBQ if is_sh else 1):
                pass
            if ei == 15:
                load_shared(1)
            w2blocks = range(NBQ)
            for blk in w2blocks:
                bt0 = tok0 + blk * 512
                if is_sh and blk > 0:
                    for fc in range(nfc):
                        sp = fc % 2
                        slot = fc
                        pa, pb = h_ps[sp]
                        for cc in range(NCH):
                            l1 = wsl(0, cc, FDFF, fc * 128, (fc + 1) * 128)
                            l3 = wsl(16384, cc, FDFF,
                                     fc * 128, (fc + 1) * 128)
                            w = sh_ld[1] + [
                                 xnd_done[cc], hfreeA[sp], hfreeV[sp]]
                            mm(pa, l1, xnT(cc, bt0, bt0 + 512),
                               start=(cc == 0), stop=(cc == NCH - 1), waits=w,
                               incs=[(s_th, 1)] if cc == NCH - 1 else ())
                            mm(pb, l3, xnT(cc, bt0, bt0 + 512),
                               start=(cc == 0), stop=(cc == NCH - 1),
                               incs=[(s_th, 1)] if cc == NCH - 1 else ())
                        thn = s_th.n
                        bc1 = sb13T[:, fc:fc + 1]
                        bc3 = sb13T[:, 8 + fc:9 + fc]
                        aop(s1_t[sp], pa, AF.Silu, bias=bc1,
                            waits=[(s_th, thn), (s_vh, hfreeV[sp][1]),
                                   (s_ah, s_ah.n)], incs=[(s_ah, 1)])
                        hfreeA[sp] = (s_ah, s_ah.n)
                        vop("scalar_tensor_tensor",
                            ghT[:, slot * 512:(slot + 1) * 512], pb, bc3,
                            s1_t[sp], ALU.add, ALU.mult,
                            waits=[(s_ah, s_ah.n), (s_th, thn), ghfree[slot],
                                   (s_vy, s_vy.n)], incs=[(s_vh, 1)])
                        hfreeV[sp] = (s_vh, s_vh.n)
                        gh_ready[(blk, fc)] = s_vh.n
                    if blk == NBQ - 1 and q + 1 < NQ:
                        wfree[0] = (s_th, s_th.n)
                        load_expert(0, 0)
                for tw in range(4):
                    tg = bt0 // 128 + tw
                    tq = tg - tok0 // 128
                    yp = y_ps[tw % 2]
                    for fc in range(nfc):
                        slot = fc if is_sh else (blk % 2) * 4 + fc
                        for dh in range(2):
                            r = wsl(32768, fc, D, dh * 512, (dh + 1) * 512) \
                                if is_sh else \
                                wsl(buf * 24576 + 16384, fc, D,
                                    dh * 512, (dh + 1) * 512)
                            stop = (not is_sh) and fc == nfc - 1
                            last = stop and dh == 1
                            w2w = [(s_vh, gh_ready[(blk, fc)]),
                                   yfree[tw % 2]]
                            if is_sh:
                                w2w += sh_ld[2]
                            mm(yp[:, dh * 512:(dh + 1) * 512],
                               ghT[:, slot * 512 + tw * 128:
                                   slot * 512 + tw * 128 + 128], r,
                               start=(fc == 0), stop=stop,
                               waits=w2w,
                               incs=[(s_ty, 1)] if last else ())
                    if is_sh:
                        for dh in range(2):
                            mm(yp[:, dh * 512:(dh + 1) * 512],
                               gatewT[:, tg * 128:(tg + 1) * 128],
                               b2_sb[:, dh * 512:(dh + 1) * 512],
                               start=False, stop=False,
                               waits=[(s_v1, t1tr.get(tg, 0)),
                                      (s_v1, V1N)])
                            mm(yp[:, dh * 512:(dh + 1) * 512],
                               ones_row[:, 0:128],
                               sb2_row[:, dh * 512:(dh + 1) * 512],
                               start=False, stop=True,
                               waits=[(s_setc, 3)],
                               incs=[(s_ty, 1)] if dh == 1 else ())
                    tyn = s_ty.n
                    ycol = y_acc[:, tq * D:(tq + 1) * D]
                    gcol = None if is_sh else \
                        gatew[:, tg * NE + ei:tg * NE + ei + 1]
                    if ei == 0:
                        vop("tensor_scalar", ycol, yp, gcol, None, ALU.mult,
                            waits=[(s_ty, tyn), (s_vy, vy_last.get(tq, 0)),
                                   (s_v1, V1N)], incs=[(s_vy, 1)])
                    elif not is_sh:
                        vop("scalar_tensor_tensor", ycol, yp, gcol, ycol,
                            ALU.mult, ALU.add,
                            waits=[(s_ty, tyn), (s_vy, vy_last.get(tq, 0))],
                            incs=[(s_vy, 1)])
                    else:
                        vop("tensor_tensor", ycol, yp, ycol, ALU.add,
                            waits=[(s_ty, tyn), (s_vy, vy_last.get(tq, 0))],
                            incs=[(s_vy, 1)])
                    vy_last[tq] = s_vy.n
                    yfree[tw % 2] = (s_vy, s_vy.n)
                    if is_sh:
                        aop(g_t[tw % 2], ycol, AF.Silu,
                            waits=[(s_vy, vy_last[tq]), (s_a1, s_a1.n),
                                   (s_ag, s_ag.n),
                                   (s_gsp[tw % 2], 16 * (tg // 2))],
                            incs=[(s_ag, 1)])
                        dmaS(G_SP[tg * 128:(tg + 1) * 128, :], g_t[tw % 2],
                             waits=[(s_ag, s_ag.n)],
                             incs=[(s_gsp[tw % 2], 16)])
                for fc in range(nfc):
                    slot = fc if is_sh else (blk % 2) * 4 + fc
                    ghfree[slot] = (s_ty, s_ty.n)
            wfree[buf] = (s_ty, s_ty.n)
            if is_sh:
                wfree[0] = wfree[1] = (s_ty, s_ty.n)

    sch.barrier(s_bar[2], extra=all_dma_sems())
    AR.pop()
    AR.push()
    o_va = AR.alloc(va_cols * 2)
    V_aug = AR.bf(o_va, 0, 128, va_cols)
    o_wp = AR.alloc(NCH * D * 2)
    Wp = AR.bf(o_wp, 0, 128, NCH * D)
    stg = [AR.f32(AR.alloc(D * 4), 0, 128, D) for _ in range(2)]
    raw = [AR.bf(AR.alloc(D * 2), 0, 128, D) for _ in range(2)]
    Bp = AR.f32(AR.alloc(D * 4), 0, 128, D)
    qf = [AR.f32(AR.alloc(D * 4), 0, 128, D) for _ in range(2)]
    qh_bf = [AR.bf(AR.alloc(D * 2), 0, 128, D) for _ in range(2)]
    sq3 = AR.f32(AR.alloc(D * 4), 0, 128, D)
    ssq = AR.f32(AR.alloc(NH * 4), 0, 128, NH)
    rn3 = AR.f32(AR.alloc(NH * 4), 0, 128, NH)
    bp_row = AR.bf(AR.alloc(D * 2), 0, 1, D)

    WMAP = [(WQd, 0, QH_SP, s_qsp), (WKd, 1, KH_SP, s_ksp),
            (WVd, 2, None, None)]
    v3l, a3l = {}, {}
    for pi, (Wd, bi, SPt, ssp) in enumerate(WMAP):
        for kk in (-2, -1):
            v3l[("w", pi, kk)] = s_v3.n
            v3l[("t", pi, kk)] = s_v3.n
            a3l[("t", pi, kk)] = s_a3.n
        prevsp = []
        if pi == 1:
            prevsp = [(s_qsp[0], s_qsp[0].n), (s_qsp[1], s_qsp[1].n)]
        elif pi == 2:
            prevsp = [(s_ksp[0], s_ksp[0].n), (s_ksp[1], s_ksp[1].n)]
        for cc in range(NCH):
            dmaS(stg[cc % 2], Wd[cc * 128:(cc + 1) * 128, :],
                 waits=[(s_v3, v3l.get(("w", pi, cc - 2), 0))],
                 incs=[(s_ws[cc % 2], 16)])
            wsn = (s_ws[cc % 2], s_ws[cc % 2].n)
            vop("tensor_scalar", Wp[:, cc * D:(cc + 1) * D], stg[cc % 2],
                lnw_sb[:, cc:cc + 1], None, ALU.mult,
                waits=[wsn, (s_set, SETUP_N), (s_setp, SETP_N),
                       (s_t3, s_t3.n), (s_v3, s_v3.n)], incs=[(s_v3, 1)])
            vop("tensor_copy", raw[cc % 2], stg[cc % 2],
                waits=[wsn, (s_t3, s_t3.n), (s_v3, s_v3.n)],
                incs=[(s_v3, 1)])
            v3l[("w", pi, cc)] = s_v3.n
            vraw = s_v3.n
            for dh in range(2):
                mm(PS[0:1, 2048 + dh * 512:2560 + dh * 512],
                   lnb_sb[:, cc:cc + 1],
                   raw[cc % 2][:, dh * 512:(dh + 1) * 512],
                   start=(cc == 0), stop=False,
                   waits=[(s_v3, vraw), (s_set, SETUP_N), (s_setp, SETP_N)], incs=[(s_t3, 1)])
        for dh in range(2):
            mm(PS[0:1, 2048 + dh * 512:2560 + dh * 512], ones_row[:, 0:1],
               brow[bi][:, dh * 512:(dh + 1) * 512], start=False, stop=True,
               waits=[(s_setc, 3)], incs=[(s_t3, 1)])
        t3b = s_t3.n
        vop("tensor_copy", bp_row, PS[0:1, 2048:3072],
            waits=[(s_t3, t3b), (s_v3, s_v3.n)], incs=[(s_v3, 1)])
        vbp = s_v3.n
        for dh in range(2):
            mm(PS[:, 3072 + dh * 512:3584 + dh * 512], ones_row[:, 0:128],
               bp_row[:, dh * 512:(dh + 1) * 512], start=True, stop=True,
               waits=[(s_v3, vbp)], incs=[(s_t3, 1)])
        t3c = s_t3.n
        vop("tensor_copy", Bp, PS[:, 3072:4096],
            waits=[(s_t3, t3c), (s_v3, s_v3.n)], incs=[(s_v3, 1)])
        bp_done = s_v3.n
        for j in range(NT):
            p = j % 2
            pq = PS[:, 2048:3072] if p == 0 else PS[:, 3072:4096]
            wl = [(s_v3, v3l.get(("t", pi, j - 1), 0)),
                  (s_a3, a3l.get(("t", pi, j - 1), 0))]
            for cc in range(NCH):
                for dh in range(2):
                    mm(pq[:, dh * 512:(dh + 1) * 512],
                       xnT(cc, j * 128, (j + 1) * 128),
                       Wp[:, cc * D + dh * 512:cc * D + (dh + 1) * 512],
                       start=(cc == 0), stop=(cc == NCH - 1),
                       waits=[(s_v3, bp_done), xnd_done[cc],
                              (s_v3, v3l.get(("t", pi, j - 2), 0))],
                       incs=[(s_t3, 1)] if (cc == NCH - 1 and dh == 1) else ())
            tq3 = s_t3.n
            vop("scalar_tensor_tensor", qf[p], pq, R2[:, j:j + 1], Bp,
                ALU.mult, ALU.add,
                waits=[(s_t3, tq3), (s_v3, s_v3.n)] + wl, incs=[(s_v3, 1)])
            vqf = s_v3.n
            if SPt is not None:
                aop(sq3, qf[p], AF.Square,
                    waits=[(s_v3, vqf), (s_a3, s_a3.n)] + wl,
                    incs=[(s_a3, 1)])
                vop("tensor_reduce", ssq.rearrange("p (h u) -> p h u", u=1),
                    sq3.rearrange("p (h d) -> p h d", d=HD), AX.X, ALU.add,
                    waits=[(s_a3, s_a3.n), (s_v3, s_v3.n)], incs=[(s_v3, 1)])
                vop("tensor_scalar", ssq, ssq, 1e-24, None, ALU.max,
                    waits=[(s_v3, s_v3.n)], incs=[(s_v3, 1)])
                vsq = s_v3.n
                aop(rn3, ssq, AF.Ln, waits=[(s_v3, vsq), (s_a3, s_a3.n)],
                    incs=[(s_a3, 1)])
                aop(rn3, rn3, AF.Exp, scale=-0.5, waits=[(s_a3, s_a3.n)],
                    incs=[(s_a3, 1)])
                arn = s_a3.n
                rnb = bass.AP(tensor=rn3.tensor, offset=rn3.offset,
                              ap=[[rn3.ap[0][0], 128], [1, NH], [0, HD]])
                vop("tensor_tensor",
                    qh_bf[p].rearrange("p (h d) -> p h d", d=HD),
                    qf[p].rearrange("p (h d) -> p h d", d=HD), rnb, ALU.mult,
                    waits=[(s_a3, arn), (s_v3, s_v3.n),
                           (ssp[p], 16 * (j // 2))] + prevsp,
                    incs=[(s_v3, 1)])
                v3l[("t", pi, j)] = s_v3.n
                a3l[("t", pi, j)] = s_a3.n
                dmaS(SPt[j * 128:(j + 1) * 128, :], qh_bf[p],
                     waits=[(s_v3, s_v3.n)], incs=[(ssp[p], 16)])
            else:
                vop("tensor_copy", V_aug[:, j * D:(j + 1) * D], qf[p],
                    waits=[(s_v3, vqf)], incs=[(s_v3, 1)])
                v3l[("t", pi, j)] = s_v3.n
                a3l[("t", pi, j)] = s_a3.n
    V3F = s_v3.n

    sch.barrier(s_bar[3], extra=all_dma_sems())
    AR.pop()
    AR.pop()
    AR.push()

    def allocA(nbytes):
        nb = (nbytes + 63) // 64 * 64
        if AR.off + nb <= o_va:
            return AR.alloc(nbytes)
        return None
    NRING = globals()['kernel_NRING']
    o_kt = allocA(NCH * S * 2)
    o_wo = allocA(NCH * D * 2)
    o_ring = allocA(NRING * 1024 * 2)
    o_st4 = [allocA(64) for _ in range(2)]
    assert AR.off <= o_va, (AR.off, o_va)
    AR.off = o_va + va_cols * 2
    V2 = AR.bf(o_va, 0, 128, va_cols)
    if o_kt is None:
        o_kt = AR.alloc(NCH * S * 2)
    if o_wo is None:
        o_wo = AR.alloc(NCH * D * 2)
    if o_ring is None:
        o_ring = AR.alloc(NRING * 1024 * 2)
    o_st4 = [o if o is not None else AR.alloc(64) for o in o_st4]
    Wo_sb = AR.bf(o_wo, 0, 128, NCH * D)
    ring = AR.bf(o_ring, 0, 128, NRING * 1024)
    ST4 = [AR.f32(o, 0, 128, 16) for o in o_st4]
    qTb = AR.bf(AR.alloc(NCH * 512 * 2), 0, 128, NCH * 512)
    O_T = AR.bf(AR.alloc(NPAIR * 512 * 2), 0, 128, NPAIR * 512)
    stT = [AR.bf(AR.alloc(512 * 2), 0, 128, 512) for _ in range(2)]
    su_s = AR.f32(AR.alloc(512 * 4), 0, 128, 512)
    su = [su_s, su_s]
    lr = AR.f32(AR.alloc(512 * 4), 0, 1, 512)
    rinv_p = [AR.f32(AR.alloc(512 * 4), 0, 1, 512) for _ in range(2)]
    rinv = [rinv_p, rinv_p]
    x4 = [AR.f32(AR.alloc(D * 4), 0, 128, D) for _ in range(2)]
    g4 = [AR.f32(AR.alloc(D * 4), 0, 128, D) for _ in range(2)]
    o4_s = AR.f32(AR.alloc(D * 4), 0, 128, D)
    o4 = [o4_s, o4_s]
    xn4_s = AR.f32(AR.alloc(D * 4), 0, 128, D)
    xn4 = [xn4_s, xn4_s]
    o_rbc = AR.alloc(512 * 4)
    rbcA_sb = AR.f32(o_rbc, 0, 64, 512)
    rbcB_sb = AR.f32(o_rbc, 64, 128, 512)

    def kTs(pair, p0, p1, c0, c1):
        return AR.h[p0:p1, o_kt // 2 + pair * S + c0: o_kt // 2 + pair * S + c1]

    kt_done = {}

    def load_kT(cc, dmaF, sm=None):
        sm = s_ktd[cc]
        dmaF(kTs(cc, 0, 128, 0, S), KH_SP[:, cc * 128:(cc + 1) * 128],
             waits=[(s_ksp[0], 16 * ((NT + 1) // 2)),
                    (s_ksp[1], 16 * (NT // 2)), (s_v3, V3F)],
             incs=[(sm, 16)], transpose=True)
        kt_done[cc] = (sm, sm.n)
    load_kT(0, dmaS)
    for cc in range(4, NCH):
        load_kT(cc, dmaA)
    dmaP(Wo_sb.rearrange("p (c d) -> p c d", d=D),
         WOd.rearrange("(c p) d -> p c d", p=128), waits=[(s_v3, V3F)],
         incs=[(s_wo4, 16)])
    WO4 = s_wo4.n

    SC = [PS[:, 0:1024], PS[:, 1024:2048]]
    AVb = [PS[:, 2048:3072], PS[:, 3072:4096]]
    avA = [AVb[s][0:64, 0:512] for s in range(2)]
    rsA = [AVb[s][64:65, 0:512] for s in range(2)]
    avB = [AVb[s][64:128, 512:1024] for s in range(2)]
    rsB = [AVb[s][0:1, 512:1024] for s in range(2)]
    rbcA = [AVb[s][0:64, 0:512] for s in range(2)]
    rbcB = [AVb[s][64:128, 512:1024] for s in range(2)]

    sc_free = {0: (s_aex, 0), 1: (s_aex, 0)}
    ring_free = {i: (s_tav, 0) for i in range(NRING)}
    av_free = {0: [], 1: []}
    stT_rd = {0: (s_tsc, 0), 1: (s_tsc, 0)}
    state_ok = {0: 0, 1: 0}
    oT_free = (s_tt, 0)
    qT_rd = (s_tsc, 0)
    gc = [0]
    rbc_rd = [(s_vn, 0)]
    pending = [None]
    ti_v, ti_a = {}, {}
    ti = 0

    def emit_norm(unit):
        pair, st2, slot, tavn = unit
        vop("reciprocal", rinv[slot][0], rsA[slot],
            waits=[(s_tav, tavn), (s_vn, s_vn.n), (s_nbc, s_nbc.n)],
            incs=[(s_vn, 1)])
        anA = (s_vn, s_vn.n)
        vop("reciprocal", rinv[slot][1], rsB[slot],
            waits=[(s_vn, s_vn.n), (s_nbc, s_nbc.n)], incs=[(s_vn, 1)])
        anB = (s_vn, s_vn.n)
        dstA = stT[slot][0:64, :] if st2 == 0 else \
            O_T[0:64, pair * 512:(pair + 1) * 512]
        dstB = stT[slot][64:128, :] if st2 == 0 else \
            O_T[64:128, pair * 512:(pair + 1) * 512]
        wdst = [stT_rd[slot]] if st2 == 0 else [oT_free]
        vop("tensor_copy", su[slot][0:64, :], AVb[slot][0:64, 0:512],
            waits=[(s_tav, tavn), (s_vn, s_vn.n)], incs=[(s_vn, 1)])
        vop("tensor_copy", su[slot][64:128, :], avB[slot],
            waits=[(s_tav, tavn), (s_vn, s_vn.n)], incs=[(s_vn, 1)])
        vsu = s_vn.n
        r0 = rinv[slot][0]
        r1 = rinv[slot][1]
        dmaP(rbcA_sb,
             bass.AP(tensor=r0.tensor, offset=r0.offset,
                     ap=[[r0.ap[0][0], 1], [0, 64], [1, 512]]),
             waits=[anA, rbc_rd[0]], incs=[(s_nbc, 16)])
        dmaP(rbcB_sb,
             bass.AP(tensor=r1.tensor, offset=r1.offset,
                     ap=[[r1.ap[0][0], 1], [0, 64], [1, 512]]),
             waits=[anB], incs=[(s_nbc, 16)])
        nbcn = s_nbc.n
        vop("tensor_tensor", dstA, su[slot][0:64, :], rbcA_sb, ALU.mult,
            waits=[(s_nbc, nbcn), (s_vn, s_vn.n)] + wdst, incs=[(s_vn, 1)])
        vop("tensor_tensor", dstB, su[slot][64:128, :], rbcB_sb, ALU.mult,
            waits=[(s_nbc, nbcn), (s_vn, s_vn.n)] + wdst, incs=[(s_vn, 1)])
        rbc_rd[0] = (s_vn, s_vn.n)
        av_free[slot] = [(s_vn, vsu), anB]
        if st2 == 0:
            state_ok[slot] = s_vn.n

    def emit_unit(qb, pair, st2, slot, qtn):
        LAG = 2
        expcnt = {}
        for it in range(NT + LAG):
            c = it
            if c < NT:
                scs = c % 2
                rgs = c % NRING
                w = [sc_free[scs], qtn, kt_done[pair]]
                if st2 == 1:
                    w.append((s_vn, state_ok[slot]))
                rhs = qTb[:, pair * 512:(pair + 1) * 512] if st2 == 0 \
                    else stT[slot][:, :]
                mm(SC[scs][:, 0:512],
                   kTs(pair, 0, 64, c * 128, (c + 1) * 128), rhs[0:64, :],
                   start=True, stop=True, waits=w, tp=(0, 0))
                mm(SC[scs][:, 512:1024],
                   kTs(pair, 64, 128, c * 128, (c + 1) * 128), rhs[64:128, :],
                   start=True, stop=True, tp=(64, 0), incs=[(s_tsc, 1)])
                aop(ring[:, rgs * 1024:(rgs + 1) * 1024], SC[scs], AF.Exp,
                    waits=[(s_tsc, s_tsc.n), ring_free[rgs],
                           (s_aex, s_aex.n)], incs=[(s_aex, 1)])
                sc_free[scs] = (s_aex, s_aex.n)
                expcnt[c] = s_aex.n
                gc[0] += 1
            ca = it - LAG
            if ca >= 0:
                rgs2 = ca % NRING
                aw = [(s_aex, expcnt[ca])]
                if ca == 0:
                    aw += av_free[slot] + [(s_setc, 3), (s_v3, V3F)]
                vaA = V2[:, ca * D + 2 * pair * HD:
                         ca * D + 2 * pair * HD + HD]
                vaB = V2[:, ca * D + (2 * pair + 1) * HD:
                         ca * D + (2 * pair + 1) * HD + HD]
                rgA = ring[:, rgs2 * 1024:rgs2 * 1024 + 512]
                rgB = ring[:, rgs2 * 1024 + 512:rgs2 * 1024 + 1024]
                mm(avA[slot], vaA, rgA, start=(ca == 0), stop=(ca == NT - 1),
                   waits=aw)
                mm(avB[slot], vaB, rgB, start=(ca == 0), stop=(ca == NT - 1),
                   tp=(0, 64))
                mm(rsB[slot], ones_col, rgB, start=(ca == 0),
                   stop=(ca == NT - 1), tp=(0, 0))
                mm(rsA[slot], ones_col, rgA, start=(ca == 0),
                   stop=(ca == NT - 1), tp=(0, 64), incs=[(s_tav, 4)])
                ring_free[rgs2] = (s_tav, s_tav.n)
            if it == DEFER_AT and pending[0] is not None and not DEFER_OFF:
                emit_norm(pending[0])
                pending[0] = None
        if st2 == 1:
            stT_rd[slot] = (s_tsc, s_tsc.n)
        tavn = s_tav.n
        if DEFER_OFF:
            emit_norm((pair, st2, slot, tavn))
        else:
            pending[0] = (pair, st2, slot, tavn)

    def emit_qT(qb):
        for cc in range(NCH):
            dmaS(qTb[:, cc * 512:(cc + 1) * 512],
                 QH_SP[qb * 512:(qb + 1) * 512, cc * 128:(cc + 1) * 128],
                 waits=[(s_qsp[0], 16 * ((NT + 1) // 2)),
                        (s_qsp[1], 16 * (NT // 2)), qT_rd, (s_v3, V3F)],
                 incs=[(s_qT[0], 16)], transpose=True)
        return (s_qT[0], s_qT[0].n)

    qtn = emit_qT(0)
    for cc in range(1, 4):
        load_kT(cc, dmaS)
    for qb in range(NB):
        qtn_next = qtn
        for duo in range(0, NPAIR, 2):
            emit_unit(qb, duo, 0, 0, qtn)
            emit_unit(qb, duo + 1, 0, 1, qtn)
            if duo == NPAIR - 2:
                qT_rd = (s_tsc, s_tsc.n)
                if qb + 1 < NB:
                    qtn_next = emit_qT(qb + 1)
            emit_unit(qb, duo, 1, 0, qtn)
            emit_unit(qb, duo + 1, 1, 1, qtn)
        if pending[0] is not None:
            emit_norm(pending[0])
            pending[0] = None
        oT_ready = s_vn.n
        for tw in range(4):
            p = ti % 2
            row0 = qb * 512 + tw * 128
            dmaS(x4[p], X[row0:row0 + 128, :],
                 waits=[(s_vt, ti_v.get(ti - 2, 0)),
                        (s_at, ti_a.get(ti - 2, 0))], incs=[(s_xt[p], 16)])
            xt_n = (s_xt[p], s_xt[p].n)
            dmaS(g4[p], G_SP[row0:row0 + 128, :],
                 waits=[(s_gsp[0], 16 * ((NT + 1) // 2)),
                        (s_gsp[1], 16 * (NT // 2)),
                        (s_vt, ti_v.get(ti - 2, 0))],
                 incs=[(s_gt[p], 16)])
            gt_n = (s_gt[p], s_gt[p].n)
            for pr in range(NPAIR):
                for dh in range(2):
                    mm(AVb[1][:, dh * 512:(dh + 1) * 512],
                       O_T[:, pr * 512 + tw * 128:pr * 512 + tw * 128 + 128],
                       Wo_sb[:, pr * D + dh * 512:pr * D + (dh + 1) * 512],
                       start=(pr == 0), stop=False,
                       waits=[(s_vn, oT_ready), (s_v3, V3F), (s_wo4, WO4),
                              (s_an, s_an.n),
                              (s_vt, ti_v.get(ti - 1, 0))] + av_free[1])
            for dh in range(2):
                mm(AVb[1][:, dh * 512:(dh + 1) * 512], ones_row[:, 0:128],
                   bo_row[:, dh * 512:(dh + 1) * 512], start=False, stop=True,
                   waits=[(s_set, SETUP_N), (s_setp, SETP_N), (s_setc, 3)],
                   incs=[(s_tt, 1)] if dh == 1 else ())
            ttn = s_tt.n
            st = ST4[p]

            def c4(i, st=st):
                return st[:, i:i + 1]
            wl = [(s_vt, ti_v.get(ti - 1, 0)), (s_at, ti_a.get(ti - 1, 0))]
            vop("reduce_sum", c4(0), x4[p], AX.X,
                waits=[xt_n, (s_vt, s_vt.n)] + wl, incs=[(s_vt, 1)])
            aop(xn4[p], x4[p], AF.Square, accum=c4(9),
                waits=[xt_n, (s_at, s_at.n)] + wl, incs=[(s_at, 1)])
            asq = s_at.n
            vop("tensor_scalar", c4(1), c4(0), 1.0 / D, None, ALU.mult,
                waits=[(s_vt, s_vt.n)], incs=[(s_vt, 1)])
            vop("tensor_tensor", c4(2), c4(0), c4(0), ALU.mult,
                waits=[(s_vt, s_vt.n)], incs=[(s_vt, 1)])
            vop("tensor_scalar", c4(3), c4(2), 1.0 / (D * D), None, ALU.mult,
                waits=[(s_vt, s_vt.n)], incs=[(s_vt, 1)])
            vop("scalar_tensor_tensor", c4(4), c4(9), 1.0 / D, c4(3),
                ALU.mult, ALU.subtract,
                waits=[(s_at, asq), (s_vt, s_vt.n)], incs=[(s_vt, 1)])
            vop("tensor_scalar", c4(5), c4(4), EPS, None, ALU.add,
                waits=[(s_vt, s_vt.n)], incs=[(s_vt, 1)])
            vv4 = s_vt.n
            aop(c4(10), c4(5), AF.Ln,
                waits=[(s_vt, vv4), (s_at, s_at.n)], incs=[(s_at, 1)])
            aop(c4(10), c4(10), AF.Exp, scale=-0.5, waits=[(s_at, s_at.n)],
                incs=[(s_at, 1)])
            ar14 = s_at.n
            vop("tensor_scalar", xn4[p], x4[p], c4(1), c4(10), ALU.subtract,
                ALU.mult, waits=[(s_at, ar14), (s_vt, s_vt.n)],
                incs=[(s_vt, 1)])
            vop("tensor_tensor", o4[p], AVb[1][:, :], xn4[p], ALU.add,
                waits=[(s_tt, ttn), (s_vt, s_vt.n),
                       (s_out[0], s_out[0].n), (s_out[1], s_out[1].n)],
                incs=[(s_vt, 1)])
            vop("tensor_tensor", xn4[p], o4[p], g4[p], ALU.mult,
                waits=[gt_n, (s_vt, s_vt.n)], incs=[(s_vt, 1)])
            vop("tensor_tensor", o4[p], xn4[p], x4[p], ALU.add,
                waits=[(s_vt, s_vt.n)], incs=[(s_vt, 1)])
            ti_v[ti] = s_vt.n
            ti_a[ti] = s_at.n
            dmaS(OUT[row0:row0 + 128, :], o4[p], waits=[(s_vt, ti_v[ti])],
                 incs=[(s_out[p], 16)])
            ti += 1
        av_free[1] = [(s_vt, ti_v[ti - 1]), (s_an, s_an.n)]
        oT_free = (s_tt, s_tt.n)
        qtn = qtn_next

    sch.op("S", lambda e: e.nop(),
           waits=[(s_out[0], 16 * ((NT + 1) // 2)),
                  (s_out[1], 16 * (NT // 2))])

    with nc.Block() as blk:
        blk.tensor(lambda e: sch.replay("T", e))
        blk.vector(lambda e: sch.replay("V", e))
        blk.scalar(lambda e: sch.replay("A", e))
        blk.sync(lambda e: sch.replay("S", e))
        blk.gpsimd(lambda e: sch.replay("P", e))
    return nc, stack


_CACHE = {}


def _get(S):
    if S not in _CACHE:
        _CACHE[S] = _build(S)
    return _CACHE[S]


def kernel(**inputs):
    ip = {k: np.asarray(v) for k, v in inputs.items()}
    x = ip["L0_attn_hidden_states"]
    B, S, D_ = x.shape
    C = np.ascontiguousarray
    common = {
        "wq": C(ip["L0_attn_Wq"]), "wk": C(ip["L0_attn_Wk"]),
        "wv": C(ip["L0_attn_Wv"]), "wo": C(ip["L0_attn_Wo"]),
        "lnw": C(ip["L0_attn_ln_w"].reshape(1, -1)),
        "lnb": C(ip["L0_attn_ln_b"].reshape(1, -1)),
        "bq": C(ip["L0_attn_bq"].reshape(1, -1)),
        "bk": C(ip["L0_attn_bk"].reshape(1, -1)),
        "bv": C(ip["L0_attn_bv"].reshape(1, -1)),
        "bo": C(ip["L0_attn_bo"].reshape(1, -1)),
        "gwt": C(ip["L0_ffn_gate_w"].T),
        "eb": C(ip["L0_ffn_expert_bias"].reshape(1, -1)),
        "w1t": C(ip["L0_ffn_w1"].transpose(0, 2, 1)),
        "w3t": C(ip["L0_ffn_w3"].transpose(0, 2, 1)),
        "w2t": C(ip["L0_ffn_w2"].transpose(0, 2, 1)),
        "b1": C(ip["L0_ffn_b1"]), "b3": C(ip["L0_ffn_b3"]),
        "b2": C(ip["L0_ffn_b2"]),
        "sw1t": C(ip["L0_ffn_sw1"].T), "sw3t": C(ip["L0_ffn_sw3"].T),
        "sw2t": C(ip["L0_ffn_sw2"].T),
        "sb1": C(ip["L0_ffn_sb1"].reshape(1, -1)),
        "sb3": C(ip["L0_ffn_sb3"].reshape(1, -1)),
        "sb2": C(ip["L0_ffn_sb2"].reshape(1, -1)),
    }
    nc, _ = _get(S)
    in_maps = [dict(common, x=C(x[b])) for b in range(B)]
    res = run_bass_kernel_spmd(nc, in_maps, core_ids=list(range(B)))
    return np.stack([r["out"] for r in res.results], axis=0)


# version 31 (final)
# speedup vs baseline: 1.4314x; 1.2075x over previous
import numpy as np
import ml_dtypes
import contextlib
import concourse.bass as bass
import concourse.mybir as mybir
from concourse.bass_utils import run_bass_kernel_spmd

F32 = mybir.dt.float32
BF16 = mybir.dt.bfloat16
AF = mybir.ActivationFunctionType
ALU = mybir.AluOpType
AX = mybir.AxisListType

D = 1024
NCH = 8
NH = 16
HD = 64
NE = 16
DFF = 512
FDFF = 1024
EPS = 1e-5
STRICT_WAITS = False
DEFER_OFF = False
DEFER_AT = 3
kernel_NRING = 3


class Sem:
    __slots__ = ("h", "n", "name")

    def __init__(self, h, name):
        self.h, self.n, self.name = h, 0, name


class Sched:
    ENGS = ("T", "V", "A", "S", "P")

    def __init__(self):
        self.ops = {k: [] for k in self.ENGS}
        self.hw = {k: {} for k in self.ENGS}
        self.last = {k: None for k in self.ENGS}
        self.lastinc = {k: None for k in self.ENGS}
        self.dirty = {k: False for k in self.ENGS}
        self.pending = {k: None for k in self.ENGS}

    def op(self, eng, build, waits=(), incs=()):
        wl = []
        if self.pending[eng]:
            wl += self.pending[eng]
            self.pending[eng] = None
        wl += list(waits)
        best = {}
        for s, v in wl:
            if v > 0 and v > best.get(id(s), (None, 0))[1]:
                best[id(s)] = (s, v)
        if globals().get('STRICT_WAITS', False):
            wfin = [(s.h, v) for s, v in best.values()]
        else:
            wfin = []
            for s, v in best.values():
                if v > self.hw[eng].get(id(s), 0):
                    self.hw[eng][id(s)] = v
                    wfin.append((s.h, v))
        il = []
        for s, k in incs:
            s.n += k
            il.append((s.h, k))
        ent = {"w": wfin, "i": il, "b": build}
        self.ops[eng].append(ent)
        self.last[eng] = ent
        if incs:
            s0, _ = incs[0]
            self.lastinc[eng] = (s0, s0.n)
        self.dirty[eng] = True
        return ent

    def barrier(self, sem=None, extra=()):
        pend = list(extra)
        for eng in ("T", "V", "A"):
            if self.dirty[eng] and self.lastinc[eng] is not None:
                pend.append(self.lastinc[eng])
        for eng in self.ENGS:
            self.dirty[eng] = False
            self.pending[eng] = list(pend)

    def replay(self, eng, e):
        for ent in self.ops[eng]:
            for h, v in ent["w"]:
                e.wait_ge(h, v)
            ins = ent["b"](e)
            for h, k in ent["i"]:
                ins.then_inc(h, k)


class Arena:
    def __init__(self, h):
        self.h = h
        self.h32 = h.bitcast(F32)
        self.off = 0
        self.marks = []
        self.cap = h.shape[1] * 2

    def alloc(self, nbytes):
        nbytes = (nbytes + 63) // 64 * 64
        o = self.off
        self.off += nbytes
        assert self.off <= self.cap, f"arena overflow {self.off} > {self.cap}"
        return o

    def bf(self, off, p0, p1, cols):
        return self.h[p0:p1, off // 2: off // 2 + cols]

    def f32(self, off, p0, p1, cols):
        return self.h32[p0:p1, off // 4: off // 4 + cols]

    def push(self):
        self.marks.append(self.off)

    def pop(self):
        self.off = self.marks.pop()


def _build(S):
    NT = S // 128
    NB = S // 512
    QT = min(S, 1024)
    NQ = S // QT
    NBQ = QT // 512
    NTQ = QT // 128
    NPAIR = NH // 2

    nc = bass.Bass("TRN2", target_bir_lowering=False)
    dt = nc.dram_tensor
    X = dt("x", [S, D], F32, kind="ExternalInput").ap()
    OUT = dt("out", [S, D], F32, kind="ExternalOutput").ap()
    WQd = dt("wq", [D, D], F32, kind="ExternalInput").ap()
    WKd = dt("wk", [D, D], F32, kind="ExternalInput").ap()
    WVd = dt("wv", [D, D], F32, kind="ExternalInput").ap()
    WOd = dt("wo", [D, D], F32, kind="ExternalInput").ap()
    LNW = dt("lnw", [1, D], F32, kind="ExternalInput").ap()
    LNB = dt("lnb", [1, D], F32, kind="ExternalInput").ap()
    BQd = dt("bq", [1, D], F32, kind="ExternalInput").ap()
    BKd = dt("bk", [1, D], F32, kind="ExternalInput").ap()
    BVd = dt("bv", [1, D], F32, kind="ExternalInput").ap()
    BOd = dt("bo", [1, D], F32, kind="ExternalInput").ap()
    GWT = dt("gwt", [D, NE], F32, kind="ExternalInput").ap()
    EBd = dt("eb", [1, NE], F32, kind="ExternalInput").ap()
    W1T = dt("w1t", [NE, D, DFF], F32, kind="ExternalInput").ap()
    W3T = dt("w3t", [NE, D, DFF], F32, kind="ExternalInput").ap()
    W2T = dt("w2t", [NE, DFF, D], F32, kind="ExternalInput").ap()
    B1d = dt("b1", [NE, DFF], F32, kind="ExternalInput").ap()
    B3d = dt("b3", [NE, DFF], F32, kind="ExternalInput").ap()
    B2d = dt("b2", [NE, D], F32, kind="ExternalInput").ap()
    SW1T = dt("sw1t", [D, FDFF], F32, kind="ExternalInput").ap()
    SW3T = dt("sw3t", [D, FDFF], F32, kind="ExternalInput").ap()
    SW2T = dt("sw2t", [FDFF, D], F32, kind="ExternalInput").ap()
    SB1d = dt("sb1", [1, FDFF], F32, kind="ExternalInput").ap()
    SB3d = dt("sb3", [1, FDFF], F32, kind="ExternalInput").ap()
    SB2d = dt("sb2", [1, D], F32, kind="ExternalInput").ap()

    XN_SP = dt("xn_sp", [S, D], BF16, kind="Internal").ap()
    QH_SP = dt("qh_sp", [S, D], BF16, kind="Internal").ap()
    KH_SP = dt("kh_sp", [S, D], BF16, kind="Internal").ap()
    va_cols = (S // 128) * D
    VA_SP = dt("va_sp", [128, va_cols], BF16, kind="Internal").ap()
    G_SP = dt("g_sp", [S, D], F32, kind="Internal").ap()
    IDEN = nc.inline_tensor(np.eye(128, dtype=ml_dtypes.bfloat16), "iden").ap()

    stack = contextlib.ExitStack()

    def sem(name):
        return Sem(stack.enter_context(nc.semaphore(name)), name)

    s_set, s_setp, s_setc = sem("set"), sem("setp"), sem("setc")
    s_bar = [sem(f"bar{i}") for i in range(5)]
    s_x = [sem("sx0"), sem("sx1")]
    s_v0, s_a0 = sem("sv0"), sem("sa0")
    s_t0 = sem("st0")
    s_xnsp = [sem("sxnsp0"), sem("sxnsp1")]
    s_xnT = sem("sxnT")
    s_xnd = [sem(f"sxnd{c}") for c in range(8)]
    s_t1, s_v1, s_a1 = sem("st1"), sem("sv1"), sem("sa1")
    s_wld = [sem("swld0"), sem("swld1")]
    s_th, s_ah, s_vh = sem("sth"), sem("sah"), sem("svh")
    s_wsh = sem("swsh")
    s_ty, s_vy, s_ag = sem("sty"), sem("svy"), sem("sag")
    s_gsp = [sem("sgsp0"), sem("sgsp1")]
    s_ws = [sem("sws0"), sem("sws1")]
    s_v3, s_a3, s_t3 = sem("sv3"), sem("sa3"), sem("st3")
    s_qsp = [sem("sqsp0"), sem("sqsp1")]
    s_ksp = [sem("sksp0"), sem("sksp1")]
    s_vsp = sem("svsp")
    s_p4l, s_wo4 = sem("sp4l"), sem("swo4")
    s_p4a = sem("sp4a")
    s_ktd = [sem(f"sktd{c}") for c in range(8)]
    s_qT = [sem("sqT0"), sem("sqT1")]
    s_tsc, s_aex, s_tav = sem("stsc"), sem("saex"), sem("stav")
    s_an, s_tn, s_vn = sem("san"), sem("stn"), sem("svn")
    s_nbc = sem("snbc")
    s_tt = sem("stt")
    s_xt = [sem("sxt0"), sem("sxt1")]
    s_gt = [sem("sgt0"), sem("sgt1")]
    s_vt, s_at = sem("svt"), sem("sat")
    s_out = [sem("sout0"), sem("sout1")]

    def all_dma_sems():
        out = []
        for s in (s_set, s_setp, s_xnT, s_vsp, s_p4l, s_wo4, s_p4a):
            out.append((s, s.n))
        for s in s_ktd:
            out.append((s, s.n))
        for pairl in (s_x, s_xnsp, s_wld, s_gsp, s_ws, s_qsp, s_ksp, s_qT,
                      s_xt, s_gt, s_out):
            for s in pairl:
                out.append((s, s.n))
        return out

    rem = nc.sbuf_bytes_remaining
    arena_cols = (rem // 2) // 64 * 64
    AR = Arena(nc.alloc_sbuf_tensor("arena", [128, arena_cols], BF16))

    ones_row = AR.bf(AR.alloc(256), 0, 1, 128)
    ones32 = AR.f32(AR.alloc(256), 0, 1, 64)
    ones_col = AR.bf(AR.alloc(64), 0, 128, 1)
    bo_row = AR.bf(AR.alloc(2048), 0, 1, D)
    R2 = AR.f32(AR.alloc(NT * 4), 0, 128, NT)

    AR.push()
    o_xnT = AR.alloc(NCH * S * 2)
    gatew = AR.f32(AR.alloc(NT * NE * 4), 0, 128, NT * NE)
    gatewT = AR.bf(AR.alloc(S * 2), 0, 16, S)
    iden_sb = AR.bf(AR.alloc(256), 0, 128, 128)
    gwt_sb = AR.bf(AR.alloc(NCH * NE * 2), 0, 128, NCH * NE)
    eb_sb = AR.f32(AR.alloc(NE * 4), 0, 128, NE)
    b1T = AR.bf(AR.alloc(4 * NE * 2), 0, 128, 4 * NE)
    b3T = AR.bf(AR.alloc(4 * NE * 2), 0, 128, 4 * NE)
    sb13T = AR.bf(AR.alloc(16 * 2), 0, 128, 16)
    b2_sb = AR.bf(AR.alloc(D * 2), 0, 16, D)
    sb2_row = AR.bf(AR.alloc(D * 2), 0, 1, D)
    b1s = AR.bf(AR.alloc(DFF * 2), 0, 16, DFF)
    b3s = AR.bf(AR.alloc(DFF * 2), 0, 16, DFF)
    sb13s = AR.bf(AR.alloc(128 * 2), 0, 16, 128)
    o_brow = AR.alloc(3 * D * 2)
    brow = [AR.bf(o_brow + i * D * 2, 0, 1, D) for i in range(3)]
    lnw_sb = AR.f32(AR.alloc(NCH * 4), 0, 128, NCH)
    lnb_sb = AR.bf(AR.alloc(NCH * 2), 0, 128, NCH)
    gsc = AR.f32(AR.alloc(NE * 4), 0, 128, NE)
    gbi = AR.f32(AR.alloc(NE * 4), 0, 128, NE)
    gb2 = AR.f32(AR.alloc(NE * 4), 0, 128, NE)
    gwu = AR.f32(AR.alloc(NE * 4), 0, 128, NE)
    gm = AR.f32(AR.alloc(32), 0, 128, 8)
    gbf = AR.bf(AR.alloc(NT * NE * 2), 0, 128, NT * NE)

    def xnT(c, t0, t1):
        return AR.h[:, o_xnT // 2 + c * S + t0: o_xnT // 2 + c * S + t1]

    PS = nc.alloc_psum_tensor("ps", [128, 4096], F32)
    PSb = PS.bitcast(BF16)

    sch = Sched()

    def mm(out, lhsT, rhs, start, stop, waits=(), incs=(), tp=None):
        def b(e, out=out, lhsT=lhsT, rhs=rhs, start=start, stop=stop, tp=tp):
            return e.matmul(out, lhsT, rhs, start=start, stop=stop,
                            tile_position=tp)
        return sch.op("T", b, waits, incs)

    def dmaS(out, in_, waits=(), incs=(), transpose=False, slow=False):
        def b(e, out=out, in_=in_, t=transpose, slow=slow):
            return e.dma_start(out, in_, transpose=t,
                               allow_slow_non_contiguous=slow)
        return sch.op("S", b, waits, incs)

    def dmaV(out, in_, waits=(), incs=(), transpose=False, slow=False):
        def b(e, out=out, in_=in_, t=transpose, slow=slow):
            return e.dma_start(out, in_, transpose=t,
                               allow_slow_non_contiguous=slow)
        return sch.op("V", b, waits, incs)

    def dmaA(out, in_, waits=(), incs=(), transpose=False, slow=False):
        def b(e, out=out, in_=in_, t=transpose, slow=slow):
            return e.dma_start(out, in_, transpose=t,
                               allow_slow_non_contiguous=slow)
        return sch.op("A", b, waits, incs)

    def dmaP(out, in_, waits=(), incs=(), slow=False):
        def b(e, out=out, in_=in_, slow=slow):
            return e.dma_start(out, in_, allow_slow_non_contiguous=slow)
        return sch.op("P", b, waits, incs)

    def vop(name, *args, waits=(), incs=()):
        def b(e, name=name, args=args):
            return getattr(e, name)(*args)
        return sch.op("V", b, waits, incs)

    def aop(out, in_, func, bias=0.0, scale=1.0, accum=None, waits=(), incs=()):
        def b(e, out=out, in_=in_, func=func, bias=bias, scale=scale, ac=accum):
            return e.activation(out, in_, func, bias=bias, scale=scale,
                                accum_out=ac)
        return sch.op("A", b, waits, incs)

    dmaP(gwt_sb.rearrange("p (c s) -> p c s", s=NE),
         GWT.rearrange("(c p) s -> p c s", p=128), incs=[(s_setp, 16)])
    dmaP(b2_sb, B2d[:, :], incs=[(s_setp, 16)])
    dmaP(sb2_row, SB2d[:, :], incs=[(s_setp, 16)])
    dmaP(bo_row, BOd[:, :], incs=[(s_setp, 16)])
    for i, B_ in enumerate((BQd, BKd, BVd)):
        dmaP(brow[i], B_[:, :], incs=[(s_setp, 16)])
    dmaP(b1s, B1d[:, :], incs=[(s_setp, 16)])
    dmaP(b3s, B3d[:, :], incs=[(s_setp, 16)])
    dmaP(sb13s[0:8, :], SB1d.rearrange("o (f p) -> (o f) p", p=128),
         incs=[(s_setp, 16)])
    dmaP(sb13s[8:16, :], SB3d.rearrange("o (f p) -> (o f) p", p=128),
         incs=[(s_setp, 16)])
    dmaP(lnb_sb.rearrange("p (c u) -> p c u", u=1),
         LNB.rearrange("o (c p u) -> (o p) c u", p=128, u=1),
         incs=[(s_setp, 16)], slow=True)
    dmaS(iden_sb, IDEN[:, :], incs=[(s_set, 16)])
    dmaS(lnw_sb.rearrange("p (c u) -> p c u", u=1),
         LNW.rearrange("o (c p u) -> (o p) c u", p=128, u=1),
         incs=[(s_set, 16)], slow=True)
    eb_bc = bass.AP(tensor=EBd.tensor, offset=0, ap=[[0, 128], [1, NE]])
    dmaS(eb_sb, eb_bc, incs=[(s_set, 16)])
    SETP_N = s_setp.n
    nset0 = s_set.n
    for i in range(4):
        dmaS(b1T[:, i * 16:(i + 1) * 16], b1s[:, i * 128:(i + 1) * 128],
             waits=[(s_set, s_set.n), (s_setp, SETP_N)],
             incs=[(s_set, 16)], transpose=True)
    for i in range(4):
        dmaS(b3T[:, i * 16:(i + 1) * 16], b3s[:, i * 128:(i + 1) * 128],
             waits=[(s_set, s_set.n)], incs=[(s_set, 16)], transpose=True)
    dmaS(sb13T, sb13s, waits=[(s_set, s_set.n)], incs=[(s_set, 16)],
         transpose=True)
    SETUP_N = s_set.n
    vop("memset", ones_row, 1.0, incs=[(s_setc, 1)])
    vop("memset", ones_col, 1.0, incs=[(s_setc, 1)])
    vop("memset", ones32, 1.0, incs=[(s_setc, 1)])

    AR.push()
    x_t = [AR.f32(AR.alloc(D * 4), 0, 128, D) for _ in range(2)]
    xn_bf = [AR.bf(AR.alloc(D * 2), 0, 128, D) for _ in range(2)]
    sq_t = AR.f32(AR.alloc(D * 4), 0, 128, D)
    STT = [AR.f32(AR.alloc(64), 0, 128, 16) for _ in range(2)]
    v_done, a_done = {}, {}
    trd, trc_done = {}, {}
    for j in range(NT):
        p = j % 2
        st = STT[p]

        def c(i, st=st):
            return st[:, i:i + 1]
        dmaS(x_t[p], X[j * 128:(j + 1) * 128, :],
             waits=[(s_v0, v_done.get(j - 2, 0)), (s_a0, a_done.get(j - 2, 0))],
             incs=[(s_x[p], 16)])
        wx = [(s_x[p], 16 * (j // 2 + 1))]
        wl = [(s_v0, v_done.get(j - 1, 0)), (s_a0, a_done.get(j - 1, 0))]
        vop("reduce_sum", c(0), x_t[p], AX.X, waits=wx + wl, incs=[(s_v0, 1)])
        aop(sq_t, x_t[p], AF.Square, accum=c(9),
            waits=wx + wl + [(s_a0, s_a0.n)], incs=[(s_a0, 1)])
        asq = s_a0.n
        vop("tensor_scalar", c(1), c(0), 1.0 / D, None, ALU.mult,
            waits=[(s_v0, s_v0.n)], incs=[(s_v0, 1)])
        vop("tensor_tensor", c(2), c(0), c(0), ALU.mult,
            waits=[(s_v0, s_v0.n)], incs=[(s_v0, 1)])
        vop("tensor_scalar", c(3), c(2), 1.0 / (D * D), None, ALU.mult,
            waits=[(s_v0, s_v0.n)], incs=[(s_v0, 1)])
        vop("scalar_tensor_tensor", c(4), c(9), 1.0 / D, c(3), ALU.mult,
            ALU.subtract, waits=[(s_a0, asq), (s_v0, s_v0.n)],
            incs=[(s_v0, 1)])
        vop("tensor_scalar", c(5), c(4), EPS, None, ALU.add,
            waits=[(s_v0, s_v0.n)], incs=[(s_v0, 1)])
        vvar = s_v0.n
        aop(c(10), c(5), AF.Ln, waits=[(s_v0, vvar), (s_a0, s_a0.n)],
            incs=[(s_a0, 1)])
        aop(c(10), c(10), AF.Exp, scale=-0.5, waits=[(s_a0, s_a0.n)],
            incs=[(s_a0, 1)])
        ar1 = s_a0.n
        vop("reciprocal", c(6), c(5), waits=[(s_v0, s_v0.n)], incs=[(s_v0, 1)])
        vop("tensor_tensor", c(7), c(4), c(6), ALU.mult,
            waits=[(s_v0, s_v0.n)], incs=[(s_v0, 1)])
        vop("tensor_scalar", c(8), c(7), EPS, None, ALU.add,
            waits=[(s_v0, s_v0.n)], incs=[(s_v0, 1)])
        vt2 = s_v0.n
        aop(R2[:, j:j + 1], c(8), AF.Ln, waits=[(s_v0, vt2), (s_a0, s_a0.n)],
            incs=[(s_a0, 1)])
        aop(R2[:, j:j + 1], R2[:, j:j + 1], AF.Exp, scale=-0.5,
            waits=[(s_a0, s_a0.n)], incs=[(s_a0, 1)])
        vop("tensor_scalar", xn_bf[p], x_t[p], c(1), c(10), ALU.subtract,
            ALU.mult, waits=[(s_a0, ar1), (s_v0, s_v0.n),
                             (s_t0, trd.get(j - 2, 0))],
            incs=[(s_v0, 1)])
        v_done[j], a_done[j] = s_v0.n, s_a0.n
        for cc in range(NCH):
            sch.op("T", lambda e, p=p, cc=cc: e.transpose(
                PSb[:, p * 1024 + cc * 128:p * 1024 + (cc + 1) * 128],
                xn_bf[p][:, cc * 128:(cc + 1) * 128], iden_sb),
                [(s_v0, v_done[j]), (s_set, SETUP_N),
                 (s_v0, trc_done.get(j - 2, 0))],
                incs=[(s_t0, 1)] if cc == NCH - 1 else ())
        trd[j] = s_t0.n
        pbase = PSb[:, p * 1024:p * 1024 + 1024]
        dbase = AR.h[:, o_xnT // 2 + j * 128:o_xnT // 2 + j * 128 + 128]
        vop("tensor_copy",
            bass.AP(tensor=dbase.tensor, offset=dbase.offset,
                    ap=[[dbase.ap[0][0], 128], [S, 8], [1, 128]]),
            bass.AP(tensor=pbase.tensor, offset=pbase.offset,
                    ap=[[pbase.ap[0][0], 128], [128, 8], [1, 128]]),
            waits=[(s_t0, trd[j]), (s_v0, s_v0.n)], incs=[(s_v0, 1)])
        trc_done[j] = s_v0.n
    xnd_done = {}
    for cc in range(NCH):
        xnd_done[cc] = (s_v0, trc_done[NT - 1])

    sch.barrier(s_bar[0], extra=all_dma_sems())
    AR.pop()
    lg_ps = [PS[:, 2048:2048 + NE], PS[:, 2560:2560 + NE]]
    tr_ps = PSb[0:16, 6144:6272]
    v1_done, t1tr = {}, {}
    for j in range(NT):
        p = j % 2
        for cc in range(NCH):
            mm(lg_ps[p], xnT(cc, j * 128, (j + 1) * 128),
               gwt_sb[:, cc * NE:(cc + 1) * NE], start=(cc == 0),
               stop=(cc == NCH - 1),
               waits=[xnd_done[cc], (s_set, SETUP_N), (s_setp, SETP_N),
                      (s_v1, v1_done.get(j - 2, 0))],
               incs=[(s_t1, 1)] if cc == NCH - 1 else ())
        tmm = s_t1.n
        wl = [(s_v1, v1_done.get(j - 1, 0))]
        aop(gsc, lg_ps[p], AF.Sigmoid,
            waits=[(s_t1, tmm), (s_a1, s_a1.n)] + wl, incs=[(s_a1, 1)])
        asig = s_a1.n
        vop("tensor_tensor", gbi, lg_ps[p], eb_sb, ALU.add,
            waits=[(s_t1, tmm), (s_set, SETUP_N), (s_setp, SETP_N)] + wl,
            incs=[(s_v1, 1)])
        vop("reduce_max", gm[:, 0:1], gbi, AX.X, waits=[(s_v1, s_v1.n)],
            incs=[(s_v1, 1)])
        vop("tensor_scalar", gwu, gbi, gm[:, 0:1], None, ALU.is_equal,
            waits=[(s_v1, s_v1.n)], incs=[(s_v1, 1)])
        vop("scalar_tensor_tensor", gb2, gwu, -1e30, gbi, ALU.mult, ALU.add,
            waits=[(s_v1, s_v1.n)], incs=[(s_v1, 1)])
        vop("reduce_max", gm[:, 1:2], gb2, AX.X, waits=[(s_v1, s_v1.n)],
            incs=[(s_v1, 1)])
        vop("tensor_scalar", gb2, gb2, gm[:, 1:2], None, ALU.is_equal,
            waits=[(s_v1, s_v1.n)], incs=[(s_v1, 1)])
        vop("tensor_tensor", gwu, gwu, gb2, ALU.add, waits=[(s_v1, s_v1.n)],
            incs=[(s_v1, 1)])
        vop("tensor_tensor", gwu, gwu, gsc, ALU.mult,
            waits=[(s_v1, s_v1.n), (s_a1, asig)], incs=[(s_v1, 1)])
        vop("reduce_sum", gm[:, 2:3], gwu, AX.X, waits=[(s_v1, s_v1.n)],
            incs=[(s_v1, 1)])
        vop("tensor_scalar", gm[:, 2:3], gm[:, 2:3], 1e-10, None, ALU.add,
            waits=[(s_v1, s_v1.n)], incs=[(s_v1, 1)])
        vop("reciprocal", gm[:, 3:4], gm[:, 2:3], waits=[(s_v1, s_v1.n)],
            incs=[(s_v1, 1)])
        vop("tensor_scalar", gatew[:, j * NE:(j + 1) * NE], gwu, gm[:, 3:4],
            None, ALU.mult, waits=[(s_v1, s_v1.n)], incs=[(s_v1, 1)])
        vop("tensor_copy", gbf[:, j * NE:(j + 1) * NE],
            gatew[:, j * NE:(j + 1) * NE],
            waits=[(s_v1, s_v1.n)], incs=[(s_v1, 1)])
        v1_done[j] = s_v1.n
    for j in range(NT):
        sch.op("T", lambda e, j=j: e.transpose(
            tr_ps, gbf[:, j * NE:(j + 1) * NE], iden_sb),
            [(s_v1, v1_done[j]), (s_set, SETUP_N),
             (s_v1, t1tr.get(j - 1, 0))],
            incs=[(s_t1, 1)])
        tn_ = s_t1.n
        vop("tensor_copy", gatewT[:, j * 128:(j + 1) * 128], tr_ps,
            waits=[(s_t1, tn_), (s_v1, s_v1.n)], incs=[(s_v1, 1)])
        t1tr[j] = s_v1.n
        V1N = s_v1.n

    P1T = s_t1.n
    P1V = s_v1.n
    AR.push()
    o_y = AR.alloc(NTQ * D * 4)
    y_acc = AR.f32(o_y, 0, 128, NTQ * D)
    o_wb = AR.alloc(2 * 24 * 1024)
    o_gh = AR.alloc(8 * 512 * 2)
    ghT = AR.bf(o_gh, 0, 128, 8 * 512)
    s1_t = [AR.f32(AR.alloc(512 * 4), 0, 128, 512) for _ in range(2)]
    g_t = [AR.f32(AR.alloc(D * 4), 0, 128, D) for _ in range(2)]

    def wsl(base_bytes, c, span, f0, f1):
        o = (o_wb + base_bytes) // 2
        return AR.h[:, o + c * span + f0: o + c * span + f1]

    h_ps = [(PS[:, 0:512], PS[:, 512:1024]),
            (PS[:, 1024:1536], PS[:, 1536:2048])]
    y_ps = [PS[:, 2048:3072], PS[:, 3072:4096]]
    wfree = {0: (s_set, 0), 1: (s_set, 0)}
    lded = {}

    def load_expert(eidx, buf):
        base = buf * 24 * 1024
        dmaP(AR.h[:, (o_wb + base) // 2:(o_wb + base) // 2 + 4096]
             .rearrange("p (c f) -> p c f", f=DFF),
             W1T[eidx].rearrange("(c p) f -> p c f", p=128),
             waits=[wfree[buf]], incs=[(s_wld[buf], 16)])
        dmaP(AR.h[:, (o_wb + base + 8192) // 2:
                  (o_wb + base + 8192) // 2 + 4096]
             .rearrange("p (c f) -> p c f", f=DFF),
             W3T[eidx].rearrange("(c p) f -> p c f", p=128),
             incs=[(s_wld[buf], 16)])
        dmaP(AR.h[:, (o_wb + base + 16384) // 2:
                  (o_wb + base + 16384) // 2 + 4096]
             .rearrange("p (c d) -> p c d", d=D),
             W2T[eidx].rearrange("(c p) d -> p c d", p=128),
             incs=[(s_wld[buf], 16)])
        lded[eidx] = (s_wld[buf], s_wld[buf].n)

    def load_shared(part):
        if part == 0:
            items = ((0, SW1T, 0, 8, FDFF, 0),
                     (16384, SW3T, 0, 4, FDFF, 0))
        elif part == 1:
            items = ((24576, SW3T, 4, 4, FDFF, 1),
                     (32768, SW2T, 0, 4, D, 1))
        else:
            items = ((40960, SW2T, 4, 4, D, 2),)
        sems = {0: s_wld[0], 1: s_wld[1], 2: s_wsh}
        for off, W_, c0, ncnk, span, sb in items:
            dmaP(AR.h[:, (o_wb + off) // 2:
                      (o_wb + off) // 2 + ncnk * span]
                 .rearrange("p (c f) -> p c f", f=span),
                 W_[c0 * 128:(c0 + ncnk) * 128, :]
                 .rearrange("(c p) f -> p c f", p=128),
                 waits=[(s_th, s_th.n), (s_ty, s_ty.n)],
                 incs=[(sems[sb], 16)])
        sh_ld[part] = ([(s_wsh, s_wsh.n)] if part == 2 else
                       [(s_wld[0], s_wld[0].n), (s_wld[1], s_wld[1].n)])

    sh_ld = {}
    hfreeA = {0: (s_ah, 0), 1: (s_ah, 0)}
    hfreeV = {0: (s_vh, 0), 1: (s_vh, 0)}
    yfree = {0: (s_v1, P1V), 1: (s_v1, P1V)}
    ghfree = {i: (s_ty, 0) for i in range(8)}
    vy_last = {}
    load_expert(0, 0)
    for q in range(NQ):
        tok0 = q * QT
        for ei in range(17):
            is_sh = ei == 16
            buf = ei % 2
            nfc = 8 if is_sh else 4
            if ei + 1 < 16:
                load_expert(ei + 1, (ei + 1) % 2)
            elif ei + 1 == 16:
                load_shared(0)
            if is_sh:
                load_shared(2)
            gh_ready = {}
            for blk in range(NBQ):
                bt0 = tok0 + blk * 512
                for fc in range(nfc):
                    sp = fc % 2
                    slot = fc if is_sh else (blk % 2) * 4 + fc
                    pa, pb = h_ps[sp]
                    for cc in range(NCH):
                        if is_sh:
                            l1 = wsl(0, cc, FDFF, fc * 128, (fc + 1) * 128)
                            l3 = wsl(16384, cc, FDFF, fc * 128, (fc + 1) * 128)
                        else:
                            l1 = wsl(buf * 24576, cc, DFF,
                                     fc * 128, (fc + 1) * 128)
                            l3 = wsl(buf * 24576 + 8192, cc, DFF,
                                     fc * 128, (fc + 1) * 128)
                        if is_sh:
                            w = sh_ld[1] + [
                                 xnd_done[cc], hfreeA[sp], hfreeV[sp]]
                        else:
                            w = [lded[ei],
                                 xnd_done[cc], hfreeA[sp], hfreeV[sp]]
                        mm(pa, l1, xnT(cc, bt0, bt0 + 512), start=(cc == 0),
                           stop=(cc == NCH - 1), waits=w,
                           incs=[(s_th, 1)] if cc == NCH - 1 else ())
                        mm(pb, l3, xnT(cc, bt0, bt0 + 512), start=(cc == 0),
                           stop=(cc == NCH - 1),
                           incs=[(s_th, 1)] if cc == NCH - 1 else ())
                    thn = s_th.n
                    if is_sh:
                        bc1 = sb13T[:, fc:fc + 1]
                        bc3 = sb13T[:, 8 + fc:9 + fc]
                    else:
                        bc1 = b1T[:, fc * NE + ei:fc * NE + ei + 1]
                        bc3 = b3T[:, fc * NE + ei:fc * NE + ei + 1]
                    aop(s1_t[sp], pa, AF.Silu, bias=bc1,
                        waits=[(s_th, thn), (s_set, SETUP_N), (s_setp, SETP_N),
                               (s_vh, hfreeV[sp][1]), (s_ah, s_ah.n)],
                        incs=[(s_ah, 1)])
                    hfreeA[sp] = (s_ah, s_ah.n)
                    vop("scalar_tensor_tensor",
                        ghT[:, slot * 512:(slot + 1) * 512], pb, bc3,
                        s1_t[sp], ALU.add, ALU.mult,
                        waits=[(s_ah, s_ah.n), (s_th, thn), ghfree[slot],
                               (s_vy, s_vy.n)], incs=[(s_vh, 1)])
                    hfreeV[sp] = (s_vh, s_vh.n)
                    gh_ready[(blk, fc)] = s_vh.n
                if is_sh and NBQ > 1:
                    break
            nblk_h = 1 if is_sh else NBQ
            for rep in range(NBQ if is_sh else 1):
                pass
            if ei == 15:
                load_shared(1)
            w2blocks = range(NBQ)
            for blk in w2blocks:
                bt0 = tok0 + blk * 512
                if is_sh and blk > 0:
                    for fc in range(nfc):
                        sp = fc % 2
                        slot = fc
                        pa, pb = h_ps[sp]
                        for cc in range(NCH):
                            l1 = wsl(0, cc, FDFF, fc * 128, (fc + 1) * 128)
                            l3 = wsl(16384, cc, FDFF,
                                     fc * 128, (fc + 1) * 128)
                            w = sh_ld[1] + [
                                 xnd_done[cc], hfreeA[sp], hfreeV[sp]]
                            mm(pa, l1, xnT(cc, bt0, bt0 + 512),
                               start=(cc == 0), stop=(cc == NCH - 1), waits=w,
                               incs=[(s_th, 1)] if cc == NCH - 1 else ())
                            mm(pb, l3, xnT(cc, bt0, bt0 + 512),
                               start=(cc == 0), stop=(cc == NCH - 1),
                               incs=[(s_th, 1)] if cc == NCH - 1 else ())
                        thn = s_th.n
                        bc1 = sb13T[:, fc:fc + 1]
                        bc3 = sb13T[:, 8 + fc:9 + fc]
                        aop(s1_t[sp], pa, AF.Silu, bias=bc1,
                            waits=[(s_th, thn), (s_vh, hfreeV[sp][1]),
                                   (s_ah, s_ah.n)], incs=[(s_ah, 1)])
                        hfreeA[sp] = (s_ah, s_ah.n)
                        vop("scalar_tensor_tensor",
                            ghT[:, slot * 512:(slot + 1) * 512], pb, bc3,
                            s1_t[sp], ALU.add, ALU.mult,
                            waits=[(s_ah, s_ah.n), (s_th, thn), ghfree[slot],
                                   (s_vy, s_vy.n)], incs=[(s_vh, 1)])
                        hfreeV[sp] = (s_vh, s_vh.n)
                        gh_ready[(blk, fc)] = s_vh.n
                    if blk == NBQ - 1 and q + 1 < NQ:
                        wfree[0] = (s_th, s_th.n)
                        load_expert(0, 0)
                for tw in range(4):
                    tg = bt0 // 128 + tw
                    tq = tg - tok0 // 128
                    yp = y_ps[tw % 2]
                    for fc in range(nfc):
                        slot = fc if is_sh else (blk % 2) * 4 + fc
                        for dh in range(2):
                            r = wsl(32768, fc, D, dh * 512, (dh + 1) * 512) \
                                if is_sh else \
                                wsl(buf * 24576 + 16384, fc, D,
                                    dh * 512, (dh + 1) * 512)
                            stop = (not is_sh) and fc == nfc - 1
                            last = stop and dh == 1
                            w2w = [(s_vh, gh_ready[(blk, fc)]),
                                   yfree[tw % 2]]
                            if is_sh:
                                w2w += sh_ld[2]
                            mm(yp[:, dh * 512:(dh + 1) * 512],
                               ghT[:, slot * 512 + tw * 128:
                                   slot * 512 + tw * 128 + 128], r,
                               start=(fc == 0), stop=stop,
                               waits=w2w,
                               incs=[(s_ty, 1)] if last else ())
                    if is_sh:
                        for dh in range(2):
                            mm(yp[:, dh * 512:(dh + 1) * 512],
                               gatewT[:, tg * 128:(tg + 1) * 128],
                               b2_sb[:, dh * 512:(dh + 1) * 512],
                               start=False, stop=False,
                               waits=[(s_v1, t1tr.get(tg, 0)),
                                      (s_v1, V1N)])
                            mm(yp[:, dh * 512:(dh + 1) * 512],
                               ones_row[:, 0:128],
                               sb2_row[:, dh * 512:(dh + 1) * 512],
                               start=False, stop=True,
                               waits=[(s_setc, 3)],
                               incs=[(s_ty, 1)] if dh == 1 else ())
                    tyn = s_ty.n
                    ycol = y_acc[:, tq * D:(tq + 1) * D]
                    gcol = None if is_sh else \
                        gatew[:, tg * NE + ei:tg * NE + ei + 1]
                    if ei == 0:
                        vop("tensor_scalar", ycol, yp, gcol, None, ALU.mult,
                            waits=[(s_ty, tyn), (s_vy, vy_last.get(tq, 0)),
                                   (s_v1, V1N)], incs=[(s_vy, 1)])
                    elif not is_sh:
                        vop("scalar_tensor_tensor", ycol, yp, gcol, ycol,
                            ALU.mult, ALU.add,
                            waits=[(s_ty, tyn), (s_vy, vy_last.get(tq, 0))],
                            incs=[(s_vy, 1)])
                    else:
                        vop("tensor_tensor", ycol, yp, ycol, ALU.add,
                            waits=[(s_ty, tyn), (s_vy, vy_last.get(tq, 0))],
                            incs=[(s_vy, 1)])
                    vy_last[tq] = s_vy.n
                    yfree[tw % 2] = (s_vy, s_vy.n)
                    if is_sh:
                        aop(g_t[tw % 2], ycol, AF.Silu,
                            waits=[(s_vy, vy_last[tq]), (s_a1, s_a1.n),
                                   (s_ag, s_ag.n),
                                   (s_gsp[tw % 2], 16 * (tg // 2))],
                            incs=[(s_ag, 1)])
                        dmaS(G_SP[tg * 128:(tg + 1) * 128, :], g_t[tw % 2],
                             waits=[(s_ag, s_ag.n)],
                             incs=[(s_gsp[tw % 2], 16)])
                for fc in range(nfc):
                    slot = fc if is_sh else (blk % 2) * 4 + fc
                    ghfree[slot] = (s_ty, s_ty.n)
            wfree[buf] = (s_ty, s_ty.n)
            if is_sh:
                wfree[0] = wfree[1] = (s_ty, s_ty.n)

    sch.barrier(s_bar[2], extra=all_dma_sems())
    AR.pop()
    AR.push()
    o_va = AR.alloc(va_cols * 2)
    V_aug = AR.bf(o_va, 0, 128, va_cols)
    o_wp = AR.alloc(NCH * D * 2)
    Wp = AR.bf(o_wp, 0, 128, NCH * D)
    stg = [AR.f32(AR.alloc(D * 4), 0, 128, D) for _ in range(2)]
    raw = [AR.bf(AR.alloc(D * 2), 0, 128, D) for _ in range(2)]
    Bp = AR.f32(AR.alloc(D * 4), 0, 128, D)
    qf = [AR.f32(AR.alloc(D * 4), 0, 128, D) for _ in range(2)]
    qh_bf = [AR.bf(AR.alloc(D * 2), 0, 128, D) for _ in range(2)]
    sq3 = AR.f32(AR.alloc(D * 4), 0, 128, D)
    ssq = AR.f32(AR.alloc(NH * 4), 0, 128, NH)
    rn3 = AR.f32(AR.alloc(NH * 4), 0, 128, NH)
    bp_row = AR.bf(AR.alloc(D * 2), 0, 1, D)

    WMAP = [(WQd, 0, QH_SP, s_qsp), (WKd, 1, KH_SP, s_ksp),
            (WVd, 2, None, None)]
    v3l, a3l = {}, {}
    for pi, (Wd, bi, SPt, ssp) in enumerate(WMAP):
        for kk in (-2, -1):
            v3l[("w", pi, kk)] = s_v3.n
            v3l[("t", pi, kk)] = s_v3.n
            a3l[("t", pi, kk)] = s_a3.n
        prevsp = []
        if pi == 1:
            prevsp = [(s_qsp[0], s_qsp[0].n), (s_qsp[1], s_qsp[1].n)]
        elif pi == 2:
            prevsp = [(s_ksp[0], s_ksp[0].n), (s_ksp[1], s_ksp[1].n)]
        for cc in range(NCH):
            dmaS(stg[cc % 2], Wd[cc * 128:(cc + 1) * 128, :],
                 waits=[(s_v3, v3l.get(("w", pi, cc - 2), 0))],
                 incs=[(s_ws[cc % 2], 16)])
            wsn = (s_ws[cc % 2], s_ws[cc % 2].n)
            vop("tensor_scalar", Wp[:, cc * D:(cc + 1) * D], stg[cc % 2],
                lnw_sb[:, cc:cc + 1], None, ALU.mult,
                waits=[wsn, (s_set, SETUP_N), (s_setp, SETP_N),
                       (s_t3, s_t3.n), (s_v3, s_v3.n)], incs=[(s_v3, 1)])
            vop("tensor_copy", raw[cc % 2], stg[cc % 2],
                waits=[wsn, (s_t3, s_t3.n), (s_v3, s_v3.n)],
                incs=[(s_v3, 1)])
            v3l[("w", pi, cc)] = s_v3.n
            vraw = s_v3.n
            for dh in range(2):
                mm(PS[0:1, 2048 + dh * 512:2560 + dh * 512],
                   lnb_sb[:, cc:cc + 1],
                   raw[cc % 2][:, dh * 512:(dh + 1) * 512],
                   start=(cc == 0), stop=False,
                   waits=[(s_v3, vraw), (s_set, SETUP_N), (s_setp, SETP_N)], incs=[(s_t3, 1)])
        for dh in range(2):
            mm(PS[0:1, 2048 + dh * 512:2560 + dh * 512], ones_row[:, 0:1],
               brow[bi][:, dh * 512:(dh + 1) * 512], start=False, stop=True,
               waits=[(s_setc, 3)], incs=[(s_t3, 1)])
        t3b = s_t3.n
        vop("tensor_copy", bp_row, PS[0:1, 2048:3072],
            waits=[(s_t3, t3b), (s_v3, s_v3.n)], incs=[(s_v3, 1)])
        vbp = s_v3.n
        for dh in range(2):
            mm(PS[:, 3072 + dh * 512:3584 + dh * 512], ones_row[:, 0:128],
               bp_row[:, dh * 512:(dh + 1) * 512], start=True, stop=True,
               waits=[(s_v3, vbp)], incs=[(s_t3, 1)])
        t3c = s_t3.n
        vop("tensor_copy", Bp, PS[:, 3072:4096],
            waits=[(s_t3, t3c), (s_v3, s_v3.n)], incs=[(s_v3, 1)])
        bp_done = s_v3.n
        for j in range(NT):
            p = j % 2
            pq = PS[:, 2048:3072] if p == 0 else PS[:, 3072:4096]
            wl = [(s_v3, v3l.get(("t", pi, j - 1), 0)),
                  (s_a3, a3l.get(("t", pi, j - 1), 0))]
            for cc in range(NCH):
                for dh in range(2):
                    mm(pq[:, dh * 512:(dh + 1) * 512],
                       xnT(cc, j * 128, (j + 1) * 128),
                       Wp[:, cc * D + dh * 512:cc * D + (dh + 1) * 512],
                       start=(cc == 0), stop=(cc == NCH - 1),
                       waits=[(s_v3, bp_done), xnd_done[cc],
                              (s_v3, v3l.get(("t", pi, j - 2), 0))],
                       incs=[(s_t3, 1)] if (cc == NCH - 1 and dh == 1) else ())
            tq3 = s_t3.n
            vop("scalar_tensor_tensor", qf[p], pq, R2[:, j:j + 1], Bp,
                ALU.mult, ALU.add,
                waits=[(s_t3, tq3), (s_v3, s_v3.n)] + wl, incs=[(s_v3, 1)])
            vqf = s_v3.n
            if SPt is not None:
                aop(sq3, qf[p], AF.Square,
                    waits=[(s_v3, vqf), (s_a3, s_a3.n)] + wl,
                    incs=[(s_a3, 1)])
                vop("tensor_reduce", ssq.rearrange("p (h u) -> p h u", u=1),
                    sq3.rearrange("p (h d) -> p h d", d=HD), AX.X, ALU.add,
                    waits=[(s_a3, s_a3.n), (s_v3, s_v3.n)], incs=[(s_v3, 1)])
                vop("tensor_scalar", ssq, ssq, 1e-24, None, ALU.max,
                    waits=[(s_v3, s_v3.n)], incs=[(s_v3, 1)])
                vsq = s_v3.n
                aop(rn3, ssq, AF.Ln, waits=[(s_v3, vsq), (s_a3, s_a3.n)],
                    incs=[(s_a3, 1)])
                aop(rn3, rn3, AF.Exp, scale=-0.5, waits=[(s_a3, s_a3.n)],
                    incs=[(s_a3, 1)])
                arn = s_a3.n
                rnb = bass.AP(tensor=rn3.tensor, offset=rn3.offset,
                              ap=[[rn3.ap[0][0], 128], [1, NH], [0, HD]])
                vop("tensor_tensor",
                    qh_bf[p].rearrange("p (h d) -> p h d", d=HD),
                    qf[p].rearrange("p (h d) -> p h d", d=HD), rnb, ALU.mult,
                    waits=[(s_a3, arn), (s_v3, s_v3.n),
                           (ssp[p], 16 * (j // 2))] + prevsp,
                    incs=[(s_v3, 1)])
                v3l[("t", pi, j)] = s_v3.n
                a3l[("t", pi, j)] = s_a3.n
                dmaS(SPt[j * 128:(j + 1) * 128, :], qh_bf[p],
                     waits=[(s_v3, s_v3.n)], incs=[(ssp[p], 16)])
            else:
                vop("tensor_copy", V_aug[:, j * D:(j + 1) * D], qf[p],
                    waits=[(s_v3, vqf)], incs=[(s_v3, 1)])
                v3l[("t", pi, j)] = s_v3.n
                a3l[("t", pi, j)] = s_a3.n
    V3F = s_v3.n

    sch.barrier(s_bar[3], extra=all_dma_sems())
    AR.pop()
    AR.pop()
    AR.push()

    def allocA(nbytes):
        nb = (nbytes + 63) // 64 * 64
        if AR.off + nb <= o_va:
            return AR.alloc(nbytes)
        return None
    NRING = globals()['kernel_NRING']
    o_kt = allocA(NCH * S * 2)
    o_wo = allocA(NCH * D * 2)
    o_ring = allocA(NRING * 1024 * 2)
    o_st4 = [allocA(64) for _ in range(2)]
    assert AR.off <= o_va, (AR.off, o_va)
    AR.off = o_va + va_cols * 2
    V2 = AR.bf(o_va, 0, 128, va_cols)
    if o_kt is None:
        o_kt = AR.alloc(NCH * S * 2)
    if o_wo is None:
        o_wo = AR.alloc(NCH * D * 2)
    if o_ring is None:
        o_ring = AR.alloc(NRING * 1024 * 2)
    o_st4 = [o if o is not None else AR.alloc(64) for o in o_st4]
    Wo_sb = AR.bf(o_wo, 0, 128, NCH * D)
    ring = AR.bf(o_ring, 0, 128, NRING * 1024)
    ST4 = [AR.f32(o, 0, 128, 16) for o in o_st4]
    qTb = AR.bf(AR.alloc(NCH * 512 * 2), 0, 128, NCH * 512)
    O_T = AR.bf(AR.alloc(NPAIR * 512 * 2), 0, 128, NPAIR * 512)
    stT = [AR.bf(AR.alloc(512 * 2), 0, 128, 512) for _ in range(2)]
    su_s = AR.f32(AR.alloc(512 * 4), 0, 128, 512)
    su = [su_s, su_s]
    lr = AR.f32(AR.alloc(512 * 4), 0, 1, 512)
    rinv_p = [AR.f32(AR.alloc(512 * 4), 0, 1, 512) for _ in range(2)]
    rinv = [rinv_p, rinv_p]
    x4 = [AR.f32(AR.alloc(D * 4), 0, 128, D) for _ in range(2)]
    g4 = [AR.f32(AR.alloc(D * 4), 0, 128, D) for _ in range(2)]
    o4_s = AR.f32(AR.alloc(D * 4), 0, 128, D)
    o4 = [o4_s, o4_s]
    xn4_s = AR.f32(AR.alloc(D * 4), 0, 128, D)
    xn4 = [xn4_s, xn4_s]
    o_rbc = AR.alloc(512 * 4)
    rbcA_sb = AR.f32(o_rbc, 0, 64, 512)
    rbcB_sb = AR.f32(o_rbc, 64, 128, 512)

    def kTs(pair, p0, p1, c0, c1):
        return AR.h[p0:p1, o_kt // 2 + pair * S + c0: o_kt // 2 + pair * S + c1]

    kt_done = {}

    def load_kT(cc, dmaF, sm=None):
        sm = s_ktd[cc]
        dmaF(kTs(cc, 0, 128, 0, S), KH_SP[:, cc * 128:(cc + 1) * 128],
             waits=[(s_ksp[0], 16 * ((NT + 1) // 2)),
                    (s_ksp[1], 16 * (NT // 2)), (s_v3, V3F)],
             incs=[(sm, 16)], transpose=True)
        kt_done[cc] = (sm, sm.n)
    load_kT(0, dmaS)
    for cc in range(4, NCH):
        load_kT(cc, dmaA)
    dmaP(Wo_sb.rearrange("p (c d) -> p c d", d=D),
         WOd.rearrange("(c p) d -> p c d", p=128), waits=[(s_v3, V3F)],
         incs=[(s_wo4, 16)])
    WO4 = s_wo4.n

    SC = [PS[:, 0:1024], PS[:, 1024:2048]]
    AVb = [PS[:, 2048:3072], PS[:, 3072:4096]]
    avA = [AVb[s][0:64, 0:512] for s in range(2)]
    rsA = [AVb[s][64:65, 0:512] for s in range(2)]
    avB = [AVb[s][64:128, 512:1024] for s in range(2)]
    rsB = [AVb[s][0:1, 512:1024] for s in range(2)]
    rbcA = [AVb[s][0:64, 0:512] for s in range(2)]
    rbcB = [AVb[s][64:128, 512:1024] for s in range(2)]

    sc_free = {0: (s_aex, 0), 1: (s_aex, 0)}
    ring_free = {i: (s_tav, 0) for i in range(NRING)}
    av_free = {0: [], 1: []}
    stT_rd = {0: (s_tsc, 0), 1: (s_tsc, 0)}
    state_ok = {0: 0, 1: 0}
    oT_free = (s_tt, 0)
    qT_rd = (s_tsc, 0)
    gc = [0]
    rbc_rd = [(s_vn, 0)]
    pending = [None]
    ti_v, ti_a = {}, {}
    ti = 0

    def emit_norm(unit):
        pair, st2, slot, tavn = unit
        vop("reciprocal", rinv[slot][0], rsA[slot],
            waits=[(s_tav, tavn), (s_vn, s_vn.n), (s_nbc, s_nbc.n)],
            incs=[(s_vn, 1)])
        anA = (s_vn, s_vn.n)
        vop("reciprocal", rinv[slot][1], rsB[slot],
            waits=[(s_vn, s_vn.n), (s_nbc, s_nbc.n)], incs=[(s_vn, 1)])
        anB = (s_vn, s_vn.n)
        dstA = stT[slot][0:64, :] if st2 == 0 else \
            O_T[0:64, pair * 512:(pair + 1) * 512]
        dstB = stT[slot][64:128, :] if st2 == 0 else \
            O_T[64:128, pair * 512:(pair + 1) * 512]
        wdst = [stT_rd[slot]] if st2 == 0 else [oT_free]
        vop("tensor_copy", su[slot][0:64, :], AVb[slot][0:64, 0:512],
            waits=[(s_tav, tavn), (s_vn, s_vn.n)], incs=[(s_vn, 1)])
        vop("tensor_copy", su[slot][64:128, :], avB[slot],
            waits=[(s_tav, tavn), (s_vn, s_vn.n)], incs=[(s_vn, 1)])
        vsu = s_vn.n
        r0 = rinv[slot][0]
        r1 = rinv[slot][1]
        dmaP(rbcA_sb,
             bass.AP(tensor=r0.tensor, offset=r0.offset,
                     ap=[[r0.ap[0][0], 1], [0, 64], [1, 512]]),
             waits=[anA, rbc_rd[0]], incs=[(s_nbc, 16)])
        dmaP(rbcB_sb,
             bass.AP(tensor=r1.tensor, offset=r1.offset,
                     ap=[[r1.ap[0][0], 1], [0, 64], [1, 512]]),
             waits=[anB], incs=[(s_nbc, 16)])
        nbcn = s_nbc.n
        vop("tensor_tensor", dstA, su[slot][0:64, :], rbcA_sb, ALU.mult,
            waits=[(s_nbc, nbcn), (s_vn, s_vn.n)] + wdst, incs=[(s_vn, 1)])
        vop("tensor_tensor", dstB, su[slot][64:128, :], rbcB_sb, ALU.mult,
            waits=[(s_nbc, nbcn), (s_vn, s_vn.n)] + wdst, incs=[(s_vn, 1)])
        rbc_rd[0] = (s_vn, s_vn.n)
        av_free[slot] = [(s_vn, vsu), anB]
        if st2 == 0:
            state_ok[slot] = s_vn.n

    def emit_unit(qb, pair, st2, slot, qtn):
        LAG = 2
        expcnt = {}
        for it in range(NT + LAG):
            c = it
            if c < NT:
                scs = c % 2
                rgs = c % NRING
                w = [sc_free[scs], qtn, kt_done[pair]]
                if st2 == 1:
                    w.append((s_vn, state_ok[slot]))
                rhs = qTb[:, pair * 512:(pair + 1) * 512] if st2 == 0 \
                    else stT[slot][:, :]
                mm(SC[scs][:, 0:512],
                   kTs(pair, 0, 64, c * 128, (c + 1) * 128), rhs[0:64, :],
                   start=True, stop=True, waits=w, tp=(0, 0))
                mm(SC[scs][:, 512:1024],
                   kTs(pair, 64, 128, c * 128, (c + 1) * 128), rhs[64:128, :],
                   start=True, stop=True, tp=(64, 0), incs=[(s_tsc, 1)])
                aop(ring[:, rgs * 1024:(rgs + 1) * 1024], SC[scs], AF.Exp,
                    waits=[(s_tsc, s_tsc.n), ring_free[rgs],
                           (s_aex, s_aex.n)], incs=[(s_aex, 1)])
                sc_free[scs] = (s_aex, s_aex.n)
                expcnt[c] = s_aex.n
                gc[0] += 1
            ca = it - LAG
            if ca >= 0:
                rgs2 = ca % NRING
                aw = [(s_aex, expcnt[ca])]
                if ca == 0:
                    aw += av_free[slot] + [(s_setc, 3), (s_v3, V3F)]
                vaA = V2[:, ca * D + 2 * pair * HD:
                         ca * D + 2 * pair * HD + HD]
                vaB = V2[:, ca * D + (2 * pair + 1) * HD:
                         ca * D + (2 * pair + 1) * HD + HD]
                rgA = ring[:, rgs2 * 1024:rgs2 * 1024 + 512]
                rgB = ring[:, rgs2 * 1024 + 512:rgs2 * 1024 + 1024]
                mm(avA[slot], vaA, rgA, start=(ca == 0), stop=(ca == NT - 1),
                   waits=aw)
                mm(avB[slot], vaB, rgB, start=(ca == 0), stop=(ca == NT - 1),
                   tp=(0, 64))
                mm(rsB[slot], ones_col, rgB, start=(ca == 0),
                   stop=(ca == NT - 1), tp=(0, 0))
                mm(rsA[slot], ones_col, rgA, start=(ca == 0),
                   stop=(ca == NT - 1), tp=(0, 64), incs=[(s_tav, 4)])
                ring_free[rgs2] = (s_tav, s_tav.n)
            if it == DEFER_AT and pending[0] is not None and not DEFER_OFF:
                emit_norm(pending[0])
                pending[0] = None
        if st2 == 1:
            stT_rd[slot] = (s_tsc, s_tsc.n)
        tavn = s_tav.n
        if DEFER_OFF:
            emit_norm((pair, st2, slot, tavn))
        else:
            pending[0] = (pair, st2, slot, tavn)

    def emit_qT(qb):
        for cc in range(NCH):
            dmaS(qTb[:, cc * 512:(cc + 1) * 512],
                 QH_SP[qb * 512:(qb + 1) * 512, cc * 128:(cc + 1) * 128],
                 waits=[(s_qsp[0], 16 * ((NT + 1) // 2)),
                        (s_qsp[1], 16 * (NT // 2)), qT_rd, (s_v3, V3F)],
                 incs=[(s_qT[0], 16)], transpose=True)
        return (s_qT[0], s_qT[0].n)

    qtn = emit_qT(0)
    for cc in range(1, 4):
        load_kT(cc, dmaS)
    for qb in range(NB):
        qtn_next = qtn
        for duo in range(0, NPAIR, 2):
            emit_unit(qb, duo, 0, 0, qtn)
            emit_unit(qb, duo + 1, 0, 1, qtn)
            if duo == NPAIR - 2:
                qT_rd = (s_tsc, s_tsc.n)
                if qb + 1 < NB:
                    qtn_next = emit_qT(qb + 1)
            emit_unit(qb, duo, 1, 0, qtn)
            emit_unit(qb, duo + 1, 1, 1, qtn)
        if pending[0] is not None:
            emit_norm(pending[0])
            pending[0] = None
        oT_ready = s_vn.n
        for tw in range(4):
            p = ti % 2
            row0 = qb * 512 + tw * 128
            dmaS(x4[p], X[row0:row0 + 128, :],
                 waits=[(s_vt, ti_v.get(ti - 2, 0)),
                        (s_at, ti_a.get(ti - 2, 0))], incs=[(s_xt[p], 16)])
            xt_n = (s_xt[p], s_xt[p].n)
            dmaS(g4[p], G_SP[row0:row0 + 128, :],
                 waits=[(s_gsp[0], 16 * ((NT + 1) // 2)),
                        (s_gsp[1], 16 * (NT // 2)),
                        (s_vt, ti_v.get(ti - 2, 0))],
                 incs=[(s_gt[p], 16)])
            gt_n = (s_gt[p], s_gt[p].n)
            for pr in range(NPAIR):
                for dh in range(2):
                    mm(AVb[1][:, dh * 512:(dh + 1) * 512],
                       O_T[:, pr * 512 + tw * 128:pr * 512 + tw * 128 + 128],
                       Wo_sb[:, pr * D + dh * 512:pr * D + (dh + 1) * 512],
                       start=(pr == 0), stop=False,
                       waits=[(s_vn, oT_ready), (s_v3, V3F), (s_wo4, WO4),
                              (s_an, s_an.n),
                              (s_vt, ti_v.get(ti - 1, 0))] + av_free[1])
            for dh in range(2):
                mm(AVb[1][:, dh * 512:(dh + 1) * 512], ones_row[:, 0:128],
                   bo_row[:, dh * 512:(dh + 1) * 512], start=False, stop=True,
                   waits=[(s_set, SETUP_N), (s_setp, SETP_N), (s_setc, 3)],
                   incs=[(s_tt, 1)] if dh == 1 else ())
            ttn = s_tt.n
            st = ST4[p]

            def c4(i, st=st):
                return st[:, i:i + 1]
            wl = [(s_vt, ti_v.get(ti - 1, 0)), (s_at, ti_a.get(ti - 1, 0))]
            vop("reduce_sum", c4(0), x4[p], AX.X,
                waits=[xt_n, (s_vt, s_vt.n)] + wl, incs=[(s_vt, 1)])
            aop(xn4[p], x4[p], AF.Square, accum=c4(9),
                waits=[xt_n, (s_at, s_at.n)] + wl, incs=[(s_at, 1)])
            asq = s_at.n
            vop("tensor_scalar", c4(1), c4(0), 1.0 / D, None, ALU.mult,
                waits=[(s_vt, s_vt.n)], incs=[(s_vt, 1)])
            vop("tensor_tensor", c4(2), c4(0), c4(0), ALU.mult,
                waits=[(s_vt, s_vt.n)], incs=[(s_vt, 1)])
            vop("tensor_scalar", c4(3), c4(2), 1.0 / (D * D), None, ALU.mult,
                waits=[(s_vt, s_vt.n)], incs=[(s_vt, 1)])
            vop("scalar_tensor_tensor", c4(4), c4(9), 1.0 / D, c4(3),
                ALU.mult, ALU.subtract,
                waits=[(s_at, asq), (s_vt, s_vt.n)], incs=[(s_vt, 1)])
            vop("tensor_scalar", c4(5), c4(4), EPS, None, ALU.add,
                waits=[(s_vt, s_vt.n)], incs=[(s_vt, 1)])
            vv4 = s_vt.n
            aop(c4(10), c4(5), AF.Ln,
                waits=[(s_vt, vv4), (s_at, s_at.n)], incs=[(s_at, 1)])
            aop(c4(10), c4(10), AF.Exp, scale=-0.5, waits=[(s_at, s_at.n)],
                incs=[(s_at, 1)])
            ar14 = s_at.n
            vop("tensor_scalar", xn4[p], x4[p], c4(1), c4(10), ALU.subtract,
                ALU.mult, waits=[(s_at, ar14), (s_vt, s_vt.n)],
                incs=[(s_vt, 1)])
            vop("tensor_tensor", o4[p], AVb[1][:, :], xn4[p], ALU.add,
                waits=[(s_tt, ttn), (s_vt, s_vt.n),
                       (s_out[0], s_out[0].n), (s_out[1], s_out[1].n)],
                incs=[(s_vt, 1)])
            vop("tensor_tensor", xn4[p], o4[p], g4[p], ALU.mult,
                waits=[gt_n, (s_vt, s_vt.n)], incs=[(s_vt, 1)])
            vop("tensor_tensor", o4[p], xn4[p], x4[p], ALU.add,
                waits=[(s_vt, s_vt.n)], incs=[(s_vt, 1)])
            ti_v[ti] = s_vt.n
            ti_a[ti] = s_at.n
            dmaS(OUT[row0:row0 + 128, :], o4[p], waits=[(s_vt, ti_v[ti])],
                 incs=[(s_out[p], 16)])
            ti += 1
        av_free[1] = [(s_vt, ti_v[ti - 1]), (s_an, s_an.n)]
        oT_free = (s_tt, s_tt.n)
        qtn = qtn_next

    sch.op("S", lambda e: e.nop(),
           waits=[(s_out[0], 16 * ((NT + 1) // 2)),
                  (s_out[1], 16 * (NT // 2))])

    with nc.Block() as blk:
        blk.tensor(lambda e: sch.replay("T", e))
        blk.vector(lambda e: sch.replay("V", e))
        blk.scalar(lambda e: sch.replay("A", e))
        blk.sync(lambda e: sch.replay("S", e))
        blk.gpsimd(lambda e: sch.replay("P", e))
    return nc, stack


_CACHE = {}


def _get(S):
    if S not in _CACHE:
        _CACHE[S] = _build(S)
    return _CACHE[S]


def kernel(**inputs):
    ip = {k: np.asarray(v) for k, v in inputs.items()}
    x = ip["L0_attn_hidden_states"]
    B, S, D_ = x.shape
    C = np.ascontiguousarray
    common = {
        "wq": C(ip["L0_attn_Wq"]), "wk": C(ip["L0_attn_Wk"]),
        "wv": C(ip["L0_attn_Wv"]), "wo": C(ip["L0_attn_Wo"]),
        "lnw": C(ip["L0_attn_ln_w"].reshape(1, -1)),
        "lnb": C(ip["L0_attn_ln_b"].reshape(1, -1)),
        "bq": C(ip["L0_attn_bq"].reshape(1, -1)),
        "bk": C(ip["L0_attn_bk"].reshape(1, -1)),
        "bv": C(ip["L0_attn_bv"].reshape(1, -1)),
        "bo": C(ip["L0_attn_bo"].reshape(1, -1)),
        "gwt": C(ip["L0_ffn_gate_w"].T),
        "eb": C(ip["L0_ffn_expert_bias"].reshape(1, -1)),
        "w1t": C(ip["L0_ffn_w1"].transpose(0, 2, 1)),
        "w3t": C(ip["L0_ffn_w3"].transpose(0, 2, 1)),
        "w2t": C(ip["L0_ffn_w2"].transpose(0, 2, 1)),
        "b1": C(ip["L0_ffn_b1"]), "b3": C(ip["L0_ffn_b3"]),
        "b2": C(ip["L0_ffn_b2"]),
        "sw1t": C(ip["L0_ffn_sw1"].T), "sw3t": C(ip["L0_ffn_sw3"].T),
        "sw2t": C(ip["L0_ffn_sw2"].T),
        "sb1": C(ip["L0_ffn_sb1"].reshape(1, -1)),
        "sb3": C(ip["L0_ffn_sb3"].reshape(1, -1)),
        "sb2": C(ip["L0_ffn_sb2"].reshape(1, -1)),
    }
    nc, _ = _get(S)
    in_maps = [dict(common, x=C(x[b])) for b in range(B)]
    res = run_bass_kernel_spmd(nc, in_maps, core_ids=list(range(B)))
    return np.stack([r["out"] for r in res.results], axis=0)


# version 32
# speedup vs baseline: 1.4354x; 1.0028x over previous
import numpy as np
import ml_dtypes
import contextlib
import concourse.bass as bass
import concourse.mybir as mybir
from concourse.bass_utils import run_bass_kernel_spmd

F32 = mybir.dt.float32
BF16 = mybir.dt.bfloat16
AF = mybir.ActivationFunctionType
ALU = mybir.AluOpType
AX = mybir.AxisListType

D = 1024
NCH = 8
NH = 16
HD = 64
NE = 16
DFF = 512
FDFF = 1024
EPS = 1e-5
STRICT_WAITS = False
DEFER_OFF = False
DEFER_AT = 3
kernel_NRING = 3


class Sem:
    __slots__ = ("h", "n", "name")

    def __init__(self, h, name):
        self.h, self.n, self.name = h, 0, name


class Sched:
    ENGS = ("T", "V", "A", "S", "P")

    def __init__(self):
        self.ops = {k: [] for k in self.ENGS}
        self.hw = {k: {} for k in self.ENGS}
        self.last = {k: None for k in self.ENGS}
        self.lastinc = {k: None for k in self.ENGS}
        self.dirty = {k: False for k in self.ENGS}
        self.pending = {k: None for k in self.ENGS}

    def op(self, eng, build, waits=(), incs=()):
        wl = []
        if self.pending[eng]:
            wl += self.pending[eng]
            self.pending[eng] = None
        wl += list(waits)
        best = {}
        for s, v in wl:
            if v > 0 and v > best.get(id(s), (None, 0))[1]:
                best[id(s)] = (s, v)
        if globals().get('STRICT_WAITS', False):
            wfin = [(s.h, v) for s, v in best.values()]
        else:
            wfin = []
            for s, v in best.values():
                if v > self.hw[eng].get(id(s), 0):
                    self.hw[eng][id(s)] = v
                    wfin.append((s.h, v))
        il = []
        for s, k in incs:
            s.n += k
            il.append((s.h, k))
        ent = {"w": wfin, "i": il, "b": build}
        self.ops[eng].append(ent)
        self.last[eng] = ent
        if incs:
            s0, _ = incs[0]
            self.lastinc[eng] = (s0, s0.n)
        self.dirty[eng] = True
        return ent

    def barrier(self, sem=None, extra=()):
        pend = list(extra)
        for eng in ("T", "V", "A"):
            if self.dirty[eng] and self.lastinc[eng] is not None:
                pend.append(self.lastinc[eng])
        for eng in self.ENGS:
            self.dirty[eng] = False
            self.pending[eng] = list(pend)

    def replay(self, eng, e):
        for ent in self.ops[eng]:
            for h, v in ent["w"]:
                e.wait_ge(h, v)
            ins = ent["b"](e)
            for h, k in ent["i"]:
                ins.then_inc(h, k)


class Arena:
    def __init__(self, h):
        self.h = h
        self.h32 = h.bitcast(F32)
        self.off = 0
        self.marks = []
        self.cap = h.shape[1] * 2

    def alloc(self, nbytes):
        nbytes = (nbytes + 63) // 64 * 64
        o = self.off
        self.off += nbytes
        assert self.off <= self.cap, f"arena overflow {self.off} > {self.cap}"
        return o

    def bf(self, off, p0, p1, cols):
        return self.h[p0:p1, off // 2: off // 2 + cols]

    def f32(self, off, p0, p1, cols):
        return self.h32[p0:p1, off // 4: off // 4 + cols]

    def push(self):
        self.marks.append(self.off)

    def pop(self):
        self.off = self.marks.pop()


def _build(S):
    NT = S // 128
    NB = S // 512
    QT = min(S, 1024)
    NQ = S // QT
    NBQ = QT // 512
    NTQ = QT // 128
    NPAIR = NH // 2

    nc = bass.Bass("TRN2", target_bir_lowering=False)
    dt = nc.dram_tensor
    X = dt("x", [S, D], F32, kind="ExternalInput").ap()
    OUT = dt("out", [S, D], F32, kind="ExternalOutput").ap()
    WQd = dt("wq", [D, D], F32, kind="ExternalInput").ap()
    WKd = dt("wk", [D, D], F32, kind="ExternalInput").ap()
    WVd = dt("wv", [D, D], F32, kind="ExternalInput").ap()
    WOd = dt("wo", [D, D], F32, kind="ExternalInput").ap()
    LNW = dt("lnw", [1, D], F32, kind="ExternalInput").ap()
    LNB = dt("lnb", [1, D], F32, kind="ExternalInput").ap()
    BQd = dt("bq", [1, D], F32, kind="ExternalInput").ap()
    BKd = dt("bk", [1, D], F32, kind="ExternalInput").ap()
    BVd = dt("bv", [1, D], F32, kind="ExternalInput").ap()
    BOd = dt("bo", [1, D], F32, kind="ExternalInput").ap()
    GWT = dt("gwt", [D, NE], F32, kind="ExternalInput").ap()
    EBd = dt("eb", [1, NE], F32, kind="ExternalInput").ap()
    W1T = dt("w1t", [NE, D, DFF], F32, kind="ExternalInput").ap()
    W3T = dt("w3t", [NE, D, DFF], F32, kind="ExternalInput").ap()
    W2T = dt("w2t", [NE, DFF, D], F32, kind="ExternalInput").ap()
    B1d = dt("b1", [NE, DFF], F32, kind="ExternalInput").ap()
    B3d = dt("b3", [NE, DFF], F32, kind="ExternalInput").ap()
    B2d = dt("b2", [NE, D], F32, kind="ExternalInput").ap()
    SW1T = dt("sw1t", [D, FDFF], F32, kind="ExternalInput").ap()
    SW3T = dt("sw3t", [D, FDFF], F32, kind="ExternalInput").ap()
    SW2T = dt("sw2t", [FDFF, D], F32, kind="ExternalInput").ap()
    SB1d = dt("sb1", [1, FDFF], F32, kind="ExternalInput").ap()
    SB3d = dt("sb3", [1, FDFF], F32, kind="ExternalInput").ap()
    SB2d = dt("sb2", [1, D], F32, kind="ExternalInput").ap()

    XN_SP = dt("xn_sp", [S, D], BF16, kind="Internal").ap()
    QH_SP = dt("qh_sp", [S, D], BF16, kind="Internal").ap()
    KH_SP = dt("kh_sp", [S, D], BF16, kind="Internal").ap()
    va_cols = (S // 128) * D
    VA_SP = dt("va_sp", [128, va_cols], BF16, kind="Internal").ap()
    G_SP = dt("g_sp", [S, D], F32, kind="Internal").ap()
    IDEN = nc.inline_tensor(np.eye(128, dtype=ml_dtypes.bfloat16), "iden").ap()

    stack = contextlib.ExitStack()

    def sem(name):
        return Sem(stack.enter_context(nc.semaphore(name)), name)

    s_set, s_setp, s_setc = sem("set"), sem("setp"), sem("setc")
    s_bar = [sem(f"bar{i}") for i in range(5)]
    s_x = [sem("sx0"), sem("sx1")]
    s_v0, s_a0 = sem("sv0"), sem("sa0")
    s_t0 = sem("st0")
    s_xnsp = [sem("sxnsp0"), sem("sxnsp1")]
    s_xnT = sem("sxnT")
    s_xnd = [sem(f"sxnd{c}") for c in range(8)]
    s_t1, s_v1, s_a1 = sem("st1"), sem("sv1"), sem("sa1")
    s_wld = [sem("swld0"), sem("swld1")]
    s_th, s_ah, s_vh = sem("sth"), sem("sah"), sem("svh")
    s_wsh = sem("swsh")
    s_ty, s_vy, s_ag = sem("sty"), sem("svy"), sem("sag")
    s_gsp = [sem("sgsp0"), sem("sgsp1")]
    s_ws = [sem("sws0"), sem("sws1")]
    s_v3, s_a3, s_t3 = sem("sv3"), sem("sa3"), sem("st3")
    s_qsp = [sem("sqsp0"), sem("sqsp1")]
    s_ksp = [sem("sksp0"), sem("sksp1")]
    s_vsp = sem("svsp")
    s_p4l, s_wo4 = sem("sp4l"), sem("swo4")
    s_p4a = sem("sp4a")
    s_ktd = [sem(f"sktd{c}") for c in range(8)]
    s_qT = [sem("sqT0"), sem("sqT1")]
    s_tsc, s_aex, s_tav = sem("stsc"), sem("saex"), sem("stav")
    s_an, s_tn, s_vn = sem("san"), sem("stn"), sem("svn")
    s_nbc = sem("snbc")
    s_tt = sem("stt")
    s_xt = [sem("sxt0"), sem("sxt1")]
    s_gt = [sem("sgt0"), sem("sgt1")]
    s_vt, s_at = sem("svt"), sem("sat")
    s_out = [sem("sout0"), sem("sout1")]

    def all_dma_sems():
        out = []
        for s in (s_set, s_setp, s_xnT, s_vsp, s_p4l, s_wo4, s_p4a):
            out.append((s, s.n))
        for s in s_ktd:
            out.append((s, s.n))
        for pairl in (s_x, s_xnsp, s_wld, s_gsp, s_ws, s_qsp, s_ksp, s_qT,
                      s_xt, s_gt, s_out):
            for s in pairl:
                out.append((s, s.n))
        return out

    rem = nc.sbuf_bytes_remaining
    arena_cols = (rem // 2) // 64 * 64
    AR = Arena(nc.alloc_sbuf_tensor("arena", [128, arena_cols], BF16))

    ones_row = AR.bf(AR.alloc(256), 0, 1, 128)
    ones32 = AR.f32(AR.alloc(256), 0, 1, 64)
    ones_col = AR.bf(AR.alloc(64), 0, 128, 1)
    bo_row = AR.bf(AR.alloc(2048), 0, 1, D)
    R2 = AR.f32(AR.alloc(NT * 4), 0, 128, NT)

    AR.push()
    o_xnT = AR.alloc(NCH * S * 2)
    gatew = AR.f32(AR.alloc(NT * NE * 4), 0, 128, NT * NE)
    gatewT = AR.bf(AR.alloc(S * 2), 0, 16, S)
    iden_sb = AR.bf(AR.alloc(256), 0, 128, 128)
    gwt_sb = AR.bf(AR.alloc(NCH * NE * 2), 0, 128, NCH * NE)
    eb_sb = AR.f32(AR.alloc(NE * 4), 0, 128, NE)
    b1T = AR.bf(AR.alloc(4 * NE * 2), 0, 128, 4 * NE)
    b3T = AR.bf(AR.alloc(4 * NE * 2), 0, 128, 4 * NE)
    sb13T = AR.bf(AR.alloc(16 * 2), 0, 128, 16)
    b2_sb = AR.bf(AR.alloc(D * 2), 0, 16, D)
    sb2_row = AR.bf(AR.alloc(D * 2), 0, 1, D)
    b1s = AR.bf(AR.alloc(DFF * 2), 0, 16, DFF)
    b3s = AR.bf(AR.alloc(DFF * 2), 0, 16, DFF)
    sb13s = AR.bf(AR.alloc(128 * 2), 0, 16, 128)
    o_brow = AR.alloc(3 * D * 2)
    brow = [AR.bf(o_brow + i * D * 2, 0, 1, D) for i in range(3)]
    lnw_sb = AR.f32(AR.alloc(NCH * 4), 0, 128, NCH)
    lnb_sb = AR.bf(AR.alloc(NCH * 2), 0, 128, NCH)
    gsc = AR.f32(AR.alloc(NE * 4), 0, 128, NE)
    gbi = AR.f32(AR.alloc(NE * 4), 0, 128, NE)
    gb2 = AR.f32(AR.alloc(NE * 4), 0, 128, NE)
    gwu = AR.f32(AR.alloc(NE * 4), 0, 128, NE)
    gm = AR.f32(AR.alloc(32), 0, 128, 8)
    gbf = AR.bf(AR.alloc(NT * NE * 2), 0, 128, NT * NE)

    def xnT(c, t0, t1):
        return AR.h[:, o_xnT // 2 + c * S + t0: o_xnT // 2 + c * S + t1]

    PS = nc.alloc_psum_tensor("ps", [128, 4096], F32)
    PSb = PS.bitcast(BF16)

    sch = Sched()

    def mm(out, lhsT, rhs, start, stop, waits=(), incs=(), tp=None):
        def b(e, out=out, lhsT=lhsT, rhs=rhs, start=start, stop=stop, tp=tp):
            return e.matmul(out, lhsT, rhs, start=start, stop=stop,
                            tile_position=tp)
        return sch.op("T", b, waits, incs)

    def dmaS(out, in_, waits=(), incs=(), transpose=False, slow=False):
        def b(e, out=out, in_=in_, t=transpose, slow=slow):
            return e.dma_start(out, in_, transpose=t,
                               allow_slow_non_contiguous=slow)
        return sch.op("S", b, waits, incs)

    def dmaV(out, in_, waits=(), incs=(), transpose=False, slow=False):
        def b(e, out=out, in_=in_, t=transpose, slow=slow):
            return e.dma_start(out, in_, transpose=t,
                               allow_slow_non_contiguous=slow)
        return sch.op("V", b, waits, incs)

    def dmaA(out, in_, waits=(), incs=(), transpose=False, slow=False):
        def b(e, out=out, in_=in_, t=transpose, slow=slow):
            return e.dma_start(out, in_, transpose=t,
                               allow_slow_non_contiguous=slow)
        return sch.op("A", b, waits, incs)

    def dmaP(out, in_, waits=(), incs=(), slow=False):
        def b(e, out=out, in_=in_, slow=slow):
            return e.dma_start(out, in_, allow_slow_non_contiguous=slow)
        return sch.op("P", b, waits, incs)

    def vop(name, *args, waits=(), incs=()):
        def b(e, name=name, args=args):
            return getattr(e, name)(*args)
        return sch.op("V", b, waits, incs)

    def aop(out, in_, func, bias=0.0, scale=1.0, accum=None, waits=(), incs=()):
        def b(e, out=out, in_=in_, func=func, bias=bias, scale=scale, ac=accum):
            return e.activation(out, in_, func, bias=bias, scale=scale,
                                accum_out=ac)
        return sch.op("A", b, waits, incs)

    dmaP(gwt_sb.rearrange("p (c s) -> p c s", s=NE),
         GWT.rearrange("(c p) s -> p c s", p=128), incs=[(s_setp, 16)])
    dmaP(b2_sb, B2d[:, :], incs=[(s_setp, 16)])
    dmaP(sb2_row, SB2d[:, :], incs=[(s_setp, 16)])
    dmaP(bo_row, BOd[:, :], incs=[(s_setp, 16)])
    for i, B_ in enumerate((BQd, BKd, BVd)):
        dmaP(brow[i], B_[:, :], incs=[(s_setp, 16)])
    dmaP(b1s, B1d[:, :], incs=[(s_setp, 16)])
    dmaP(b3s, B3d[:, :], incs=[(s_setp, 16)])
    dmaP(sb13s[0:8, :], SB1d.rearrange("o (f p) -> (o f) p", p=128),
         incs=[(s_setp, 16)])
    dmaP(sb13s[8:16, :], SB3d.rearrange("o (f p) -> (o f) p", p=128),
         incs=[(s_setp, 16)])
    dmaP(lnb_sb.rearrange("p (c u) -> p c u", u=1),
         LNB.rearrange("o (c p u) -> (o p) c u", p=128, u=1),
         incs=[(s_setp, 16)], slow=True)
    dmaS(iden_sb, IDEN[:, :], incs=[(s_set, 16)])
    dmaS(lnw_sb.rearrange("p (c u) -> p c u", u=1),
         LNW.rearrange("o (c p u) -> (o p) c u", p=128, u=1),
         incs=[(s_set, 16)], slow=True)
    eb_bc = bass.AP(tensor=EBd.tensor, offset=0, ap=[[0, 128], [1, NE]])
    dmaS(eb_sb, eb_bc, incs=[(s_set, 16)])
    SETP_N = s_setp.n
    nset0 = s_set.n
    for i in range(4):
        dmaS(b1T[:, i * 16:(i + 1) * 16], b1s[:, i * 128:(i + 1) * 128],
             waits=[(s_set, s_set.n), (s_setp, SETP_N)],
             incs=[(s_set, 16)], transpose=True)
    for i in range(4):
        dmaS(b3T[:, i * 16:(i + 1) * 16], b3s[:, i * 128:(i + 1) * 128],
             waits=[(s_set, s_set.n)], incs=[(s_set, 16)], transpose=True)
    dmaS(sb13T, sb13s, waits=[(s_set, s_set.n)], incs=[(s_set, 16)],
         transpose=True)
    SETUP_N = s_set.n
    vop("memset", ones_row, 1.0, incs=[(s_setc, 1)])
    vop("memset", ones_col, 1.0, incs=[(s_setc, 1)])
    vop("memset", ones32, 1.0, incs=[(s_setc, 1)])

    AR.push()
    x_t = [AR.f32(AR.alloc(D * 4), 0, 128, D) for _ in range(2)]
    xn_bf = [AR.bf(AR.alloc(D * 2), 0, 128, D) for _ in range(2)]
    sq_t = AR.f32(AR.alloc(D * 4), 0, 128, D)
    STT = [AR.f32(AR.alloc(64), 0, 128, 16) for _ in range(2)]
    v_done, a_done = {}, {}
    trd, trc_done = {}, {}
    for j in range(NT):
        p = j % 2
        st = STT[p]

        def c(i, st=st):
            return st[:, i:i + 1]
        dmaS(x_t[p], X[j * 128:(j + 1) * 128, :],
             waits=[(s_v0, v_done.get(j - 2, 0)), (s_a0, a_done.get(j - 2, 0))],
             incs=[(s_x[p], 16)])
        wx = [(s_x[p], 16 * (j // 2 + 1))]
        wl = [(s_v0, v_done.get(j - 1, 0)), (s_a0, a_done.get(j - 1, 0))]
        vop("reduce_sum", c(0), x_t[p], AX.X, waits=wx + wl, incs=[(s_v0, 1)])
        aop(sq_t, x_t[p], AF.Square, accum=c(9),
            waits=wx + wl + [(s_a0, s_a0.n)], incs=[(s_a0, 1)])
        asq = s_a0.n
        vop("tensor_scalar", c(1), c(0), 1.0 / D, None, ALU.mult,
            waits=[(s_v0, s_v0.n)], incs=[(s_v0, 1)])
        vop("tensor_tensor", c(2), c(0), c(0), ALU.mult,
            waits=[(s_v0, s_v0.n)], incs=[(s_v0, 1)])
        vop("tensor_scalar", c(3), c(2), 1.0 / (D * D), None, ALU.mult,
            waits=[(s_v0, s_v0.n)], incs=[(s_v0, 1)])
        vop("scalar_tensor_tensor", c(4), c(9), 1.0 / D, c(3), ALU.mult,
            ALU.subtract, waits=[(s_a0, asq), (s_v0, s_v0.n)],
            incs=[(s_v0, 1)])
        vop("tensor_scalar", c(5), c(4), EPS, None, ALU.add,
            waits=[(s_v0, s_v0.n)], incs=[(s_v0, 1)])
        vvar = s_v0.n
        aop(c(10), c(5), AF.Ln, waits=[(s_v0, vvar), (s_a0, s_a0.n)],
            incs=[(s_a0, 1)])
        aop(c(10), c(10), AF.Exp, scale=-0.5, waits=[(s_a0, s_a0.n)],
            incs=[(s_a0, 1)])
        ar1 = s_a0.n
        vop("reciprocal", c(6), c(5), waits=[(s_v0, s_v0.n)], incs=[(s_v0, 1)])
        vop("tensor_tensor", c(7), c(4), c(6), ALU.mult,
            waits=[(s_v0, s_v0.n)], incs=[(s_v0, 1)])
        vop("tensor_scalar", c(8), c(7), EPS, None, ALU.add,
            waits=[(s_v0, s_v0.n)], incs=[(s_v0, 1)])
        vt2 = s_v0.n
        aop(R2[:, j:j + 1], c(8), AF.Ln, waits=[(s_v0, vt2), (s_a0, s_a0.n)],
            incs=[(s_a0, 1)])
        aop(R2[:, j:j + 1], R2[:, j:j + 1], AF.Exp, scale=-0.5,
            waits=[(s_a0, s_a0.n)], incs=[(s_a0, 1)])
        vop("tensor_scalar", xn_bf[p], x_t[p], c(1), c(10), ALU.subtract,
            ALU.mult, waits=[(s_a0, ar1), (s_v0, s_v0.n),
                             (s_t0, trd.get(j - 2, 0))],
            incs=[(s_v0, 1)])
        v_done[j], a_done[j] = s_v0.n, s_a0.n
        for cc in range(NCH):
            sch.op("T", lambda e, p=p, cc=cc: e.transpose(
                PSb[:, p * 1024 + cc * 128:p * 1024 + (cc + 1) * 128],
                xn_bf[p][:, cc * 128:(cc + 1) * 128], iden_sb),
                [(s_v0, v_done[j]), (s_set, SETUP_N),
                 (s_v0, trc_done.get(j - 2, 0))],
                incs=[(s_t0, 1)] if cc == NCH - 1 else ())
        trd[j] = s_t0.n
        pbase = PSb[:, p * 1024:p * 1024 + 1024]
        dbase = AR.h[:, o_xnT // 2 + j * 128:o_xnT // 2 + j * 128 + 128]
        vop("tensor_copy",
            bass.AP(tensor=dbase.tensor, offset=dbase.offset,
                    ap=[[dbase.ap[0][0], 128], [S, 8], [1, 128]]),
            bass.AP(tensor=pbase.tensor, offset=pbase.offset,
                    ap=[[pbase.ap[0][0], 128], [128, 8], [1, 128]]),
            waits=[(s_t0, trd[j]), (s_v0, s_v0.n)], incs=[(s_v0, 1)])
        trc_done[j] = s_v0.n
    xnd_done = {}
    for cc in range(NCH):
        xnd_done[cc] = (s_v0, trc_done[NT - 1])

    sch.barrier(s_bar[0], extra=all_dma_sems())
    AR.pop()
    lg_ps = [PS[:, 2048:2048 + NE], PS[:, 2560:2560 + NE]]
    tr_ps = PSb[0:16, 6144:6272]
    v1_done, t1tr = {}, {}
    for j in range(NT):
        p = j % 2
        for cc in range(NCH):
            mm(lg_ps[p], xnT(cc, j * 128, (j + 1) * 128),
               gwt_sb[:, cc * NE:(cc + 1) * NE], start=(cc == 0),
               stop=(cc == NCH - 1),
               waits=[xnd_done[cc], (s_set, SETUP_N), (s_setp, SETP_N),
                      (s_v1, v1_done.get(j - 2, 0))],
               incs=[(s_t1, 1)] if cc == NCH - 1 else ())
        tmm = s_t1.n
        wl = [(s_v1, v1_done.get(j - 1, 0))]
        aop(gsc, lg_ps[p], AF.Sigmoid,
            waits=[(s_t1, tmm), (s_a1, s_a1.n)] + wl, incs=[(s_a1, 1)])
        asig = s_a1.n
        vop("tensor_tensor", gbi, lg_ps[p], eb_sb, ALU.add,
            waits=[(s_t1, tmm), (s_set, SETUP_N), (s_setp, SETP_N)] + wl,
            incs=[(s_v1, 1)])
        vop("reduce_max", gm[:, 0:1], gbi, AX.X, waits=[(s_v1, s_v1.n)],
            incs=[(s_v1, 1)])
        vop("tensor_scalar", gwu, gbi, gm[:, 0:1], None, ALU.is_equal,
            waits=[(s_v1, s_v1.n)], incs=[(s_v1, 1)])
        vop("scalar_tensor_tensor", gb2, gwu, -1e30, gbi, ALU.mult, ALU.add,
            waits=[(s_v1, s_v1.n)], incs=[(s_v1, 1)])
        vop("reduce_max", gm[:, 1:2], gb2, AX.X, waits=[(s_v1, s_v1.n)],
            incs=[(s_v1, 1)])
        vop("tensor_scalar", gb2, gb2, gm[:, 1:2], None, ALU.is_equal,
            waits=[(s_v1, s_v1.n)], incs=[(s_v1, 1)])
        vop("tensor_tensor", gwu, gwu, gb2, ALU.add, waits=[(s_v1, s_v1.n)],
            incs=[(s_v1, 1)])
        vop("tensor_tensor", gwu, gwu, gsc, ALU.mult,
            waits=[(s_v1, s_v1.n), (s_a1, asig)], incs=[(s_v1, 1)])
        vop("reduce_sum", gm[:, 2:3], gwu, AX.X, waits=[(s_v1, s_v1.n)],
            incs=[(s_v1, 1)])
        vop("tensor_scalar", gm[:, 2:3], gm[:, 2:3], 1e-10, None, ALU.add,
            waits=[(s_v1, s_v1.n)], incs=[(s_v1, 1)])
        vop("reciprocal", gm[:, 3:4], gm[:, 2:3], waits=[(s_v1, s_v1.n)],
            incs=[(s_v1, 1)])
        vop("tensor_scalar", gatew[:, j * NE:(j + 1) * NE], gwu, gm[:, 3:4],
            None, ALU.mult, waits=[(s_v1, s_v1.n)], incs=[(s_v1, 1)])
        vop("tensor_copy", gbf[:, j * NE:(j + 1) * NE],
            gatew[:, j * NE:(j + 1) * NE],
            waits=[(s_v1, s_v1.n)], incs=[(s_v1, 1)])
        v1_done[j] = s_v1.n
    for j in range(NT):
        sch.op("T", lambda e, j=j: e.transpose(
            tr_ps, gbf[:, j * NE:(j + 1) * NE], iden_sb),
            [(s_v1, v1_done[j]), (s_set, SETUP_N),
             (s_v1, t1tr.get(j - 1, 0))],
            incs=[(s_t1, 1)])
        tn_ = s_t1.n
        vop("tensor_copy", gatewT[:, j * 128:(j + 1) * 128], tr_ps,
            waits=[(s_t1, tn_), (s_v1, s_v1.n)], incs=[(s_v1, 1)])
        t1tr[j] = s_v1.n
        V1N = s_v1.n

    P1T = s_t1.n
    P1V = s_v1.n
    AR.push()
    o_y = AR.alloc(NTQ * D * 4)
    y_acc = AR.f32(o_y, 0, 128, NTQ * D)
    o_wb = AR.alloc(2 * 24 * 1024)
    o_gh = AR.alloc(8 * 512 * 2)
    ghT = AR.bf(o_gh, 0, 128, 8 * 512)
    s1_t = [AR.f32(AR.alloc(512 * 4), 0, 128, 512) for _ in range(2)]
    g_t = [AR.f32(AR.alloc(D * 4), 0, 128, D) for _ in range(2)]

    def wsl(base_bytes, c, span, f0, f1):
        o = (o_wb + base_bytes) // 2
        return AR.h[:, o + c * span + f0: o + c * span + f1]

    h_ps = [(PS[:, 0:512], PS[:, 512:1024]),
            (PS[:, 1024:1536], PS[:, 1536:2048])]
    y_ps = [PS[:, 2048:3072], PS[:, 3072:4096]]
    wfree = {0: (s_set, 0), 1: (s_set, 0)}
    lded = {}

    def load_expert(eidx, buf):
        base = buf * 24 * 1024
        dmaP(AR.h[:, (o_wb + base) // 2:(o_wb + base) // 2 + 4096]
             .rearrange("p (c f) -> p c f", f=DFF),
             W1T[eidx].rearrange("(c p) f -> p c f", p=128),
             waits=[wfree[buf]], incs=[(s_wld[buf], 16)])
        dmaP(AR.h[:, (o_wb + base + 8192) // 2:
                  (o_wb + base + 8192) // 2 + 4096]
             .rearrange("p (c f) -> p c f", f=DFF),
             W3T[eidx].rearrange("(c p) f -> p c f", p=128),
             incs=[(s_wld[buf], 16)])
        dmaP(AR.h[:, (o_wb + base + 16384) // 2:
                  (o_wb + base + 16384) // 2 + 4096]
             .rearrange("p (c d) -> p c d", d=D),
             W2T[eidx].rearrange("(c p) d -> p c d", p=128),
             incs=[(s_wld[buf], 16)])
        lded[eidx] = (s_wld[buf], s_wld[buf].n)

    def load_shared(part):
        if part == 0:
            items = ((0, SW1T, 0, 8, FDFF, 0),
                     (16384, SW3T, 0, 4, FDFF, 0))
        elif part == 1:
            items = ((24576, SW3T, 4, 4, FDFF, 1),
                     (32768, SW2T, 0, 4, D, 1))
        else:
            items = ((40960, SW2T, 4, 4, D, 2),)
        sems = {0: s_wld[0], 1: s_wld[1], 2: s_wsh}
        for off, W_, c0, ncnk, span, sb in items:
            dmaP(AR.h[:, (o_wb + off) // 2:
                      (o_wb + off) // 2 + ncnk * span]
                 .rearrange("p (c f) -> p c f", f=span),
                 W_[c0 * 128:(c0 + ncnk) * 128, :]
                 .rearrange("(c p) f -> p c f", p=128),
                 waits=[(s_th, s_th.n), (s_ty, s_ty.n)],
                 incs=[(sems[sb], 16)])
        sh_ld[part] = ([(s_wsh, s_wsh.n)] if part == 2 else
                       [(s_wld[0], s_wld[0].n), (s_wld[1], s_wld[1].n)])

    sh_ld = {}
    hfreeA = {0: (s_ah, 0), 1: (s_ah, 0)}
    hfreeV = {0: (s_vh, 0), 1: (s_vh, 0)}
    yfree = {0: (s_v1, P1V), 1: (s_v1, P1V)}
    ghfree = {i: (s_ty, 0) for i in range(8)}
    vy_last = {}
    load_expert(0, 0)
    for q in range(NQ):
        tok0 = q * QT
        for ei in range(17):
            is_sh = ei == 16
            buf = ei % 2
            nfc = 8 if is_sh else 4
            if ei + 1 < 16:
                load_expert(ei + 1, (ei + 1) % 2)
            elif ei + 1 == 16:
                load_shared(0)
            if is_sh:
                load_shared(2)
            gh_ready = {}
            for blk in range(NBQ):
                bt0 = tok0 + blk * 512
                for fc in range(nfc):
                    sp = fc % 2
                    slot = fc if is_sh else (blk % 2) * 4 + fc
                    pa, pb = h_ps[sp]
                    for cc in range(NCH):
                        if is_sh:
                            l1 = wsl(0, cc, FDFF, fc * 128, (fc + 1) * 128)
                            l3 = wsl(16384, cc, FDFF, fc * 128, (fc + 1) * 128)
                        else:
                            l1 = wsl(buf * 24576, cc, DFF,
                                     fc * 128, (fc + 1) * 128)
                            l3 = wsl(buf * 24576 + 8192, cc, DFF,
                                     fc * 128, (fc + 1) * 128)
                        if is_sh:
                            w = sh_ld[1] + [
                                 xnd_done[cc], hfreeA[sp], hfreeV[sp]]
                        else:
                            w = [lded[ei],
                                 xnd_done[cc], hfreeA[sp], hfreeV[sp]]
                        mm(pa, l1, xnT(cc, bt0, bt0 + 512), start=(cc == 0),
                           stop=(cc == NCH - 1), waits=w,
                           incs=[(s_th, 1)] if cc == NCH - 1 else ())
                        mm(pb, l3, xnT(cc, bt0, bt0 + 512), start=(cc == 0),
                           stop=(cc == NCH - 1),
                           incs=[(s_th, 1)] if cc == NCH - 1 else ())
                    thn = s_th.n
                    if is_sh:
                        bc1 = sb13T[:, fc:fc + 1]
                        bc3 = sb13T[:, 8 + fc:9 + fc]
                    else:
                        bc1 = b1T[:, fc * NE + ei:fc * NE + ei + 1]
                        bc3 = b3T[:, fc * NE + ei:fc * NE + ei + 1]
                    aop(s1_t[sp], pa, AF.Silu, bias=bc1,
                        waits=[(s_th, thn), (s_set, SETUP_N), (s_setp, SETP_N),
                               (s_vh, hfreeV[sp][1]), (s_ah, s_ah.n)],
                        incs=[(s_ah, 1)])
                    hfreeA[sp] = (s_ah, s_ah.n)
                    vop("scalar_tensor_tensor",
                        ghT[:, slot * 512:(slot + 1) * 512], pb, bc3,
                        s1_t[sp], ALU.add, ALU.mult,
                        waits=[(s_ah, s_ah.n), (s_th, thn), ghfree[slot],
                               (s_vy, s_vy.n)], incs=[(s_vh, 1)])
                    hfreeV[sp] = (s_vh, s_vh.n)
                    gh_ready[(blk, fc)] = s_vh.n
                if is_sh and NBQ > 1:
                    break
            nblk_h = 1 if is_sh else NBQ
            for rep in range(NBQ if is_sh else 1):
                pass
            if ei == 15:
                load_shared(1)
            w2blocks = range(NBQ)
            for blk in w2blocks:
                bt0 = tok0 + blk * 512
                if is_sh and blk > 0:
                    for fc in range(nfc):
                        sp = fc % 2
                        slot = fc
                        pa, pb = h_ps[sp]
                        for cc in range(NCH):
                            l1 = wsl(0, cc, FDFF, fc * 128, (fc + 1) * 128)
                            l3 = wsl(16384, cc, FDFF,
                                     fc * 128, (fc + 1) * 128)
                            w = sh_ld[1] + [
                                 xnd_done[cc], hfreeA[sp], hfreeV[sp]]
                            mm(pa, l1, xnT(cc, bt0, bt0 + 512),
                               start=(cc == 0), stop=(cc == NCH - 1), waits=w,
                               incs=[(s_th, 1)] if cc == NCH - 1 else ())
                            mm(pb, l3, xnT(cc, bt0, bt0 + 512),
                               start=(cc == 0), stop=(cc == NCH - 1),
                               incs=[(s_th, 1)] if cc == NCH - 1 else ())
                        thn = s_th.n
                        bc1 = sb13T[:, fc:fc + 1]
                        bc3 = sb13T[:, 8 + fc:9 + fc]
                        aop(s1_t[sp], pa, AF.Silu, bias=bc1,
                            waits=[(s_th, thn), (s_vh, hfreeV[sp][1]),
                                   (s_ah, s_ah.n)], incs=[(s_ah, 1)])
                        hfreeA[sp] = (s_ah, s_ah.n)
                        vop("scalar_tensor_tensor",
                            ghT[:, slot * 512:(slot + 1) * 512], pb, bc3,
                            s1_t[sp], ALU.add, ALU.mult,
                            waits=[(s_ah, s_ah.n), (s_th, thn), ghfree[slot],
                                   (s_vy, s_vy.n)], incs=[(s_vh, 1)])
                        hfreeV[sp] = (s_vh, s_vh.n)
                        gh_ready[(blk, fc)] = s_vh.n
                    if blk == NBQ - 1 and q + 1 < NQ:
                        wfree[0] = (s_th, s_th.n)
                        load_expert(0, 0)
                for tw in range(4):
                    tg = bt0 // 128 + tw
                    tq = tg - tok0 // 128
                    yp = y_ps[tw % 2]
                    for fc in range(nfc):
                        slot = fc if is_sh else (blk % 2) * 4 + fc
                        for dh in range(2):
                            r = wsl(32768, fc, D, dh * 512, (dh + 1) * 512) \
                                if is_sh else \
                                wsl(buf * 24576 + 16384, fc, D,
                                    dh * 512, (dh + 1) * 512)
                            stop = (not is_sh) and fc == nfc - 1
                            last = stop and dh == 1
                            w2w = [(s_vh, gh_ready[(blk, fc)]),
                                   yfree[tw % 2]]
                            if is_sh:
                                w2w += sh_ld[2]
                            mm(yp[:, dh * 512:(dh + 1) * 512],
                               ghT[:, slot * 512 + tw * 128:
                                   slot * 512 + tw * 128 + 128], r,
                               start=(fc == 0), stop=stop,
                               waits=w2w,
                               incs=[(s_ty, 1)] if last else ())
                    if is_sh:
                        for dh in range(2):
                            mm(yp[:, dh * 512:(dh + 1) * 512],
                               gatewT[:, tg * 128:(tg + 1) * 128],
                               b2_sb[:, dh * 512:(dh + 1) * 512],
                               start=False, stop=False,
                               waits=[(s_v1, t1tr.get(tg, 0)),
                                      (s_v1, V1N)])
                            mm(yp[:, dh * 512:(dh + 1) * 512],
                               ones_row[:, 0:128],
                               sb2_row[:, dh * 512:(dh + 1) * 512],
                               start=False, stop=True,
                               waits=[(s_setc, 3)],
                               incs=[(s_ty, 1)] if dh == 1 else ())
                    tyn = s_ty.n
                    ycol = y_acc[:, tq * D:(tq + 1) * D]
                    gcol = None if is_sh else \
                        gatew[:, tg * NE + ei:tg * NE + ei + 1]
                    if ei == 0:
                        vop("tensor_scalar", ycol, yp, gcol, None, ALU.mult,
                            waits=[(s_ty, tyn), (s_vy, vy_last.get(tq, 0)),
                                   (s_v1, V1N)], incs=[(s_vy, 1)])
                    elif not is_sh:
                        vop("scalar_tensor_tensor", ycol, yp, gcol, ycol,
                            ALU.mult, ALU.add,
                            waits=[(s_ty, tyn), (s_vy, vy_last.get(tq, 0))],
                            incs=[(s_vy, 1)])
                    else:
                        vop("tensor_tensor", ycol, yp, ycol, ALU.add,
                            waits=[(s_ty, tyn), (s_vy, vy_last.get(tq, 0))],
                            incs=[(s_vy, 1)])
                    vy_last[tq] = s_vy.n
                    yfree[tw % 2] = (s_vy, s_vy.n)
                    if is_sh:
                        aop(g_t[tw % 2], ycol, AF.Silu,
                            waits=[(s_vy, vy_last[tq]), (s_a1, s_a1.n),
                                   (s_ag, s_ag.n),
                                   (s_gsp[tw % 2], 16 * (tg // 2))],
                            incs=[(s_ag, 1)])
                        dmaS(G_SP[tg * 128:(tg + 1) * 128, :], g_t[tw % 2],
                             waits=[(s_ag, s_ag.n)],
                             incs=[(s_gsp[tw % 2], 16)])
                for fc in range(nfc):
                    slot = fc if is_sh else (blk % 2) * 4 + fc
                    ghfree[slot] = (s_ty, s_ty.n)
            wfree[buf] = (s_ty, s_ty.n)
            if is_sh:
                wfree[0] = wfree[1] = (s_ty, s_ty.n)

    sch.barrier(s_bar[2], extra=all_dma_sems())
    AR.pop()
    AR.push()
    o_va = AR.alloc(va_cols * 2)
    V_aug = AR.bf(o_va, 0, 128, va_cols)
    o_wp = AR.alloc(NCH * D * 2)
    Wp = AR.bf(o_wp, 0, 128, NCH * D)
    stg = [AR.f32(AR.alloc(D * 4), 0, 128, D) for _ in range(2)]
    raw = [AR.bf(AR.alloc(D * 2), 0, 128, D) for _ in range(2)]
    Bp = AR.f32(AR.alloc(D * 4), 0, 128, D)
    qf = [AR.f32(AR.alloc(D * 4), 0, 128, D) for _ in range(2)]
    qh_bf = [AR.bf(AR.alloc(D * 2), 0, 128, D) for _ in range(2)]
    sq3 = AR.f32(AR.alloc(D * 4), 0, 128, D)
    ssq = AR.f32(AR.alloc(NH * 4), 0, 128, NH)
    rn3 = AR.f32(AR.alloc(NH * 4), 0, 128, NH)
    bp_row = AR.bf(AR.alloc(D * 2), 0, 1, D)

    WMAP = [(WQd, 0, QH_SP, s_qsp), (WKd, 1, KH_SP, s_ksp),
            (WVd, 2, None, None)]
    v3l, a3l = {}, {}
    for pi, (Wd, bi, SPt, ssp) in enumerate(WMAP):
        for kk in (-2, -1):
            v3l[("w", pi, kk)] = s_v3.n
            v3l[("t", pi, kk)] = s_v3.n
            a3l[("t", pi, kk)] = s_a3.n
        prevsp = []
        if pi == 1:
            prevsp = [(s_qsp[0], s_qsp[0].n), (s_qsp[1], s_qsp[1].n)]
        elif pi == 2:
            prevsp = [(s_ksp[0], s_ksp[0].n), (s_ksp[1], s_ksp[1].n)]
        for cc in range(NCH):
            dmaA(stg[cc % 2], Wd[cc * 128:(cc + 1) * 128, :],
                 waits=[(s_v3, v3l.get(("w", pi, cc - 2), 0))],
                 incs=[(s_ws[cc % 2], 16)])
            wsn = (s_ws[cc % 2], s_ws[cc % 2].n)
            vop("tensor_scalar", Wp[:, cc * D:(cc + 1) * D], stg[cc % 2],
                lnw_sb[:, cc:cc + 1], None, ALU.mult,
                waits=[wsn, (s_set, SETUP_N), (s_setp, SETP_N),
                       (s_t3, s_t3.n), (s_v3, s_v3.n)], incs=[(s_v3, 1)])
            vop("tensor_copy", raw[cc % 2], stg[cc % 2],
                waits=[wsn, (s_t3, s_t3.n), (s_v3, s_v3.n)],
                incs=[(s_v3, 1)])
            v3l[("w", pi, cc)] = s_v3.n
            vraw = s_v3.n
            for dh in range(2):
                mm(PS[0:1, 2048 + dh * 512:2560 + dh * 512],
                   lnb_sb[:, cc:cc + 1],
                   raw[cc % 2][:, dh * 512:(dh + 1) * 512],
                   start=(cc == 0), stop=False,
                   waits=[(s_v3, vraw), (s_set, SETUP_N), (s_setp, SETP_N)], incs=[(s_t3, 1)])
        for dh in range(2):
            mm(PS[0:1, 2048 + dh * 512:2560 + dh * 512], ones_row[:, 0:1],
               brow[bi][:, dh * 512:(dh + 1) * 512], start=False, stop=True,
               waits=[(s_setc, 3)], incs=[(s_t3, 1)])
        t3b = s_t3.n
        vop("tensor_copy", bp_row, PS[0:1, 2048:3072],
            waits=[(s_t3, t3b), (s_v3, s_v3.n)], incs=[(s_v3, 1)])
        vbp = s_v3.n
        for dh in range(2):
            mm(PS[:, 3072 + dh * 512:3584 + dh * 512], ones_row[:, 0:128],
               bp_row[:, dh * 512:(dh + 1) * 512], start=True, stop=True,
               waits=[(s_v3, vbp)], incs=[(s_t3, 1)])
        t3c = s_t3.n
        vop("tensor_copy", Bp, PS[:, 3072:4096],
            waits=[(s_t3, t3c), (s_v3, s_v3.n)], incs=[(s_v3, 1)])
        bp_done = s_v3.n
        for j in range(NT):
            p = j % 2
            pq = PS[:, 2048:3072] if p == 0 else PS[:, 3072:4096]
            wl = [(s_v3, v3l.get(("t", pi, j - 1), 0)),
                  (s_a3, a3l.get(("t", pi, j - 1), 0))]
            for cc in range(NCH):
                for dh in range(2):
                    mm(pq[:, dh * 512:(dh + 1) * 512],
                       xnT(cc, j * 128, (j + 1) * 128),
                       Wp[:, cc * D + dh * 512:cc * D + (dh + 1) * 512],
                       start=(cc == 0), stop=(cc == NCH - 1),
                       waits=[(s_v3, bp_done), xnd_done[cc],
                              (s_v3, v3l.get(("t", pi, j - 2), 0))],
                       incs=[(s_t3, 1)] if (cc == NCH - 1 and dh == 1) else ())
            tq3 = s_t3.n
            vop("scalar_tensor_tensor", qf[p], pq, R2[:, j:j + 1], Bp,
                ALU.mult, ALU.add,
                waits=[(s_t3, tq3), (s_v3, s_v3.n)] + wl, incs=[(s_v3, 1)])
            vqf = s_v3.n
            if SPt is not None:
                aop(sq3, qf[p], AF.Square,
                    waits=[(s_v3, vqf), (s_a3, s_a3.n)] + wl,
                    incs=[(s_a3, 1)])
                vop("tensor_reduce", ssq.rearrange("p (h u) -> p h u", u=1),
                    sq3.rearrange("p (h d) -> p h d", d=HD), AX.X, ALU.add,
                    waits=[(s_a3, s_a3.n), (s_v3, s_v3.n)], incs=[(s_v3, 1)])
                vop("tensor_scalar", ssq, ssq, 1e-24, None, ALU.max,
                    waits=[(s_v3, s_v3.n)], incs=[(s_v3, 1)])
                vsq = s_v3.n
                aop(rn3, ssq, AF.Ln, waits=[(s_v3, vsq), (s_a3, s_a3.n)],
                    incs=[(s_a3, 1)])
                aop(rn3, rn3, AF.Exp, scale=-0.5, waits=[(s_a3, s_a3.n)],
                    incs=[(s_a3, 1)])
                arn = s_a3.n
                rnb = bass.AP(tensor=rn3.tensor, offset=rn3.offset,
                              ap=[[rn3.ap[0][0], 128], [1, NH], [0, HD]])
                vop("tensor_tensor",
                    qh_bf[p].rearrange("p (h d) -> p h d", d=HD),
                    qf[p].rearrange("p (h d) -> p h d", d=HD), rnb, ALU.mult,
                    waits=[(s_a3, arn), (s_v3, s_v3.n),
                           (ssp[p], 16 * (j // 2))] + prevsp,
                    incs=[(s_v3, 1)])
                v3l[("t", pi, j)] = s_v3.n
                a3l[("t", pi, j)] = s_a3.n
                dmaS(SPt[j * 128:(j + 1) * 128, :], qh_bf[p],
                     waits=[(s_v3, s_v3.n)], incs=[(ssp[p], 16)])
            else:
                vop("tensor_copy", V_aug[:, j * D:(j + 1) * D], qf[p],
                    waits=[(s_v3, vqf)], incs=[(s_v3, 1)])
                v3l[("t", pi, j)] = s_v3.n
                a3l[("t", pi, j)] = s_a3.n
    V3F = s_v3.n

    sch.barrier(s_bar[3], extra=all_dma_sems())
    AR.pop()
    AR.pop()
    AR.push()

    def allocA(nbytes):
        nb = (nbytes + 63) // 64 * 64
        if AR.off + nb <= o_va:
            return AR.alloc(nbytes)
        return None
    NRING = globals()['kernel_NRING']
    o_kt = allocA(NCH * S * 2)
    o_wo = allocA(NCH * D * 2)
    o_ring = allocA(NRING * 1024 * 2)
    o_st4 = [allocA(64) for _ in range(2)]
    assert AR.off <= o_va, (AR.off, o_va)
    AR.off = o_va + va_cols * 2
    V2 = AR.bf(o_va, 0, 128, va_cols)
    if o_kt is None:
        o_kt = AR.alloc(NCH * S * 2)
    if o_wo is None:
        o_wo = AR.alloc(NCH * D * 2)
    if o_ring is None:
        o_ring = AR.alloc(NRING * 1024 * 2)
    o_st4 = [o if o is not None else AR.alloc(64) for o in o_st4]
    Wo_sb = AR.bf(o_wo, 0, 128, NCH * D)
    ring = AR.bf(o_ring, 0, 128, NRING * 1024)
    ST4 = [AR.f32(o, 0, 128, 16) for o in o_st4]
    qTb = AR.bf(AR.alloc(NCH * 512 * 2), 0, 128, NCH * 512)
    O_T = AR.bf(AR.alloc(NPAIR * 512 * 2), 0, 128, NPAIR * 512)
    stT = [AR.bf(AR.alloc(512 * 2), 0, 128, 512) for _ in range(2)]
    su_s = AR.f32(AR.alloc(512 * 4), 0, 128, 512)
    su = [su_s, su_s]
    lr = AR.f32(AR.alloc(512 * 4), 0, 1, 512)
    rinv_p = [AR.f32(AR.alloc(512 * 4), 0, 1, 512) for _ in range(2)]
    rinv = [rinv_p, rinv_p]
    x4 = [AR.f32(AR.alloc(D * 4), 0, 128, D) for _ in range(2)]
    g4 = [AR.f32(AR.alloc(D * 4), 0, 128, D) for _ in range(2)]
    o4_s = AR.f32(AR.alloc(D * 4), 0, 128, D)
    o4 = [o4_s, o4_s]
    xn4_s = AR.f32(AR.alloc(D * 4), 0, 128, D)
    xn4 = [xn4_s, xn4_s]
    o_rbc = AR.alloc(512 * 4)
    rbcA_sb = AR.f32(o_rbc, 0, 64, 512)
    rbcB_sb = AR.f32(o_rbc, 64, 128, 512)

    def kTs(pair, p0, p1, c0, c1):
        return AR.h[p0:p1, o_kt // 2 + pair * S + c0: o_kt // 2 + pair * S + c1]

    kt_done = {}

    def load_kT(cc, dmaF, sm=None):
        sm = s_ktd[cc]
        dmaF(kTs(cc, 0, 128, 0, S), KH_SP[:, cc * 128:(cc + 1) * 128],
             waits=[(s_ksp[0], 16 * ((NT + 1) // 2)),
                    (s_ksp[1], 16 * (NT // 2)), (s_v3, V3F)],
             incs=[(sm, 16)], transpose=True)
        kt_done[cc] = (sm, sm.n)
    load_kT(0, dmaS)
    for cc in range(4, NCH):
        load_kT(cc, dmaA)
    dmaP(Wo_sb.rearrange("p (c d) -> p c d", d=D),
         WOd.rearrange("(c p) d -> p c d", p=128), waits=[(s_v3, V3F)],
         incs=[(s_wo4, 16)])
    WO4 = s_wo4.n

    SC = [PS[:, 0:1024], PS[:, 1024:2048]]
    AVb = [PS[:, 2048:3072], PS[:, 3072:4096]]
    avA = [AVb[s][0:64, 0:512] for s in range(2)]
    rsA = [AVb[s][64:65, 0:512] for s in range(2)]
    avB = [AVb[s][64:128, 512:1024] for s in range(2)]
    rsB = [AVb[s][0:1, 512:1024] for s in range(2)]
    rbcA = [AVb[s][0:64, 0:512] for s in range(2)]
    rbcB = [AVb[s][64:128, 512:1024] for s in range(2)]

    sc_free = {0: (s_aex, 0), 1: (s_aex, 0)}
    ring_free = {i: (s_tav, 0) for i in range(NRING)}
    av_free = {0: [], 1: []}
    stT_rd = {0: (s_tsc, 0), 1: (s_tsc, 0)}
    state_ok = {0: 0, 1: 0}
    oT_free = (s_tt, 0)
    qT_rd = (s_tsc, 0)
    gc = [0]
    rbc_rd = [(s_vn, 0)]
    pending = [None]
    ti_v, ti_a = {}, {}
    ti = 0

    def emit_norm(unit):
        pair, st2, slot, tavn = unit
        vop("reciprocal", rinv[slot][0], rsA[slot],
            waits=[(s_tav, tavn), (s_vn, s_vn.n), (s_nbc, s_nbc.n)],
            incs=[(s_vn, 1)])
        anA = (s_vn, s_vn.n)
        vop("reciprocal", rinv[slot][1], rsB[slot],
            waits=[(s_vn, s_vn.n), (s_nbc, s_nbc.n)], incs=[(s_vn, 1)])
        anB = (s_vn, s_vn.n)
        dstA = stT[slot][0:64, :] if st2 == 0 else \
            O_T[0:64, pair * 512:(pair + 1) * 512]
        dstB = stT[slot][64:128, :] if st2 == 0 else \
            O_T[64:128, pair * 512:(pair + 1) * 512]
        wdst = [stT_rd[slot]] if st2 == 0 else [oT_free]
        vop("tensor_copy", su[slot][0:64, :], AVb[slot][0:64, 0:512],
            waits=[(s_tav, tavn), (s_vn, s_vn.n)], incs=[(s_vn, 1)])
        vop("tensor_copy", su[slot][64:128, :], avB[slot],
            waits=[(s_tav, tavn), (s_vn, s_vn.n)], incs=[(s_vn, 1)])
        vsu = s_vn.n
        r0 = rinv[slot][0]
        r1 = rinv[slot][1]
        dmaP(rbcA_sb,
             bass.AP(tensor=r0.tensor, offset=r0.offset,
                     ap=[[r0.ap[0][0], 1], [0, 64], [1, 512]]),
             waits=[anA, rbc_rd[0]], incs=[(s_nbc, 16)])
        dmaP(rbcB_sb,
             bass.AP(tensor=r1.tensor, offset=r1.offset,
                     ap=[[r1.ap[0][0], 1], [0, 64], [1, 512]]),
             waits=[anB], incs=[(s_nbc, 16)])
        nbcn = s_nbc.n
        vop("tensor_tensor", dstA, su[slot][0:64, :], rbcA_sb, ALU.mult,
            waits=[(s_nbc, nbcn), (s_vn, s_vn.n)] + wdst, incs=[(s_vn, 1)])
        vop("tensor_tensor", dstB, su[slot][64:128, :], rbcB_sb, ALU.mult,
            waits=[(s_nbc, nbcn), (s_vn, s_vn.n)] + wdst, incs=[(s_vn, 1)])
        rbc_rd[0] = (s_vn, s_vn.n)
        av_free[slot] = [(s_vn, vsu), anB]
        if st2 == 0:
            state_ok[slot] = s_vn.n

    def emit_unit(qb, pair, st2, slot, qtn):
        LAG = 2
        expcnt = {}
        for it in range(NT + LAG):
            c = it
            if c < NT:
                scs = c % 2
                rgs = c % NRING
                w = [sc_free[scs], qtn, kt_done[pair]]
                if st2 == 1:
                    w.append((s_vn, state_ok[slot]))
                rhs = qTb[:, pair * 512:(pair + 1) * 512] if st2 == 0 \
                    else stT[slot][:, :]
                mm(SC[scs][:, 0:512],
                   kTs(pair, 0, 64, c * 128, (c + 1) * 128), rhs[0:64, :],
                   start=True, stop=True, waits=w, tp=(0, 0))
                mm(SC[scs][:, 512:1024],
                   kTs(pair, 64, 128, c * 128, (c + 1) * 128), rhs[64:128, :],
                   start=True, stop=True, tp=(64, 0), incs=[(s_tsc, 1)])
                aop(ring[:, rgs * 1024:(rgs + 1) * 1024], SC[scs], AF.Exp,
                    waits=[(s_tsc, s_tsc.n), ring_free[rgs],
                           (s_aex, s_aex.n)], incs=[(s_aex, 1)])
                sc_free[scs] = (s_aex, s_aex.n)
                expcnt[c] = s_aex.n
                gc[0] += 1
            ca = it - LAG
            if ca >= 0:
                rgs2 = ca % NRING
                aw = [(s_aex, expcnt[ca])]
                if ca == 0:
                    aw += av_free[slot] + [(s_setc, 3), (s_v3, V3F)]
                vaA = V2[:, ca * D + 2 * pair * HD:
                         ca * D + 2 * pair * HD + HD]
                vaB = V2[:, ca * D + (2 * pair + 1) * HD:
                         ca * D + (2 * pair + 1) * HD + HD]
                rgA = ring[:, rgs2 * 1024:rgs2 * 1024 + 512]
                rgB = ring[:, rgs2 * 1024 + 512:rgs2 * 1024 + 1024]
                mm(avA[slot], vaA, rgA, start=(ca == 0), stop=(ca == NT - 1),
                   waits=aw)
                mm(avB[slot], vaB, rgB, start=(ca == 0), stop=(ca == NT - 1),
                   tp=(0, 64))
                mm(rsB[slot], ones_col, rgB, start=(ca == 0),
                   stop=(ca == NT - 1), tp=(0, 0))
                mm(rsA[slot], ones_col, rgA, start=(ca == 0),
                   stop=(ca == NT - 1), tp=(0, 64), incs=[(s_tav, 4)])
                ring_free[rgs2] = (s_tav, s_tav.n)
            if it == DEFER_AT and pending[0] is not None and not DEFER_OFF:
                emit_norm(pending[0])
                pending[0] = None
        if st2 == 1:
            stT_rd[slot] = (s_tsc, s_tsc.n)
        tavn = s_tav.n
        if DEFER_OFF:
            emit_norm((pair, st2, slot, tavn))
        else:
            pending[0] = (pair, st2, slot, tavn)

    def emit_qT(qb):
        for cc in range(NCH):
            dmaS(qTb[:, cc * 512:(cc + 1) * 512],
                 QH_SP[qb * 512:(qb + 1) * 512, cc * 128:(cc + 1) * 128],
                 waits=[(s_qsp[0], 16 * ((NT + 1) // 2)),
                        (s_qsp[1], 16 * (NT // 2)), qT_rd, (s_v3, V3F)],
                 incs=[(s_qT[0], 16)], transpose=True)
        return (s_qT[0], s_qT[0].n)

    qtn = emit_qT(0)
    for cc in range(1, 4):
        load_kT(cc, dmaS)
    for qb in range(NB):
        qtn_next = qtn
        for duo in range(0, NPAIR, 2):
            emit_unit(qb, duo, 0, 0, qtn)
            emit_unit(qb, duo + 1, 0, 1, qtn)
            if duo == NPAIR - 2:
                qT_rd = (s_tsc, s_tsc.n)
                if qb + 1 < NB:
                    qtn_next = emit_qT(qb + 1)
            emit_unit(qb, duo, 1, 0, qtn)
            emit_unit(qb, duo + 1, 1, 1, qtn)
        if pending[0] is not None:
            emit_norm(pending[0])
            pending[0] = None
        oT_ready = s_vn.n
        for tw in range(4):
            p = ti % 2
            row0 = qb * 512 + tw * 128
            dmaS(x4[p], X[row0:row0 + 128, :],
                 waits=[(s_vt, ti_v.get(ti - 2, 0)),
                        (s_at, ti_a.get(ti - 2, 0))], incs=[(s_xt[p], 16)])
            xt_n = (s_xt[p], s_xt[p].n)
            dmaS(g4[p], G_SP[row0:row0 + 128, :],
                 waits=[(s_gsp[0], 16 * ((NT + 1) // 2)),
                        (s_gsp[1], 16 * (NT // 2)),
                        (s_vt, ti_v.get(ti - 2, 0))],
                 incs=[(s_gt[p], 16)])
            gt_n = (s_gt[p], s_gt[p].n)
            for pr in range(NPAIR):
                for dh in range(2):
                    mm(AVb[1][:, dh * 512:(dh + 1) * 512],
                       O_T[:, pr * 512 + tw * 128:pr * 512 + tw * 128 + 128],
                       Wo_sb[:, pr * D + dh * 512:pr * D + (dh + 1) * 512],
                       start=(pr == 0), stop=False,
                       waits=[(s_vn, oT_ready), (s_v3, V3F), (s_wo4, WO4),
                              (s_an, s_an.n),
                              (s_vt, ti_v.get(ti - 1, 0))] + av_free[1])
            for dh in range(2):
                mm(AVb[1][:, dh * 512:(dh + 1) * 512], ones_row[:, 0:128],
                   bo_row[:, dh * 512:(dh + 1) * 512], start=False, stop=True,
                   waits=[(s_set, SETUP_N), (s_setp, SETP_N), (s_setc, 3)],
                   incs=[(s_tt, 1)] if dh == 1 else ())
            ttn = s_tt.n
            st = ST4[p]

            def c4(i, st=st):
                return st[:, i:i + 1]
            wl = [(s_vt, ti_v.get(ti - 1, 0)), (s_at, ti_a.get(ti - 1, 0))]
            vop("reduce_sum", c4(0), x4[p], AX.X,
                waits=[xt_n, (s_vt, s_vt.n)] + wl, incs=[(s_vt, 1)])
            aop(xn4[p], x4[p], AF.Square, accum=c4(9),
                waits=[xt_n, (s_at, s_at.n)] + wl, incs=[(s_at, 1)])
            asq = s_at.n
            vop("tensor_scalar", c4(1), c4(0), 1.0 / D, None, ALU.mult,
                waits=[(s_vt, s_vt.n)], incs=[(s_vt, 1)])
            vop("tensor_tensor", c4(2), c4(0), c4(0), ALU.mult,
                waits=[(s_vt, s_vt.n)], incs=[(s_vt, 1)])
            vop("tensor_scalar", c4(3), c4(2), 1.0 / (D * D), None, ALU.mult,
                waits=[(s_vt, s_vt.n)], incs=[(s_vt, 1)])
            vop("scalar_tensor_tensor", c4(4), c4(9), 1.0 / D, c4(3),
                ALU.mult, ALU.subtract,
                waits=[(s_at, asq), (s_vt, s_vt.n)], incs=[(s_vt, 1)])
            vop("tensor_scalar", c4(5), c4(4), EPS, None, ALU.add,
                waits=[(s_vt, s_vt.n)], incs=[(s_vt, 1)])
            vv4 = s_vt.n
            aop(c4(10), c4(5), AF.Ln,
                waits=[(s_vt, vv4), (s_at, s_at.n)], incs=[(s_at, 1)])
            aop(c4(10), c4(10), AF.Exp, scale=-0.5, waits=[(s_at, s_at.n)],
                incs=[(s_at, 1)])
            ar14 = s_at.n
            vop("tensor_scalar", xn4[p], x4[p], c4(1), c4(10), ALU.subtract,
                ALU.mult, waits=[(s_at, ar14), (s_vt, s_vt.n)],
                incs=[(s_vt, 1)])
            vop("tensor_tensor", o4[p], AVb[1][:, :], xn4[p], ALU.add,
                waits=[(s_tt, ttn), (s_vt, s_vt.n),
                       (s_out[0], s_out[0].n), (s_out[1], s_out[1].n)],
                incs=[(s_vt, 1)])
            vop("tensor_tensor", xn4[p], o4[p], g4[p], ALU.mult,
                waits=[gt_n, (s_vt, s_vt.n)], incs=[(s_vt, 1)])
            vop("tensor_tensor", o4[p], xn4[p], x4[p], ALU.add,
                waits=[(s_vt, s_vt.n)], incs=[(s_vt, 1)])
            ti_v[ti] = s_vt.n
            ti_a[ti] = s_at.n
            dmaS(OUT[row0:row0 + 128, :], o4[p], waits=[(s_vt, ti_v[ti])],
                 incs=[(s_out[p], 16)])
            ti += 1
        av_free[1] = [(s_vt, ti_v[ti - 1]), (s_an, s_an.n)]
        oT_free = (s_tt, s_tt.n)
        qtn = qtn_next

    sch.op("S", lambda e: e.nop(),
           waits=[(s_out[0], 16 * ((NT + 1) // 2)),
                  (s_out[1], 16 * (NT // 2))])

    with nc.Block() as blk:
        blk.tensor(lambda e: sch.replay("T", e))
        blk.vector(lambda e: sch.replay("V", e))
        blk.scalar(lambda e: sch.replay("A", e))
        blk.sync(lambda e: sch.replay("S", e))
        blk.gpsimd(lambda e: sch.replay("P", e))
    return nc, stack


_CACHE = {}


def _get(S):
    if S not in _CACHE:
        _CACHE[S] = _build(S)
    return _CACHE[S]


def kernel(**inputs):
    ip = {k: np.asarray(v) for k, v in inputs.items()}
    x = ip["L0_attn_hidden_states"]
    B, S, D_ = x.shape
    C = np.ascontiguousarray
    common = {
        "wq": C(ip["L0_attn_Wq"]), "wk": C(ip["L0_attn_Wk"]),
        "wv": C(ip["L0_attn_Wv"]), "wo": C(ip["L0_attn_Wo"]),
        "lnw": C(ip["L0_attn_ln_w"].reshape(1, -1)),
        "lnb": C(ip["L0_attn_ln_b"].reshape(1, -1)),
        "bq": C(ip["L0_attn_bq"].reshape(1, -1)),
        "bk": C(ip["L0_attn_bk"].reshape(1, -1)),
        "bv": C(ip["L0_attn_bv"].reshape(1, -1)),
        "bo": C(ip["L0_attn_bo"].reshape(1, -1)),
        "gwt": C(ip["L0_ffn_gate_w"].T),
        "eb": C(ip["L0_ffn_expert_bias"].reshape(1, -1)),
        "w1t": C(ip["L0_ffn_w1"].transpose(0, 2, 1)),
        "w3t": C(ip["L0_ffn_w3"].transpose(0, 2, 1)),
        "w2t": C(ip["L0_ffn_w2"].transpose(0, 2, 1)),
        "b1": C(ip["L0_ffn_b1"]), "b3": C(ip["L0_ffn_b3"]),
        "b2": C(ip["L0_ffn_b2"]),
        "sw1t": C(ip["L0_ffn_sw1"].T), "sw3t": C(ip["L0_ffn_sw3"].T),
        "sw2t": C(ip["L0_ffn_sw2"].T),
        "sb1": C(ip["L0_ffn_sb1"].reshape(1, -1)),
        "sb3": C(ip["L0_ffn_sb3"].reshape(1, -1)),
        "sb2": C(ip["L0_ffn_sb2"].reshape(1, -1)),
    }
    nc, _ = _get(S)
    in_maps = [dict(common, x=C(x[b])) for b in range(B)]
    res = run_bass_kernel_spmd(nc, in_maps, core_ids=list(range(B)))
    return np.stack([r["out"] for r in res.results], axis=0)
